# version 15
# baseline (speedup 1.0000x reference)
import numpy as np
import ml_dtypes
import concourse.bass as bass
import concourse.mybir as mybir
from concourse.bass_utils import run_bass_kernel_spmd
from contextlib import ExitStack

F32 = mybir.dt.float32
BF16 = mybir.dt.bfloat16
AF = mybir.ActivationFunctionType
OP = mybir.AluOpType
AX = mybir.AxisListType.X

MAGIC = 12582912.0
MASKVAL = -30000.0
EPS = 1e-5
D = 1024
NH = 16
DH = 64
FF = 4096
C = 128
SCALE = 0.125
NCORES = 8
G = 512
FG = 512
KD = D // 128
KF = FF // 128

ENGINES = ("tensor", "vector", "scalar", "sync", "gpsimd")


class Pool:

    def __init__(self, base_ap_f32, nwords):
        self.base = base_ap_f32
        self.n = nwords
        self.off = 0

    def tensor(self, shape, dt):
        P, F = shape
        words = (F * (2 if dt == BF16 else 4) + 3) // 4
        words = (words + 7) & ~7
        assert self.off + words <= self.n, (self.off, words, self.n)
        view = self.base[:, self.off:self.off + words]
        self.off += words
        if dt == BF16:
            view = view.bitcast(BF16)
        return view[0:P, 0:F]


QP = 256
QNP = (D * FF) // (128 * QP)


def _mk_qphase(kind, w, np_):
    st = [("ld", w, 0), ("ld", w, 1)]
    for p in range(np_):
        if p + 2 < np_:
            st.append(("ld", w, p + 2))
        st.append((kind, w, p))
    return st


QSTEPS = (_mk_qphase("am", 0, QNP) + [("ws", 0)]
          + _mk_qphase("am", 1, QNP) + [("ws", 1)]
          + _mk_qphase("qz", 0, QNP) + _mk_qphase("qz", 1, QNP))


class Builder:
    def __init__(self, T=2048, do_prefix=True, debug_xmid=False, stop_stage=None):
        self.T = T
        self.do_prefix = do_prefix
        self.debug_xmid = debug_xmid
        self.stop_stage = stop_stage
        self.NG = T // G
        self.NFG = T // FG
        self.NCH = G // C

    def cut(self, label):
        if self.stop_stage is None:
            return False
        if self._nstage >= self.stop_stage:
            return True
        self._nstage += 1
        return False

    def on(self, e):
        return self.ename == e

    def w(self, engine, sem, val=None):
        if val is None:
            val = self.c[sem]
        if self.ename == engine:
            self.eng.wait_ge(self.sems[sem], val)
        return val

    def i(self, instr, sem, n=1):
        self.c[sem] += n
        if instr is not None:
            instr.then_inc(self.sems[sem], n)
        return self.c[sem]

    def build(self):
        nc = bass.Bass("TRN2", target_bir_lowering=False, debug=False)
        self.nc = nc
        T = self.T

        def din(name, shape, dt=F32):
            return nc.dram_tensor(name, shape, dt, kind="ExternalInput").ap()

        d = self.d = {}
        d["xT"] = din("xT", [D, T])
        d["xpT"] = din("xpT", [D, T])
        for nmm in ("wqT", "wkT", "wvT", "woT"):
            d[nmm] = din(nmm, [D, D])
        d["wgT"] = din("wgT", [D, NH])
        d["bgp"] = din("bgp", [128, NH], BF16)
        d["w1T"] = din("w1T", [D, FF])
        d["w2T"] = din("w2T", [FF, D])
        d["b1r"] = din("b1r", [128, KF])
        d["b2r"] = din("b2r", [1, D])
        d["c_identb"] = din("c_identb", [128, 128], BF16)
        d["c_identf"] = din("c_identf", [128, 128], F32)
        d["c_tri1"] = din("c_tri1", [128, 128], BF16)
        d["c_tri2"] = din("c_tri2", [128, 128], BF16)
        d["c_mask"] = din("c_mask", [128, 128], BF16)
        d["c_parityf"] = din("c_parityf", [NH, 128], F32)
        d["c_parityb"] = din("c_parityb", [NH, 128], BF16)
        d["c_pairsel"] = din("c_pairsel", [NH, 8], F32)
        d["c_pj8"] = din("c_pj8", [NH, 8 * 128], BF16)
        d["c_ind16"] = din("c_ind16", [NH, NH * 128], F32)
        d["c_ones1"] = din("c_ones1", [1, 128], F32)
        d["c_qcol"] = din("c_qcol", [128, 3], F32)
        d["c_onesc"] = din("c_onesc", [128, 1], F32)
        d["c_onescb"] = din("c_onescb", [128, 1], BF16)
        if self.debug_xmid or self.stop_stage is not None:
            d["out"] = nc.dram_tensor("out", [128, KD * T], F32, kind="ExternalOutput").ap()
        else:
            d["out"] = nc.dram_tensor("out", [T, D], F32, kind="ExternalOutput").ap()
        d["w2q_dram"] = nc.dram_tensor("w2q_scratch", [FF, D], BF16).ap()
        d["w1q_dram"] = nc.dram_tensor("w1q_scratch", [D, FF], BF16).ap()

        es = ExitStack()
        SB = lambda n, s, dt: es.enter_context(nc.sbuf_tensor(n, s, dt))
        with es:
            s = self.s = {}
            for nmm, shape, dt in [
                ("IDENTB", [128, 128], BF16), ("IDENTF", [128, 128], F32),
                ("ONES1", [1, 128], F32), ("ONESC", [128, 1], F32), ("ONESCB", [128, 1], BF16),
            ]:
                s[nmm] = SB(nmm, shape, dt)
            s["XF"] = SB("XF", [128, KD * T], F32)
            s["SM"] = SB("SM", [128, 8 * DH], F32)
            s["SHIZ"] = SB("SHIZ", [128, NH * DH], BF16)
            s["WSC"] = SB("WSC", [128, 2], F32)
            s["MAB"] = SB("MAB", [1, 2], F32)
            s["MABP"] = SB("MABP", [128, 2], F32)

            POOLW = (nc.sbuf_bytes_remaining // 4) & ~7
            s["POOL"] = SB("POOL", [128, POOLW], F32)
            att = Pool(s["POOL"][:, :], POOLW)
            SBa = lambda n, sh, dt: att.tensor(sh, dt)
            for nmm, shape, dt in [
                ("TRI1", [128, 128], BF16), ("TRI2", [128, 128], BF16),
                ("MASK", [128, 128], BF16),
                ("PARITYF", [NH, 128], F32), ("PARITYB", [NH, 128], BF16),
                ("PAIRSEL", [NH, 8], F32), ("PJ8", [NH, 8 * 128], BF16),
                ("IND17F", [128, NH * 128], F32), ("BGRP", [128, NH], BF16),
                ("WQT", [128, KD * D], BF16), ("WKT", [128, KD * D], BF16),
                ("WVT", [128, KD * D], BF16), ("WOT", [128, KD * D], BF16),
                ("WGT", [128, KD * NH], BF16),
                ("U1", [128, KD * G], BF16),
                ("KA", [128, KD * G], BF16), ("VA", [128, 4 * D], BF16),
                ("XPU", [128, KD * G], F32),
                ("QT", [128, 8 * C], BF16), ("KH", [128, NH * DH], BF16),
                ("EBC", [128, 8 * C], BF16),
                ("ZALL", [128, 4 * NH], F32),
                ("SPt", [128, NH], BF16), ("CP", [128, NH], F32),
                ("EK", [128, NH], F32), ("STAGE17", [NH + 1, 128], F32),
                ("ETT", [NH, 128], F32), ("AR16", [NH, 128], F32),
                ("EB2", [NH, 8 * 128], BF16), ("DRP", [128, 8], F32),
                ("TMPE", [128, 512], F32), ("TMPE2", [128, 512], F32),
                ("QLD", [128, 2 * QP], F32), ("QSTG", [128, 2 * QP], BF16),
                ("QACC", [128, QP], F32), ("QCOL", [128, 3], F32),
                ("LNRA", [1, 2 * G], F32),
            ]:
                s[nmm] = SBa(nmm, shape, dt)
            s["IND17"] = s["IND17F"][0:17, :]
            s["LNR"] = s["LNRA"][0:1, 0:2 * G]
            s["DL"] = s["TMPE2"][:, :]
            s["AP4"] = s["TMPE"][:, :].bitcast(BF16)[:, 0:512]
            s["XP"] = s["XPU"][:, :]
            _ubf = s["XPU"][:, :].bitcast(BF16)
            s["QA"] = _ubf[:, 0:KD * G]
            s["YA"] = _ubf[:, KD * G:2 * KD * G]
            pa = self.pa = {}
            pbanks = []
            for k in range(8):
                pbanks.append(es.enter_context(nc.psum_tensor("PBK%d" % k, [128, 512], F32)))
            for k in range(8):
                if k == 2:
                    pa[k] = pbanks[k].ap().bitcast(BF16)
                else:
                    pa[k] = pbanks[k].ap()

            semnames = ["LD", "LG", "LS", "LS2", "LX", "LW", "ST", "TV", "VT", "SC", "GC",
                        "QL0", "QL1", "QS0", "QS1", "QF", "QV", "QW"]
            self.sems = {k: es.enter_context(nc.semaphore(k)) for k in semnames}

            block = es.enter_context(nc.Block())

            ffn = Pool(s["POOL"][:, :], POOLW)
            SBf = lambda n, sh, dt: ffn.tensor(sh, dt)
            for nmm, shape, dt in [
                ("W1Q", [128, KD * FF], BF16),
                ("W2S0", [128, 2 * D], BF16), ("W2S1", [128, 2 * D], BF16),
                ("H", [128, KF * FG], BF16),
                ("C1", [128, KD * FG], BF16),
                ("XT2", [128, 4 * D], BF16), ("OST", [128, 512], F32),
                ("B2BC", [128, D], BF16),
                ("B1R", [128, KF], F32),
                ("QB2", [128, FG], BF16),
                ("MXC", [128, 4], F32),
                ("IV2", [128, 4], F32),
                ("TQ1", [128, FG], BF16),
                ("FTMP", [128, 512], BF16),
                ("ROWS", [1, 5 * G], F32),
            ]:
                s[nmm] = SBf(nmm, shape, dt)
            s["MAXA"] = s["OST"][:, :]
            s["QB1"] = s["QB2"][:, :]
            s["B2R"] = s["ROWS"][0:1, 0:D]
            pf = self.pf = {}
            for k in range(8):
                pf[k] = pbanks[k].ap()

            @block.tensor
            def _(eng):
                self.walk("tensor", eng)

            @block.vector
            def _(eng):
                self.walk("vector", eng)

            @block.scalar
            def _(eng):
                self.walk("scalar", eng)

            @block.sync
            def _(eng):
                self.walk("sync", eng)

            @block.gpsimd
            def _(eng):
                self.walk("gpsimd", eng)

        return nc

    def walk(self, ename, eng):
        self.ename = ename
        self.eng = eng
        self._nstage = 0
        self.c = {k: 0 for k in self.sems}
        self.qi = 0
        self.q_pending = []
        self.q_lpar = 0
        self.q_spar = 0
        self.q_stmark = [0, 0]
        self.q_ldmark = [(None, 0), (None, 0)]
        self.q_wsdone = [0, 0]
        self.q_ammark = 0
        t_, v_, s_, y_, g_ = (self.on("tensor"), self.on("vector"), self.on("scalar"),
                              self.on("sync"), self.on("gpsimd"))
        s, d, pa, pf = self.s, self.d, self.pa, self.pf
        T, NG, NCH = self.T, self.NG, self.NCH

        def ld(dst_ap, src_ap, q="sync"):
            ins = None
            if (q == "sync" and y_) or (q == "gpsimd" and g_):
                ins = eng.dma_start(out=dst_ap, in_=src_ap)
            return self.i(ins, "LD" if q == "sync" else "LG", 16)

        ld(s["XF"][:, :].rearrange("p (k t) -> p k t", k=KD),
           d["xT"].rearrange("(k p) t -> p k t", p=128))
        for nmm, srcn in [("IDENTB", "c_identb"), ("IDENTF", "c_identf"),
                          ("TRI1", "c_tri1"), ("TRI2", "c_tri2"), ("MASK", "c_mask"),
                          ("PARITYF", "c_parityf"), ("PARITYB", "c_parityb"),
                          ("PAIRSEL", "c_pairsel"), ("PJ8", "c_pj8"),
                          ("ONES1", "c_ones1"),
                          ("ONESC", "c_onesc"), ("ONESCB", "c_onescb"),
                          ("BGRP", "bgp")]:
            ap = s[nmm]
            ld(ap[:, :] if not isinstance(ap, bass.AP) else ap, d[srcn])
        ld(s["IND17"][0:NH, :], d["c_ind16"])
        ld(s["QCOL"][:, :], d["c_qcol"])
        ld_const = self.c["LD"]

        for nmm, src, wide in [("WKT", "wkT", D), ("WVT", "wvT", D), ("WGT", "wgT", NH),
                               ("WQT", "wqT", D), ("WOT", "woT", D)]:
            ld(s[nmm][:, :].rearrange("p (k f) -> p k f", k=KD),
               d[src].rearrange("(k p) f -> p k f", p=128), q="gpsimd")
        lg_kvg = self.c["LG"]
        lg_allw = self.c["LG"]

        ins = None
        if v_:
            eng.memset(s["SM"][:, :], 0.0)
            eng.memset(s["SHIZ"][:, :], 0.0)
            ins = eng.memset(s["STAGE17"][:, :], 1.0)
        self.i(ins, "VT")

        def ln_group(xsrc, Tx, g0, Gw, dst, rows, scr1, scr2):
            scr = [scr1, scr2]
            for k in range(KD):
                xc = xsrc[:, k * Tx + g0: k * Tx + g0 + Gw]
                tile = scr[k % 2][:, 0:Gw]
                if k >= 2:
                    self.w("scalar", "TV", tv_sq[k % 2])
                ins = None
                if s_:
                    ins = eng.activation(tile, xc, AF.Square)
                self.i(ins, "SC")
                sq_done = self.c["SC"]
                self.w("tensor", "SC", sq_done)
                if k == 0:
                    self.w("tensor", "VT")
                ins = None
                if t_:
                    onesq = s["ONESC"] if tile.dtype == F32 else s["ONESCB"]
                    eng.matmul(self.P(6)[0:1, 0:Gw], s["ONESC"][:, :], xc,
                               start=(k == 0), stop=(k == KD - 1))
                    ins = eng.matmul(self.P(7)[0:1, 0:Gw], onesq[:, :], tile,
                                     start=(k == 0), stop=(k == KD - 1))
                mark = self.i(ins, "TV")
                if k == 0:
                    tv_sq = {0: mark, 1: mark}
                tv_sq[k % 2] = mark
            tv_stats = self.c["TV"]
            self.w("vector", "TV", tv_stats)
            self.w("vector", "LG")
            L = rows
            ins = None
            if v_:
                eng.tensor_scalar(L[0:1, 0:Gw], self.P(6)[0:1, 0:Gw], 1.0 / D, None, OP.mult)
                eng.tensor_scalar(L[0:1, G:G + Gw], self.P(7)[0:1, 0:Gw], 1.0 / D, None, OP.mult)
                eng.drain()
                eng.tensor_tensor(scr1[0:1, 0:Gw], L[0:1, 0:Gw], L[0:1, 0:Gw], OP.mult)
                eng.drain()
                eng.tensor_scalar(scr1[0:1, 0:Gw], scr1[0:1, 0:Gw],
                                  -1.0, EPS, OP.mult, OP.add)
                eng.drain()
                ins = eng.tensor_tensor(L[0:1, G:G + Gw], L[0:1, G:G + Gw],
                                        scr1[0:1, 0:Gw], OP.add)
            self.i(ins, "VT")
            self.w("scalar", "VT")
            ins = None
            if s_:
                eng.activation(L[0:1, G:G + Gw], L[0:1, G:G + Gw], AF.Ln)
                eng.drain()
                ins = eng.activation(L[0:1, G:G + Gw], L[0:1, G:G + Gw],
                                     AF.Exp, scale=-0.5)
            self.i(ins, "SC")
            self.w("vector", "SC")
            ins = None
            if v_:
                eng.tensor_tensor(L[0:1, 0:Gw], L[0:1, 0:Gw],
                                  L[0:1, G:G + Gw], OP.mult)
                eng.drain()
                ins = eng.tensor_scalar(L[0:1, 0:Gw], L[0:1, 0:Gw],
                                        -1.0, None, OP.mult)
            self.i(ins, "VT")
            self.w("tensor", "VT")
            ins = None
            if t_:
                eng.matmul(self.P(0)[:, 0:Gw], s["ONES1"][0:1, :], L[0:1, G:G + Gw],
                           start=True, stop=True)
                ins = eng.matmul(self.P(1)[:, 0:Gw], s["ONES1"][0:1, :], L[0:1, 0:Gw],
                                 start=True, stop=True)
            self.i(ins, "TV")
            tv_ab = self.c["TV"]
            self.w("vector", "TV", tv_ab)
            ins = None
            for k0 in range(0, KD, 2):
                xc0 = xsrc[:, k0 * Tx + g0: k0 * Tx + g0 + Gw]
                xc1 = xsrc[:, (k0 + 1) * Tx + g0: (k0 + 1) * Tx + g0 + Gw]
                if v_:
                    eng.tensor_tensor(scr1[:, 0:Gw], xc0, self.P(0)[:, 0:Gw], OP.mult)
                    eng.tensor_tensor(scr2[:, 0:Gw], xc1, self.P(0)[:, 0:Gw], OP.mult)
                    eng.drain()
                    eng.tensor_tensor(dst[:, k0 * Gw:(k0 + 1) * Gw], scr1[:, 0:Gw],
                                      self.P(1)[:, 0:Gw], OP.add)
                    ins = eng.tensor_tensor(dst[:, (k0 + 1) * Gw:(k0 + 2) * Gw],
                                            scr2[:, 0:Gw], self.P(1)[:, 0:Gw], OP.add)
                    if k0 + 2 < KD:
                        eng.drain()
            return self.i(ins, "VT")

        self.Pmap = pa
        self.walk_attention(t_, v_, s_, y_, g_, ld_const, lg_kvg, lg_allw, ln_group)
        if not (self.debug_xmid or (self.stop_stage is not None and self._nstage >= self.stop_stage)):
            self.Pmap = pf
            self.walk_ffn(t_, v_, s_, y_, g_, ln_group)
        if self.stop_stage is not None and not self.debug_xmid:
            self.w("sync", "VT")
            self.w("sync", "TV")
            self.w("sync", "SC")
            ins = None
            if y_:
                ins = eng.dma_start(out=d["out"], in_=s["XF"][:, :])
            self.i(ins, "ST", 16)

        self.w("sync", "ST", self.c["ST"])

    def P(self, k):
        return self.Pmap[k][:, :]

    def q_src(self, w, p):
        if w == 0:
            r, c0 = divmod(p, FF // QP)
            return self.d["w1T"][r * 128:(r + 1) * 128, c0 * QP:(c0 + 1) * QP]
        r, c0 = divmod(p, D // QP)
        return self.d["w2T"][r * 128:(r + 1) * 128, c0 * QP:(c0 + 1) * QP]

    def q_dst(self, w, p):
        if w == 0:
            r, c0 = divmod(p, FF // QP)
            return self.d["w1q_dram"][r * 128:(r + 1) * 128, c0 * QP:(c0 + 1) * QP]
        r, c0 = divmod(p, D // QP)
        return self.d["w2q_dram"][r * 128:(r + 1) * 128, c0 * QP:(c0 + 1) * QP]

    def svc(self, n=1):
        eng, s = self.eng, self.s
        g_ = self.on("gpsimd")
        s_ = self.on("scalar")
        v_ = self.on("vector")
        for _ in range(n):
            if self.qi >= len(QSTEPS):
                return
            st = QSTEPS[self.qi]
            self.qi += 1
            kind = st[0]
            if kind == "ld":
                _, w, p = st
                pb = self.q_lpar % 2
                self.q_lpar += 1
                gsem, gval = self.q_ldmark[pb]
                if gsem is not None:
                    self.w("scalar", gsem, gval)
                self.w("scalar", "QF", max(self.q_wsdone))
                ins = None
                if s_:
                    ins = eng.dma_start(out=s["QLD"][:, pb * QP:(pb + 1) * QP],
                                        in_=self.q_src(w, p))
                self.i(ins, "QL%d" % pb, 16)
                self.q_pending.append((pb, self.c["QL%d" % pb]))
            elif kind == "am":
                _, w, p = st
                pb, qlc = self.q_pending.pop(0)
                self.w("scalar", "QL%d" % pb, qlc)
                ql = s["QLD"][:, pb * QP:(pb + 1) * QP]
                col = w * QNP + p
                ins = None
                if s_:
                    ins = eng.activation(ql, ql, AF.Abs,
                                         accum_out=s["QACC"][:, col:col + 1])
                self.q_ldmark[pb] = (None, 0)
                self.i(ins, "QV")
                self.q_ammark = self.c["QV"]
            elif kind == "ws":
                _, w = st
                numel = float(D * FF)
                QL_ = s["QLD"]
                base = w * QNP
                self.w("vector", "QV", self.q_ammark)
                ins = None
                if v_:
                    ins = eng.tensor_reduce(s["QACC"][:, base:base + 1],
                                            s["QACC"][:, base:base + QNP],
                                            AX, OP.add)
                mw1 = self.i(ins, "QW")
                self.w("gpsimd", "QW", mw1)
                ins = None
                if g_:
                    ins = eng.dma_start(out=QL_[0:1, 0:128],
                                        in_=s["QACC"][:, base:base + 1])
                self.i(ins, "QF", 16)
                qf_row = self.c["QF"]
                self.w("vector", "QF", qf_row)
                ins = None
                if v_:
                    eng.tensor_reduce(QL_[0:1, 0:1], QL_[0:1, 0:128], AX, OP.add)
                    eng.tensor_scalar(QL_[0:1, 0:1], QL_[0:1, 0:1],
                                      1.0 / numel, 1e-5, OP.mult, OP.max)
                    eng.reciprocal(QL_[0:1, 1:2], QL_[0:1, 0:1])
                    eng.tensor_scalar(QL_[0:1, 128:256], s["ONES1"][0:1, :],
                                      QL_[0:1, 1:2], None, OP.mult)
                    ins = eng.tensor_scalar(QL_[0:1, 256:384], s["ONES1"][0:1, :],
                                            QL_[0:1, 0:1], None, OP.mult)
                m2 = self.i(ins, "QW")
                self.w("gpsimd", "QW", m2)
                ins = None
                if g_:
                    eng.dma_start(out=s["WSC"][:, w:w + 1],
                                  in_=QL_[0:1, 128:256]).then_inc(
                        self.sems["QF"], 16)
                    ins = eng.dma_start(out=s["MABP"][:, w:w + 1],
                                        in_=QL_[0:1, 256:384])
                self.c["QF"] += 16
                self.i(ins, "QF", 16)
                self.q_wsdone[w] = self.c["QF"]
            elif kind == "qz":
                _, w, p = st
                pb, qlc = self.q_pending.pop(0)
                self.w("scalar", "QL%d" % pb, qlc)
                if p == 0:
                    self.w("scalar", "QF", self.q_wsdone[w])
                ql = s["QLD"][:, pb * QP:(pb + 1) * QP]
                ins = None
                if s_:
                    ins = eng.activation(ql, ql, AF.Identity,
                                         scale=s["WSC"][:, w:w + 1])
                mva = self.i(ins, "QV")
                sb = self.q_spar % 2
                self.q_spar += 1
                self.w("vector", "QV", mva)
                self.w("vector", "QS%d" % sb, self.q_stmark[sb])
                qst = s["QSTG"][:, sb * QP:(sb + 1) * QP]
                ins = None
                if v_:
                    eng.tensor_scalar(ql, ql, MAGIC, MAGIC, OP.add, OP.subtract)
                    ins = eng.tensor_scalar(qst, ql, 1.0, -1.0, OP.min, OP.max)
                mvw = self.i(ins, "QW")
                self.q_ldmark[pb] = ("QW", mvw)
                self.w("scalar", "QW", mvw)
                ins = None
                if s_:
                    ins = eng.dma_start(out=self.q_dst(w, p), in_=qst)
                self.i(ins, "QS%d" % sb, 16)
                self.q_stmark[sb] = self.c["QS%d" % sb]

    def walk_attention(self, t_, v_, s_, y_, g_, ld_const, lg_kvg, lg_allw, ln_group):
        eng, s, d = self.eng, self.s, self.d
        T, NG, NCH = self.T, self.NG, self.NCH

        def projections(with_q):
            plist = [("k", "WKT", "KA")] + ([("q", "WQT", "QA")] if with_q else [])
            for nm, W, dstA in plist:
                sc = SCALE if nm == "q" else 1.0
                for j in range(KD):
                    bank = self.P(0) if (j % 2 == 0) else self.P(1)
                    self.w("tensor", "VT")
                    ins = None
                    if t_:
                        for k in range(KD):
                            ins = eng.matmul(bank[:, 0:G],
                                             s[W][:, k * D + j * 128: k * D + j * 128 + 128],
                                             s["U1"][:, k * G:(k + 1) * G],
                                             start=(k == 0), stop=(k == KD - 1))
                    self.i(ins, "TV")
                    tvp = self.c["TV"]
                    self.w("vector", "TV", tvp)
                    ins = None
                    if v_:
                        ins = eng.tensor_scalar(s["TMPE"][:, :], bank[:, 0:G], sc, 0.0,
                                                OP.mult, OP.min)
                    self.i(ins, "VT")
                    self.w("scalar", "VT")
                    ebf = s["TMPE2"][:, :].bitcast(BF16)
                    ins = None
                    if s_:
                        eng.activation(ebf[:, 0:G], s["TMPE"][:, :], AF.Exp)
                        ins = eng.activation(ebf[:, G:2 * G], bank[:, 0:G], AF.Relu, scale=sc)
                    self.i(ins, "SC")
                    self.w("vector", "SC")
                    ins = None
                    if v_:
                        ins = eng.tensor_tensor(s[dstA][:, j * G:(j + 1) * G],
                                                ebf[:, 0:G], ebf[:, G:2 * G], OP.add)
                    self.i(ins, "VT")
            for tt in range(NCH):
                for half in range(2):
                    bank = self.P(0) if (half == 0) else self.P(1)
                    self.w("tensor", "VT")
                    ins = None
                    if t_:
                        for k in range(KD):
                            ins = eng.matmul(
                                bank[:, 0:512],
                                s["U1"][:, k * G + tt * C: k * G + tt * C + C],
                                s["WVT"][:, k * D + half * 512: k * D + half * 512 + 512],
                                start=(k == 0), stop=(k == KD - 1))
                    self.i(ins, "TV")
                    tvv = self.c["TV"]
                    self.w("vector", "TV", tvv)
                    ins = None
                    if v_:
                        ins = eng.tensor_copy(
                            out=s["VA"][:, tt * D + half * 512: tt * D + half * 512 + 512],
                            in_=bank[:, 0:512])
                    self.i(ins, "VT")
            self.w("tensor", "VT")
            ins = None
            if t_:
                for tt in range(NCH):
                    for k in range(KD):
                        eng.matmul(self.P(4)[:, tt * NH:(tt + 1) * NH],
                                   s["U1"][:, k * G + tt * C: k * G + tt * C + C],
                                   s["WGT"][:, k * NH:(k + 1) * NH],
                                   start=(k == 0), stop=False)
                    ins = eng.matmul(self.P(4)[:, tt * NH:(tt + 1) * NH],
                                     s["TRI1"][:, :], s["BGRP"][:, :],
                                     start=False, stop=True)
            self.i(ins, "TV")
            tv_g = self.c["TV"]
            self.w("vector", "TV", tv_g)
            ins = None
            if v_:
                ins = eng.tensor_copy(out=s["ZALL"][:, :], in_=self.P(4)[:, 0:4 * NH])
            self.i(ins, "VT")
            return self.c["VT"]

        def gate_pipeline(tt):
            self.w("scalar", "VT")
            ins = None
            if s_:
                eng.activation(s["SPt"][:, :], s["ZALL"][:, tt * NH:(tt + 1) * NH],
                               AF.Exp, scale=-1.0)
                eng.drain()
                ins = eng.activation(s["SPt"][:, :], s["SPt"][:, :], AF.Ln, bias=1.0)
            self.i(ins, "SC")
            sc_sp = self.c["SC"]
            if self.cut("g_sp"):
                return None
            self.w("tensor", "SC", sc_sp)
            self.w("tensor", "VT")
            ins = None
            if t_:
                eng.matmul(self.P(4)[:, 64:64 + NH], s["TRI1"][:, :], s["SPt"][:, :],
                           start=True, stop=True)
                ins = eng.matmul(self.P(4)[:, 64 + NH:64 + 2 * NH], s["TRI2"][:, :],
                                 s["SPt"][:, :], start=True, stop=True)
            self.i(ins, "TV")
            tv_cum = self.c["TV"]
            self.w("vector", "TV", tv_cum)
            ins = None
            if v_:
                ins = eng.tensor_copy(out=s["CP"][:, :], in_=self.P(4)[:, 64:64 + NH])
            self.i(ins, "VT")
            vt_cp = self.c["VT"]
            self.w("scalar", "TV", tv_cum)
            ins = None
            if s_:
                ins = eng.activation(s["EK"][:, :], self.P(4)[:, 64 + NH:64 + 2 * NH],
                                     AF.Exp, scale=-1.0)
            self.i(ins, "SC")
            sc_ek = self.c["SC"]
            if self.cut("g_ek"):
                return None
            self.w("tensor", "VT", vt_cp)
            self.w("tensor", "SC", sc_ek)
            ins = None
            if t_:
                ins = eng.transpose(self.P(4)[0:NH, 128:256], s["CP"][:, :], s["IDENTF"][:, :])
            self.i(ins, "TV")
            tv_cpt = self.c["TV"]
            self.w("vector", "TV", tv_cpt)
            self.w("vector", "LG")
            ins = None
            if v_:
                ins = eng.tensor_copy(out=s["STAGE17"][0:NH, :], in_=self.P(4)[0:NH, 128:256])
            self.i(ins, "VT")
            vt_cpt = self.c["VT"]
            self.w("scalar", "VT", vt_cpt)
            ins = None
            if s_:
                ins = eng.activation(s["ETT"][:, :], s["STAGE17"][0:NH, :], AF.Exp, scale=-1.0)
            self.i(ins, "SC")
            sc_ett = self.c["SC"]
            self.w("vector", "VT", vt_cpt)
            ins = None
            if v_:
                ins = eng.tensor_scalar(s["AR16"][:, :], s["PARITYF"][:, :],
                                        s["STAGE17"][0:NH, 127:128], None, OP.mult)
            self.i(ins, "VT")
            vt_ar = self.c["VT"]
            if self.cut("g_ett"):
                return None
            self.w("gpsimd", "VT", vt_cpt)
            self.w("gpsimd", "TV")
            self.w("gpsimd", "LG")
            ins = None
            if g_:
                ins = eng.dma_start(
                    out=s["IND17"][NH:NH + 1, :].rearrange("o (h t) -> o h t", h=NH),
                    in_=s["STAGE17"][0:NH, :])
            self.i(ins, "LG", 16)
            lg_flat = self.c["LG"]
            if self.cut("g_flat"):
                return None
            self.w("tensor", "VT", vt_ar)
            ins = None
            if t_:
                ins = eng.matmul(self.P(4)[:, 256:264], s["AR16"][:, :],
                                 s["PAIRSEL"][:, :], start=True, stop=True)
            self.i(ins, "TV")
            tv_drp = self.c["TV"]
            self.w("scalar", "TV", tv_drp)
            ins = None
            if s_:
                ins = eng.activation(s["DRP"][:, :], self.P(4)[:, 256:264], AF.Exp, scale=-1.0)
            self.i(ins, "SC")
            sc_drp = self.c["SC"]
            if self.cut("g_drp"):
                return None
            self.w("vector", "SC", sc_ett)
            ins = None
            if v_:
                ev = s["ETT"][:, :].unsqueeze(1).broadcast_to([NH, 4, 128])
                eng.tensor_tensor(s["EB2"][:, 0:512].rearrange("p (j t) -> p j t", j=4),
                                  s["PJ8"][:, 0:512].rearrange("p (j t) -> p j t", j=4),
                                  ev, OP.mult)
                ins = eng.tensor_tensor(s["EB2"][:, 512:1024].rearrange("p (j t) -> p j t", j=4),
                                        s["PJ8"][:, 512:1024].rearrange("p (j t) -> p j t", j=4),
                                        ev, OP.mult)
            self.i(ins, "VT")
            vt_eb2 = self.c["VT"]
            self.w("tensor", "VT", vt_eb2)
            self.w("tensor", "SC")
            ins = None
            if t_:
                eng.matmul(self.P(5)[:, 0:512], s["PARITYB"][:, :], s["EB2"][:, 0:512],
                           start=True, stop=True)
                ins = eng.matmul(self.P(6)[:, 0:512], s["PARITYB"][:, :], s["EB2"][:, 512:1024],
                                 start=True, stop=True)
            self.i(ins, "TV")
            tv_ebc = self.c["TV"]
            if self.cut("g_ebcmm"):
                return None
            self.w("vector", "TV", tv_ebc)
            ins = None
            if v_:
                eng.tensor_copy(out=s["EBC"][:, 0:512], in_=self.P(5)[:, 0:512])
                ins = eng.tensor_copy(out=s["EBC"][:, 512:1024], in_=self.P(6)[:, 0:512])
            self.i(ins, "VT")
            return {"sc_ek": sc_ek, "sc_drp": sc_drp, "vt_rest": self.c["VT"],
                    "lg_flat": lg_flat}

        def chunk_heads(gi, tt, gm, with_qy):
            self.w("tensor", "VT")
            ins = None
            if t_:
                for j in range(KD):
                    ins = eng.transpose(self.pa[2][:, j * 128:(j + 1) * 128],
                                        s["KA"][:, j * G + tt * C: j * G + tt * C + C],
                                        s["IDENTB"][:, :])
            self.i(ins, "TV")
            tv_ktr = self.c["TV"]
            self.w("vector", "TV", tv_ktr)
            self.w("vector", "SC", gm["sc_ek"])
            ins = None
            if v_:
                ekb = s["EK"][:, :].unsqueeze(2).broadcast_to([128, NH, DH])
                ins = eng.tensor_tensor(
                    s["KH"][:, :].rearrange("p (h e) -> p h e", h=NH),
                    self.pa[2][:, 0:1024].rearrange("p (h e) -> p h e", h=NH),
                    ekb, OP.mult)
            self.i(ins, "VT")
            vt_kh = self.c["VT"]
            if self.cut("c_kh"):
                return
            if with_qy:
                ins = None
                if v_:
                    qsl = s["QA"].rearrange("p (j t) -> p j t", j=KD)[:, :, tt * C:(tt + 1) * C]
                    esl = s["EBC"][:, :].rearrange("p (j t) -> p j t", j=KD)
                    ins = eng.tensor_tensor(
                        s["QT"][:, :].rearrange("p (j t) -> p j t", j=KD), qsl, esl, OP.mult)
                self.i(ins, "VT")
            vt_qt = self.c["VT"]

            for pj in range(8):
                h0, h1 = 2 * pj, 2 * pj + 1
                j = pj
                if with_qy:
                    self.w("tensor", "VT")
                    ins = None
                    if t_:
                        eng.matmul(self.P(0)[:, 0:128],
                                   s["KA"][0:64, j * G + tt * C: j * G + tt * C + C],
                                   s["QA"][0:64, j * G + tt * C: j * G + tt * C + C],
                                   start=True, stop=True)
                        ins = eng.matmul(self.P(1)[:, 0:128],
                                         s["KA"][64:128, j * G + tt * C: j * G + tt * C + C],
                                         s["QA"][64:128, j * G + tt * C: j * G + tt * C + C],
                                         start=True, stop=True)
                    self.i(ins, "TV")
                    tv_sc = self.c["TV"]
                    if self.cut("b_sc"):
                        return
                    self.w("tensor", "LG", gm["lg_flat"])
                    ins = None
                    if t_:
                        bank = self.P(3) if (pj % 2 == 0) else self.P(4)
                        ins = eng.matmul(bank[:, 0:256], s["STAGE17"][:, :],
                                         s["IND17"][:, pj * 256:(pj + 1) * 256],
                                         start=True, stop=True)
                    self.i(ins, "TV")
                    tv_rt = self.c["TV"]
                    self.w("vector", "TV", tv_rt)
                    ins = None
                    if v_:
                        eng.drain()
                        bank = self.P(3) if (pj % 2 == 0) else self.P(4)
                        mb = s["MASK"][:, :].unsqueeze(1).broadcast_to([128, 2, 128])
                        ins = eng.tensor_tensor(
                            s["DL"][:, 0:256].rearrange("p (a t) -> p a t", a=2),
                            bank[:, 0:256].rearrange("p (a t) -> p a t", a=2),
                            mb, OP.add)
                    self.i(ins, "VT")
                    vt_dl = self.c["VT"]
                    self.w("scalar", "VT", vt_dl)
                    ins = None
                    if s_:
                        ins = eng.activation(s["DL"][:, 0:256], s["DL"][:, 0:256],
                                             AF.Exp, scale=-1.0)
                    self.i(ins, "SC")
                    sc_de = self.c["SC"]
                    self.w("vector", "SC", sc_de)
                    self.w("vector", "TV", tv_sc)
                    ins = None
                    if v_:
                        eng.tensor_tensor(s["AP4"][:, 0:128], s["DL"][:, 0:128],
                                          self.P(0)[:, 0:128], OP.mult)
                        ins = eng.tensor_tensor(s["AP4"][:, 128:256], s["DL"][:, 128:256],
                                                self.P(1)[:, 0:128], OP.mult)
                    self.i(ins, "VT")
                    vt_a = self.c["VT"]
                    if self.cut("b_a"):
                        return
                    ybank = self.P(5) if (pj % 2 == 0) else self.P(6)
                    self.w("tensor", "VT", vt_a)
                    self.w("tensor", "SC")
                    ins = None
                    if t_:
                        for z in range(2):
                            h = 2 * pj + z
                            base = z * 64
                            eng.matmul(ybank[base:base + 64, 0:128],
                                       s["VA"][:, tt * D + h * DH: tt * D + (h + 1) * DH],
                                       s["AP4"][:, z * 128:(z + 1) * 128],
                                       start=True, stop=False,
                                       tile_position=(0, base))
                            ins = eng.matmul(ybank[base:base + 64, 0:128],
                                             s["SHIZ"][:, h * DH:(h + 1) * DH],
                                             s["QT"][:, pj * 128:(pj + 1) * 128],
                                             start=False, stop=True,
                                             tile_position=(0, base))
                    self.i(ins, "TV")
                    tv_y = self.c["TV"]
                    if self.cut("b_y"):
                        return
                    self.w("scalar", "TV", tv_y)
                    ins = None
                    if s_:
                        ins = eng.activation(
                            s["YA"][:, pj * G + tt * C: pj * G + tt * C + C],
                            ybank[:, 0:128], AF.Identity)
                    self.i(ins, "SC")
                self.w("tensor", "VT", vt_kh)
                ins = None
                if t_:
                    for z in range(2):
                        h = 2 * pj + z
                        base = z * 64
                        ins = eng.matmul(self.P(7)[base:base + 64, pj * DH:(pj + 1) * DH],
                                         s["KH"][:, h * DH:(h + 1) * DH],
                                         s["VA"][:, tt * D + h * DH: tt * D + (h + 1) * DH],
                                         start=True, stop=True,
                                         tile_position=(0, base))
                self.i(ins, "TV")
                if self.cut("b_ds"):
                    return
            tv_ds_all = self.c["TV"]
            self.w("vector", "TV", tv_ds_all)
            self.w("vector", "SC", gm["sc_drp"])
            ins = None
            if v_:
                for pj in range(8):
                    eng.scalar_tensor_tensor(
                        s["SM"][:, pj * DH:(pj + 1) * DH], s["SM"][:, pj * DH:(pj + 1) * DH],
                        s["DRP"][:, pj:pj + 1], self.P(7)[:, pj * DH:(pj + 1) * DH],
                        OP.mult, OP.add)
                eng.drain()
                smv = s["SM"][:, :].rearrange("p (j c) -> p j c", j=8)
                shz = s["SHIZ"][:, :].rearrange("p (h2 two c) -> p h2 two c", two=2, c=DH)
                eng.tensor_copy(out=shz[0:64, :, 0, :], in_=smv[0:64, :, :])
                ins = eng.tensor_copy(out=shz[64:128, :, 1, :], in_=smv[64:128, :, :])
            self.i(ins, "VT")

        phases = ([("prefix", gi) for gi in range(NG)] if self.do_prefix else []) \
            + [("main", gi) for gi in range(NG)]
        for phase, gi in phases:
            if self.cut("group"):
                return
            is_main = (phase == "main")
            if not is_main:
                self.w("sync", "VT")
                self.w("sync", "TV")
                self.w("sync", "SC")
                self.w("sync", "LX")
                ins = None
                if y_:
                    ins = eng.dma_start(
                        out=s["XP"].rearrange("p (k t) -> p k t", k=KD),
                        in_=d["xpT"].rearrange("(k p) t -> p k t", p=128)[:, :, gi * G:(gi + 1) * G])
                self.i(ins, "LX", 16)
                ld_x = self.c["LX"]
                for e in ("tensor", "vector", "scalar"):
                    self.w(e, "LX", ld_x)
                    self.w(e, "LD", ld_const)
                self.w("tensor", "LG", lg_kvg)
                xsrc, g0, Tx = s["XP"], 0, G
            else:
                for e in ("tensor", "vector", "scalar"):
                    self.w(e, "LD", ld_const)
                self.w("tensor", "LG", lg_allw)
                xsrc, g0, Tx = s["XF"], gi * G, T
            ln_group(xsrc, Tx, g0, G, s["U1"], s["LNR"], s["TMPE"], s["TMPE2"])
            if self.cut("ln"):
                return
            projections(with_q=is_main)
            if self.cut("proj"):
                return
            self.svc(25)
            for tt in range(NCH):
                gm = gate_pipeline(tt)
                if gm is None:
                    return
                if self.cut("gate"):
                    return
                self.svc(8)
                chunk_heads(gi, tt, gm, with_qy=is_main)
                if self.cut("chunk"):
                    return
                self.svc(18)
            if is_main:
                for j in range(KD):
                    bank = self.P(0) if (j % 2 == 0) else self.P(1)
                    self.w("tensor", "VT")
                    self.w("tensor", "SC")
                    ins = None
                    if t_:
                        for k in range(KD):
                            ins = eng.matmul(
                                bank[:, 0:G],
                                s["WOT"][:, k * D + j * 128: k * D + j * 128 + 128],
                                s["YA"][:, k * G:(k + 1) * G],
                                start=(k == 0), stop=(k == KD - 1))
                    self.i(ins, "TV")
                    tvw = self.c["TV"]
                    self.w("vector", "TV", tvw)
                    ins = None
                    if v_:
                        xc = s["XF"][:, j * T + gi * G: j * T + gi * G + G]
                        ins = eng.tensor_tensor(xc, bank[:, 0:G], xc, OP.add)
                    self.i(ins, "VT")

        if self.debug_xmid:
            vt_all = self.w("sync", "VT")
            ins = None
            if y_:
                ins = eng.dma_start(out=d["out"], in_=s["XF"][:, :])
            self.i(ins, "ST", 16)

    def walk_ffn(self, t_, v_, s_, y_, g_, ln_group):
        eng, s, d = self.eng, self.s, self.d
        T = self.T
        if self.debug_xmid:
            return

        while self.qi < len(QSTEPS):
            self.svc(1)
        qs0_final, qs1_final = self.c["QS0"], self.c["QS1"]
        qf_final = self.c["QF"]

        for e in ENGINES:
            for sm in ("VT", "SC", "TV", "LD", "LG", "LX"):
                self.w(e, sm)
        self.w("vector", "QF", qf_final)
        ins = None
        if y_:
            eng.dma_start(out=s["B1R"][:, :], in_=d["b1r"]).then_inc(self.sems["LW"], 16)
            ins = eng.dma_start(out=s["B2R"], in_=d["b2r"])
        self.c["LW"] += 16
        self.i(ins, "LW", 16)
        ld_fconst = self.c["LW"]
        for e in ("tensor", "vector", "scalar"):
            self.w(e, "LW", ld_fconst)
        self.w("gpsimd", "QS0", qs0_final)
        self.w("gpsimd", "QS1", qs1_final)
        ins = None
        if g_:
            ins = eng.dma_start(
                out=s["W1Q"][:, :].rearrange("p (k f) -> p k f", k=KD),
                in_=d["w1q_dram"].rearrange("(k p) f -> p k f", p=128))
        self.i(ins, "LW", 16)
        lw_w1q = self.c["LW"]

        self.w("tensor", "VT")
        ins = None
        if t_:
            eng.matmul(self.P(2)[:, 0:512], s["ONES1"][0:1, :], s["B2R"][0:1, 0:512],
                       start=True, stop=True)
            ins = eng.matmul(self.P(3)[:, 0:512], s["ONES1"][0:1, :], s["B2R"][0:1, 512:1024],
                             start=True, stop=True)
        self.i(ins, "TV")
        tvb2 = self.c["TV"]
        self.w("vector", "TV", tvb2)
        ins = None
        if v_:
            eng.tensor_copy(out=s["B2BC"][:, 0:512], in_=self.P(2)[:, 0:512])
            ins = eng.tensor_copy(out=s["B2BC"][:, 512:1024], in_=self.P(3)[:, 0:512])
        self.i(ins, "VT")

        NFG = self.NFG
        prev_f2_marks = [None, None]
        RO = s["ROWS"]
        for fg in range(NFG):
            if self.cut("ffngroup"):
                return
            g0 = fg * FG
            ln_group(s["XF"], T, g0, FG, s["C1"], s["ROWS"][0:1, 0:2 * G],
                     s["FTMP"], s["TQ1"])
            ins = None
            if v_:
                eng.tensor_tensor(RO[0:1, 3 * G:3 * G + FG], RO[0:1, G:2 * G],
                                  RO[0:1, G:2 * G], OP.mult)
                eng.drain()
                ins = eng.tensor_scalar(RO[0:1, 3 * G:3 * G + FG], RO[0:1, 3 * G:3 * G + FG],
                                        -EPS, 1.0 + EPS, OP.mult, OP.add)
            self.i(ins, "VT")
            self.w("scalar", "VT")
            ins = None
            if s_:
                eng.activation(RO[0:1, 2 * G:2 * G + FG], RO[0:1, 3 * G:3 * G + FG], AF.Ln)
                eng.drain()
                ins = eng.activation(RO[0:1, 2 * G:2 * G + FG], RO[0:1, 2 * G:2 * G + FG],
                                     AF.Exp, scale=-0.5)
            self.i(ins, "SC")
            sc_rho = self.c["SC"]
            self.w("vector", "ST")
            ins = None
            if v_:
                eng.memset(s["MAXA"][:, 0:FG], 0.0)
                for k in range(KD):
                    c = s["C1"][:, k * FG:(k + 1) * FG]
                    eng.scalar_tensor_tensor(s["FTMP"][:, 0:FG], c, -1.0, c,
                                             OP.mult, OP.max)
                    eng.drain()
                    ins = eng.tensor_tensor(s["MAXA"][:, 0:FG], s["MAXA"][:, 0:FG],
                                            s["FTMP"][:, 0:FG], OP.max)
                    eng.drain()
            self.i(ins, "VT")
            vt_mx = self.c["VT"]
            self.cross_max(t_, v_, s_, y_, g_, s["MAXA"][:, 0:FG], FG, vt_mx, s["ROWS"], 3 * G)
            self.w("vector", "SC", sc_rho)
            ins = None
            if v_:
                eng.tensor_tensor(RO[0:1, 3 * G:3 * G + FG], RO[0:1, 2 * G:2 * G + FG],
                                  RO[0:1, 3 * G:3 * G + FG], OP.mult)
                eng.drain()
                eng.tensor_scalar(RO[0:1, 3 * G:3 * G + FG], RO[0:1, 3 * G:3 * G + FG],
                                  1e-5, None, OP.max)
                eng.drain()
                ins = eng.tensor_scalar(RO[0:1, 4 * G:4 * G + FG], RO[0:1, 3 * G:3 * G + FG],
                                        s["MABP"][0:1, 0:1], 1.0 / 127.0, OP.mult, OP.mult)
            self.i(ins, "VT")
            vt_inv = self.c["VT"]
            self.w("tensor", "VT", vt_inv)
            ins = None
            if t_:
                ins = eng.matmul(self.P(6)[:, 0:FG], s["ONES1"][0:1, :],
                                 RO[0:1, 4 * G:4 * G + FG], start=True, stop=True)
            self.i(ins, "TV")
            tv_inv = self.c["TV"]
            ins = None
            if v_:
                self.w("vector", "TV", tv_inv)
                eng.reciprocal(RO[0:1, 4 * G:4 * G + FG], RO[0:1, 3 * G:3 * G + FG])
                eng.drain()
                eng.tensor_scalar(RO[0:1, 4 * G:4 * G + FG], RO[0:1, 4 * G:4 * G + FG],
                                  127.0, None, OP.mult)
                eng.drain()
                eng.tensor_tensor(RO[0:1, 4 * G:4 * G + FG], RO[0:1, 4 * G:4 * G + FG],
                                  RO[0:1, 2 * G:2 * G + FG], OP.mult)
                eng.drain()
                eng.tensor_tensor(RO[0:1, 2 * G:2 * G + FG], RO[0:1, G:2 * G],
                                  RO[0:1, 4 * G:4 * G + FG], OP.mult)
                ins = eng.tensor_tensor(RO[0:1, 3 * G:3 * G + FG], RO[0:1, 0:G],
                                        RO[0:1, 4 * G:4 * G + FG], OP.mult)
            self.i(ins, "VT")
            vt_rows = self.c["VT"]
            self.w("tensor", "VT", vt_rows)
            ins = None
            if t_:
                eng.matmul(self.P(2)[:, 0:FG], s["ONES1"][0:1, :],
                           RO[0:1, 2 * G:2 * G + FG], start=True, stop=True)
                ins = eng.matmul(self.P(3)[:, 0:FG], s["ONES1"][0:1, :],
                                 RO[0:1, 3 * G:3 * G + FG], start=True, stop=True)
            self.i(ins, "TV")
            tv_qb = self.c["TV"]
            self.w("vector", "TV", tv_qb)
            ins = None
            if v_:
                ins = eng.tensor_copy(out=s["QB2"][:, 0:FG], in_=self.P(6)[:, 0:FG])
            self.i(ins, "VT")
            ins = None
            for k0 in range(0, KD, 2):
                c0 = s["C1"][:, k0 * FG:(k0 + 1) * FG]
                c1_ = s["C1"][:, (k0 + 1) * FG:(k0 + 2) * FG]
                x0 = s["XF"][:, k0 * T + g0: k0 * T + g0 + FG]
                x1 = s["XF"][:, (k0 + 1) * T + g0: (k0 + 1) * T + g0 + FG]
                if v_:
                    eng.tensor_tensor(c0, x0, self.P(2)[:, 0:FG], OP.mult)
                    eng.tensor_tensor(c1_, x1, self.P(2)[:, 0:FG], OP.mult)
                    eng.drain()
                    eng.tensor_tensor(c0, c0, self.P(3)[:, 0:FG], OP.add)
                    eng.tensor_tensor(c1_, c1_, self.P(3)[:, 0:FG], OP.add)
                    eng.drain()
                    eng.tensor_scalar(c0, c0, MAGIC, MAGIC, OP.add, OP.subtract)
                    eng.tensor_scalar(c1_, c1_, MAGIC, MAGIC, OP.add, OP.subtract)
                    eng.drain()
                    eng.tensor_scalar(c0, c0, 127.0, -128.0, OP.min, OP.max)
                    ins = eng.tensor_scalar(c1_, c1_, 127.0, -128.0, OP.min, OP.max)
            self.i(ins, "VT")
            vt_c1 = self.c["VT"]
            for tt in range(4):
                for k in range(KD):
                    self.w("tensor", "VT")
                    ins = None
                    if t_:
                        ins = eng.transpose(self.P(4)[:, 0:128] if (k % 2 == 0) else self.P(5)[:, 0:128],
                                            s["XF"][:, k * T + g0 + tt * 128: k * T + g0 + tt * 128 + 128],
                                            s["IDENTF"][:, :])
                    self.i(ins, "TV")
                    tvt = self.c["TV"]
                    self.w("vector", "TV", tvt)
                    ins = None
                    if v_:
                        bank = self.P(4) if (k % 2 == 0) else self.P(5)
                        ins = eng.tensor_tensor(s["XT2"][:, tt * D + k * 128:(k + 1) * 128 + tt * D],
                                                bank[:, 0:128], s["B2BC"][:, k * 128:(k + 1) * 128],
                                                OP.add)
                    self.i(ins, "VT")
            if fg == 0:
                self.w("tensor", "LW", lw_w1q)
            self.w("vector", "ST")
            ins = None
            if v_:
                ins = eng.memset(s["MAXA"][:, 0:FG], 0.0)
            self.i(ins, "VT")
            for Fj in range(KF):
                bank = self.P(0) if (Fj % 2 == 0) else self.P(1)
                hsl = s["H"][:, Fj * FG:(Fj + 1) * FG]
                self.w("tensor", "VT")
                ins = None
                if t_:
                    for k in range(KD):
                        ins = eng.matmul(bank[:, 0:FG],
                                         s["W1Q"][:, k * FF + Fj * 128: k * FF + Fj * 128 + 128],
                                         s["C1"][:, k * FG:(k + 1) * FG],
                                         start=(k == 0), stop=(k == KD - 1))
                self.i(ins, "TV")
                tvf = self.c["TV"]
                self.w("vector", "TV", tvf)
                ins = None
                if v_:
                    ins = eng.tensor_tensor(s["TQ1"][:, 0:FG], bank[:, 0:FG],
                                            s["QB2"][:, 0:FG], OP.mult)
                self.i(ins, "VT")
                vt_t1 = self.c["VT"]
                self.w("scalar", "VT", vt_t1)
                ins = None
                if s_:
                    ins = eng.activation(hsl, s["TQ1"][:, 0:FG],
                                         AF.Gelu, bias=s["B1R"][:, Fj:Fj + 1])
                self.i(ins, "SC")
                sch = self.c["SC"]
                self.w("vector", "SC", sch)
                if Fj > 0:
                    self.w("vector", "TV", ssq_mark)
                if Fj > 1:
                    pass
                ins = None
                if v_:
                    eng.tensor_tensor(s["TQ1"][:, 0:FG], hsl, hsl, OP.mult)
                    eng.scalar_tensor_tensor(s["FTMP"][:, 0:FG], hsl, -1.0, hsl,
                                             OP.mult, OP.max)
                    eng.drain()
                    ins = eng.tensor_tensor(s["MAXA"][:, 0:FG], s["MAXA"][:, 0:FG],
                                            s["FTMP"][:, 0:FG], OP.max)
                self.i(ins, "VT")
                vt_sq = self.c["VT"]
                self.w("tensor", "VT", vt_sq)
                ins = None
                if t_:
                    ins = eng.matmul(self.P(6)[0:1, 0:FG], s["ONESCB"][:, :], s["TQ1"][:, 0:FG],
                                     start=(Fj == 0), stop=(Fj == KF - 1))
                self.i(ins, "TV")
                ssq_mark = self.c["TV"]
            tv_ssq = self.c["TV"]
            vt_mx2 = self.c["VT"]
            self.w("vector", "TV", tv_ssq)
            ins = None
            if v_:
                ins = eng.tensor_scalar(RO[0:1, 3 * G:3 * G + FG], self.P(6)[0:1, 0:FG],
                                        1.0 / FF, EPS, OP.mult, OP.add)
            self.i(ins, "VT")
            self.w("scalar", "VT")
            ins = None
            if s_:
                eng.activation(RO[0:1, 2 * G:2 * G + FG], RO[0:1, 3 * G:3 * G + FG], AF.Ln)
                eng.drain()
                ins = eng.activation(RO[0:1, 2 * G:2 * G + FG], RO[0:1, 2 * G:2 * G + FG],
                                     AF.Exp, scale=-0.5)
            self.i(ins, "SC")
            sc_rho2 = self.c["SC"]
            self.cross_max(t_, v_, s_, y_, g_, s["MAXA"][:, 0:FG], FG, vt_mx2, s["ROWS"], 3 * G)
            self.w("vector", "SC", sc_rho2)
            ins = None
            if v_:
                eng.tensor_tensor(RO[0:1, 3 * G:3 * G + FG], RO[0:1, 2 * G:2 * G + FG],
                                  RO[0:1, 3 * G:3 * G + FG], OP.mult)
                eng.drain()
                eng.tensor_scalar(RO[0:1, 3 * G:3 * G + FG], RO[0:1, 3 * G:3 * G + FG],
                                  1e-5, None, OP.max)
                eng.drain()
                eng.reciprocal(RO[0:1, 4 * G:4 * G + FG], RO[0:1, 3 * G:3 * G + FG])
                eng.drain()
                eng.tensor_scalar(RO[0:1, 4 * G:4 * G + FG], RO[0:1, 4 * G:4 * G + FG],
                                  127.0, None, OP.mult)
                eng.drain()
                ins = eng.tensor_tensor(RO[0:1, 4 * G:4 * G + FG], RO[0:1, 4 * G:4 * G + FG],
                                        RO[0:1, 2 * G:2 * G + FG], OP.mult)
            self.i(ins, "VT")
            vt_q2 = self.c["VT"]
            self.w("tensor", "VT", vt_q2)
            ins = None
            if t_:
                ins = eng.matmul(self.P(2)[:, 0:FG], s["ONES1"][0:1, :],
                                 RO[0:1, 4 * G:4 * G + FG], start=True, stop=True)
            self.i(ins, "TV")
            tv_q2b = self.c["TV"]
            self.w("vector", "TV", tv_q2b)
            ins = None
            if v_:
                ins = eng.tensor_copy(out=s["QB1"][:, 0:FG], in_=self.P(2)[:, 0:FG])
            self.i(ins, "VT")
            self.w("tensor", "VT")
            ins = None
            if t_:
                for tt in range(4):
                    ins = eng.matmul(self.P(3)[:, tt:tt + 1],
                                     RO[0:1, 2 * G + tt * 128: 2 * G + (tt + 1) * 128],
                                     s["ONES1"][0:1, 0:1], start=True, stop=True)
            self.i(ins, "TV")
            tvr = self.c["TV"]
            self.w("vector", "TV", tvr)
            ins = None
            if v_:
                eng.tensor_tensor(s["IV2"][:, 0:4], self.P(3)[:, 0:4],
                                  s["MXC"][:, 0:4], OP.mult)
                eng.drain()
                eng.tensor_scalar(s["IV2"][:, 0:4], s["IV2"][:, 0:4],
                                  1e-5, None, OP.max)
                eng.drain()
                ins = eng.tensor_scalar(s["IV2"][:, 0:4], s["IV2"][:, 0:4],
                                        s["MABP"][:, 1:2], 1.0 / 127.0, OP.mult, OP.mult)
            self.i(ins, "VT")
            HB = KF * FG // 2
            ins = None
            if v_:
                hv = s["H"][:, :]
                for h0 in (0, HB):
                    eng.tensor_tensor(
                        hv[:, h0:h0 + HB].rearrange("p (k t) -> p k t", t=FG),
                        hv[:, h0:h0 + HB].rearrange("p (k t) -> p k t", t=FG),
                        s["QB1"][:, 0:FG].unsqueeze(1).broadcast_to([128, KF // 2, FG]),
                        OP.mult)
                eng.drain()
                eng.tensor_scalar(hv[:, 0:HB], hv[:, 0:HB], MAGIC, MAGIC, OP.add, OP.subtract)
                eng.tensor_scalar(hv[:, HB:2 * HB], hv[:, HB:2 * HB], MAGIC, MAGIC,
                                  OP.add, OP.subtract)
                eng.drain()
                eng.tensor_scalar(hv[:, 0:HB], hv[:, 0:HB], 127.0, -128.0, OP.min, OP.max)
                ins = eng.tensor_scalar(hv[:, HB:2 * HB], hv[:, HB:2 * HB], 127.0, -128.0,
                                        OP.min, OP.max)
            self.i(ins, "VT")
            vt_c2 = self.c["VT"]
            NPC = 16
            def load_piece(pc, prev_mm_mark):
                buf = s["W2S0"] if (pc % 2 == 0) else s["W2S1"]
                sem = "LS" if (pc % 2 == 0) else "LS2"
                if prev_mm_mark is not None:
                    self.w("scalar", "TV", prev_mm_mark)
                self.w("scalar", sem)
                if fg == 0 and pc == 0:
                    self.w("scalar", "QS0", qs0_final)
                    self.w("scalar", "QS1", qs1_final)
                ins = None
                if s_:
                    ins = eng.dma_start(
                        out=buf[:, :].rearrange("p (a f) -> p a f", a=2),
                        in_=d["w2q_dram"].rearrange("(a p) f -> p a f", p=128)[:, 2 * pc:2 * pc + 2, :])
                return self.i(ins, sem, 16)

            mm_marks = [None] * NPC
            ld_marks = [None] * NPC
            ld_marks[0] = load_piece(0, prev_f2_marks[0])
            ld_marks[1] = load_piece(1, prev_f2_marks[1])
            self.w("tensor", "VT", vt_c2)
            for pc in range(NPC):
                buf = s["W2S0"] if (pc % 2 == 0) else s["W2S1"]
                self.w("tensor", "LS" if (pc % 2 == 0) else "LS2", ld_marks[pc])
                ins = None
                if t_:
                    for a in range(2):
                        Fk = 2 * pc + a
                        for tt in range(4):
                            for half in range(2):
                                ins = eng.matmul(
                                    self.pf[tt * 2 + half][:, 0:512],
                                    s["H"][:, Fk * FG + tt * 128: Fk * FG + tt * 128 + 128],
                                    buf[:, a * D + half * 512: a * D + half * 512 + 512],
                                    start=(Fk == 0), stop=(Fk == KF - 1))
                self.i(ins, "TV")
                mm_marks[pc] = self.c["TV"]
                if pc + 2 < NPC:
                    ld_marks[pc + 2] = load_piece(pc + 2, mm_marks[pc])
            prev_f2_marks = [mm_marks[NPC - 2], mm_marks[NPC - 1]]
            tv_f2 = self.c["TV"]
            for tt in range(4):
                for half in range(2):
                    self.w("vector", "TV", tv_f2)
                    self.w("vector", "ST")
                    ins = None
                    if v_:
                        ins = eng.scalar_tensor_tensor(
                            s["OST"][:, 0:512],
                            self.pf[tt * 2 + half][:, 0:512],
                            s["IV2"][:, tt:tt + 1],
                            s["XT2"][:, tt * D + half * 512: tt * D + half * 512 + 512],
                            OP.mult, OP.add)
                    self.i(ins, "VT")
                    vto = self.c["VT"]
                    self.w("sync", "VT", vto)
                    self.w("sync", "ST")
                    ins = None
                    if y_:
                        ins = eng.dma_start(
                            out=d["out"][g0 + tt * 128: g0 + tt * 128 + 128,
                                         half * 512:(half + 1) * 512],
                            in_=s["OST"][:, 0:512])
                    self.i(ins, "ST", 16)

    def cross_max(self, t_, v_, s_, y_, g_, src, Gw, vt_src, rowdst, rowoff, keep_cols=False):
        eng, s = self.eng, self.s
        nt = Gw // 128
        for tt in range(nt):
            self.w("tensor", "VT", vt_src)
            self.w("tensor", "VT")
            ins = None
            if t_:
                ins = eng.transpose(self.P(4)[:, 0:128] if tt % 2 == 0 else self.P(5)[:, 0:128],
                                    src[:, tt * 128:(tt + 1) * 128], s["IDENTF"][:, :])
            self.i(ins, "TV")
            tvt = self.c["TV"]
            self.w("vector", "TV", tvt)
            ins = None
            if v_:
                bank = self.P(4) if tt % 2 == 0 else self.P(5)
                ins = eng.tensor_reduce(s["MXC"][:, tt:tt + 1], bank[:, 0:128],
                                        AX, OP.max, apply_absolute_value=True)
            self.i(ins, "VT")
        vt_cols = self.c["VT"]
        self.w("tensor", "VT", vt_cols)
        ins = None
        if t_:
            ins = eng.transpose(self.P(4)[0:nt, 128:256], s["MXC"][:, 0:nt], s["IDENTF"][:, :])
        self.i(ins, "TV")
        tvt = self.c["TV"]
        self.w("vector", "TV", tvt)
        ins = None
        if v_:
            ins = eng.tensor_copy(out=s["TQ1"][0:nt, 0:128], in_=self.P(4)[0:nt, 128:256])
        self.i(ins, "VT")
        vtc = self.c["VT"]
        self.w("gpsimd", "VT", vtc)
        self.w("gpsimd", "SC")
        self.w("gpsimd", "LG")
        ins = None
        if g_:
            ins = eng.dma_start(
                out=rowdst[0:1, rowoff:rowoff + Gw].rearrange("o (a t) -> o a t", a=nt),
                in_=s["TQ1"][0:nt, 0:128])
        self.i(ins, "LG", 16)
        lgf = self.c["LG"]
        self.w("vector", "LG", lgf)
        return self.c["LG"]



def _consts():
    i = np.arange(128)
    k16 = np.arange(NH)
    tri1 = (i[:, None] <= i[None, :]).astype(ml_dtypes.bfloat16)
    tri2 = (i[:, None] > i[None, :]).astype(ml_dtypes.bfloat16)
    mask = np.where(i[:, None] <= i[None, :], 0.0, -MASKVAL).astype(ml_dtypes.bfloat16)
    parity = (i[None, :] // 64 == k16[:, None] % 2).astype(np.float32)
    pairsel = (k16[:, None] // 2 == np.arange(8)[None, :]).astype(np.float32)
    pj8 = np.repeat((k16[:, None] // 2 == np.arange(8)[None, :]).astype(ml_dtypes.bfloat16),
                    128, axis=1)
    ind16 = np.zeros((NH, NH * 128), np.float32)
    for k in range(NH):
        ind16[k, k * 128:(k + 1) * 128] = -1.0
    return dict(
        c_identb=np.eye(128, dtype=ml_dtypes.bfloat16),
        c_identf=np.eye(128, dtype=np.float32),
        c_tri1=tri1, c_tri2=tri2, c_mask=mask, c_ind16=ind16,
        c_parityf=parity, c_parityb=parity.astype(ml_dtypes.bfloat16),
        c_pairsel=pairsel, c_pj8=pj8,
        c_ones1=np.ones((1, 128), np.float32),
        c_qcol=np.concatenate([np.full((128, 1), MAGIC, np.float32),
                               np.full((128, 1), -1.0, np.float32),
                               np.zeros((128, 1), np.float32)],
                              axis=1),
        c_onesc=np.ones((128, 1), np.float32),
        c_onescb=np.ones((128, 1), ml_dtypes.bfloat16),
    )


def make_inputs_for_core(core, x, Wq, Wk, Wv, Wo, Wg, bgv, w1, b1, w2, b2, T=2048):
    b, sq, _ = x.shape
    xf = np.asarray(x, np.float32).reshape(b * sq, D)
    seq = core // 2
    half = core % 2
    own = xf[seq * sq + half * T: seq * sq + (half + 1) * T]
    if half == 1:
        pfx = xf[seq * sq: seq * sq + T]
    else:
        pfx = np.zeros((T, D), np.float32)
    ins = dict(
        xT=np.ascontiguousarray(own.T),
        xpT=np.ascontiguousarray(pfx.T),
        wqT=np.ascontiguousarray(np.asarray(Wq, np.float32).T),
        wkT=np.ascontiguousarray(np.asarray(Wk, np.float32).T),
        wvT=np.ascontiguousarray(np.asarray(Wv, np.float32).T),
        woT=np.ascontiguousarray(np.asarray(Wo, np.float32).T),
        wgT=np.ascontiguousarray(np.asarray(Wg, np.float32).T),
        bgp=np.concatenate([np.asarray(bgv, ml_dtypes.bfloat16).reshape(1, NH),
                            np.zeros((127, NH), ml_dtypes.bfloat16)], axis=0),
        w1T=np.ascontiguousarray(np.asarray(w1, np.float32).T),
        w2T=np.ascontiguousarray(np.asarray(w2, np.float32).T),
        b1r=np.ascontiguousarray(np.asarray(b1, np.float32).reshape(KF, 128).T),
        b2r=np.asarray(b2, np.float32).reshape(1, D),
    )
    ins.update(_consts())
    return ins


_NC_CACHE = {}


def kernel(L0_attn_x, L0_attn_Wq, L0_attn_Wk, L0_attn_Wv, L0_attn_Wo, L0_attn_Wg,
           L0_attn_bg, L0_ffn_w1, L0_ffn_b1, L0_ffn_w2, L0_ffn_b2):
    T = 2048
    key = ("full", T)
    if key not in _NC_CACHE:
        _NC_CACHE[key] = Builder(T=T, do_prefix=True).build()
    nc = _NC_CACHE[key]
    in_maps = [
        make_inputs_for_core(c, L0_attn_x, L0_attn_Wq, L0_attn_Wk, L0_attn_Wv,
                             L0_attn_Wo, L0_attn_Wg, L0_attn_bg,
                             L0_ffn_w1, L0_ffn_b1, L0_ffn_w2, L0_ffn_b2, T=T)
        for c in range(NCORES)
    ]
    res = run_bass_kernel_spmd(nc, in_maps, core_ids=list(range(NCORES)))
    outs = [np.asarray(r["out"], np.float32) for r in res.results]
    b, sq, _ = L0_attn_x.shape
    full = np.empty((b * sq, D), np.float32)
    for c in range(NCORES):
        seq, half = c // 2, c % 2
        full[seq * sq + half * T: seq * sq + (half + 1) * T] = outs[c]
    return full.reshape(b, sq, D)


# version 16
# speedup vs baseline: 1.0741x; 1.0741x over previous
import numpy as np
import ml_dtypes
import concourse.bass as bass
import concourse.mybir as mybir
from concourse.bass_utils import run_bass_kernel_spmd
from contextlib import ExitStack

F32 = mybir.dt.float32
BF16 = mybir.dt.bfloat16
AF = mybir.ActivationFunctionType
OP = mybir.AluOpType
AX = mybir.AxisListType.X

MAGIC = 12582912.0
MASKVAL = -30000.0
EPS = 1e-5
D = 1024
NH = 16
DH = 64
FF = 4096
C = 128
SCALE = 0.125
NCORES = 8
G = 512
FG = 512
KD = D // 128
KF = FF // 128

ENGINES = ("tensor", "vector", "scalar", "sync", "gpsimd")


class Pool:

    def __init__(self, base_ap_f32, nwords):
        self.base = base_ap_f32
        self.n = nwords
        self.off = 0

    def tensor(self, shape, dt):
        P, F = shape
        words = (F * (2 if dt == BF16 else 4) + 3) // 4
        words = (words + 7) & ~7
        assert self.off + words <= self.n, (self.off, words, self.n)
        view = self.base[:, self.off:self.off + words]
        self.off += words
        if dt == BF16:
            view = view.bitcast(BF16)
        return view[0:P, 0:F]


QP = 256
QNP = (D * FF) // (128 * QP)


def _mk_qphase(kind, w, np_):
    st = [("ld", w, 0), ("ld", w, 1)]
    for p in range(np_):
        if p + 2 < np_:
            st.append(("ld", w, p + 2))
        st.append((kind, w, p))
    return st


QSTEPS = (_mk_qphase("am", 0, QNP) + [("ws", 0)]
          + _mk_qphase("am", 1, QNP) + [("ws", 1)])


class Builder:
    def __init__(self, T=2048, do_prefix=True, debug_xmid=False, stop_stage=None):
        self.T = T
        self.do_prefix = do_prefix
        self.debug_xmid = debug_xmid
        self.stop_stage = stop_stage
        self.NG = T // G
        self.NFG = T // FG
        self.NCH = G // C

    def cut(self, label):
        if self.stop_stage is None:
            return False
        if self._nstage >= self.stop_stage:
            return True
        self._nstage += 1
        return False

    def on(self, e):
        return self.ename == e

    def w(self, engine, sem, val=None):
        if val is None:
            val = self.c[sem]
        if self.ename == engine:
            self.eng.wait_ge(self.sems[sem], val)
        return val

    def i(self, instr, sem, n=1):
        self.c[sem] += n
        if instr is not None:
            instr.then_inc(self.sems[sem], n)
        return self.c[sem]

    def build(self):
        nc = bass.Bass("TRN2", target_bir_lowering=False, debug=False)
        self.nc = nc
        T = self.T

        def din(name, shape, dt=F32):
            return nc.dram_tensor(name, shape, dt, kind="ExternalInput").ap()

        d = self.d = {}
        d["xT"] = din("xT", [D, T])
        d["xpT"] = din("xpT", [D, T])
        for nmm in ("wqT", "wkT", "wvT", "woT"):
            d[nmm] = din(nmm, [D, D])
        d["wgT"] = din("wgT", [D, NH])
        d["bgp"] = din("bgp", [128, NH], BF16)
        d["w1T"] = din("w1T", [D, FF])
        d["w2T"] = din("w2T", [FF, D])
        d["b1r"] = din("b1r", [128, KF])
        d["b2r"] = din("b2r", [1, D])
        d["c_identb"] = din("c_identb", [128, 128], BF16)
        d["c_identf"] = din("c_identf", [128, 128], F32)
        d["c_tri1"] = din("c_tri1", [128, 128], BF16)
        d["c_tri2"] = din("c_tri2", [128, 128], BF16)
        d["c_mask"] = din("c_mask", [128, 128], BF16)
        d["c_parityf"] = din("c_parityf", [NH, 128], F32)
        d["c_parityb"] = din("c_parityb", [NH, 128], BF16)
        d["c_pairsel"] = din("c_pairsel", [NH, 8], F32)
        d["c_pj8"] = din("c_pj8", [NH, 8 * 128], BF16)
        d["c_ind16"] = din("c_ind16", [NH, NH * 128], F32)
        d["c_ones1"] = din("c_ones1", [1, 128], F32)
        d["c_qcol"] = din("c_qcol", [128, 3], F32)
        d["c_onesc"] = din("c_onesc", [128, 1], F32)
        d["c_onescb"] = din("c_onescb", [128, 1], BF16)
        if self.debug_xmid or self.stop_stage is not None:
            d["out"] = nc.dram_tensor("out", [128, KD * T], F32, kind="ExternalOutput").ap()
        else:
            d["out"] = nc.dram_tensor("out", [T, D], F32, kind="ExternalOutput").ap()
        d["w2q_dram"] = nc.dram_tensor("w2q_scratch", [FF, D], BF16).ap()

        es = ExitStack()
        SB = lambda n, s, dt: es.enter_context(nc.sbuf_tensor(n, s, dt))
        with es:
            s = self.s = {}
            for nmm, shape, dt in [
                ("IDENTB", [128, 128], BF16), ("IDENTF", [128, 128], F32),
                ("ONES1", [1, 128], F32), ("ONESC", [128, 1], F32), ("ONESCB", [128, 1], BF16),
            ]:
                s[nmm] = SB(nmm, shape, dt)
            s["XF"] = SB("XF", [128, KD * T], F32)
            s["SM"] = SB("SM", [128, 8 * DH], F32)
            s["SHIZ"] = SB("SHIZ", [128, NH * DH], BF16)
            s["WSC"] = SB("WSC", [128, 2], F32)
            s["MAB"] = SB("MAB", [1, 2], F32)
            s["MABP"] = SB("MABP", [128, 2], F32)

            POOLW = (nc.sbuf_bytes_remaining // 4) & ~7
            s["POOL"] = SB("POOL", [128, POOLW], F32)
            att = Pool(s["POOL"][:, :], POOLW)
            SBa = lambda n, sh, dt: att.tensor(sh, dt)
            for nmm, shape, dt in [
                ("TRI1", [128, 128], BF16), ("TRI2", [128, 128], BF16),
                ("MASK", [128, 128], BF16),
                ("PARITYF", [NH, 128], F32), ("PARITYB", [NH, 128], BF16),
                ("PAIRSEL", [NH, 8], F32), ("PJ8", [NH, 8 * 128], BF16),
                ("IND17F", [128, NH * 128], F32), ("BGRP", [128, NH], BF16),
                ("WQT", [128, KD * D], BF16), ("WKT", [128, KD * D], BF16),
                ("WVT", [128, KD * D], BF16), ("WOT", [128, KD * D], BF16),
                ("WGT", [128, KD * NH], BF16),
                ("U1", [128, KD * G], BF16),
                ("KA", [128, KD * G], BF16), ("VA", [128, 4 * D], BF16),
                ("XPU", [128, KD * G], F32),
                ("QT", [128, 8 * C], BF16), ("KH", [128, NH * DH], BF16),
                ("EBC", [128, 8 * C], BF16),
                ("ZALL", [128, 4 * NH], F32),
                ("SPt", [128, NH], BF16), ("CP", [128, NH], F32),
                ("EK", [128, NH], F32), ("STAGE17", [NH + 1, 128], F32),
                ("ETT", [NH, 128], F32), ("AR16", [NH, 128], F32),
                ("EB2", [NH, 8 * 128], BF16), ("DRP", [128, 8], F32),
                ("TMPE", [128, 512], F32), ("TMPE2", [128, 512], F32),
                ("QLD", [128, 2 * QP], F32), ("QSTG", [128, 2 * QP], BF16),
                ("QACC", [128, QP], F32), ("QCOL", [128, 3], F32),
                ("LNRA", [1, 2 * G], F32),
            ]:
                s[nmm] = SBa(nmm, shape, dt)
            s["IND17"] = s["IND17F"][0:17, :]
            s["LNR"] = s["LNRA"][0:1, 0:2 * G]
            s["DL"] = s["TMPE2"][:, :]
            s["AP4"] = s["TMPE"][:, :].bitcast(BF16)[:, 0:512]
            s["XP"] = s["XPU"][:, :]
            _ubf = s["XPU"][:, :].bitcast(BF16)
            s["QA"] = _ubf[:, 0:KD * G]
            s["YA"] = _ubf[:, KD * G:2 * KD * G]
            pa = self.pa = {}
            pbanks = []
            for k in range(8):
                pbanks.append(es.enter_context(nc.psum_tensor("PBK%d" % k, [128, 512], F32)))
            for k in range(8):
                if k == 2:
                    pa[k] = pbanks[k].ap().bitcast(BF16)
                else:
                    pa[k] = pbanks[k].ap()

            semnames = ["LD", "LG", "LS", "LS2", "LX", "LW", "ST", "TV", "VT", "SC", "GC",
                        "QL0", "QL1", "QS0", "QS1", "QF", "QV", "QW"]
            self.sems = {k: es.enter_context(nc.semaphore(k)) for k in semnames}

            block = es.enter_context(nc.Block())

            ffn = Pool(s["POOL"][:, :], POOLW)
            SBf = lambda n, sh, dt: ffn.tensor(sh, dt)
            for nmm, shape, dt in [
                ("W1Q", [128, KD * FF], BF16),
                ("W2S0", [128, 2 * D], BF16), ("W2S1", [128, 2 * D], BF16),
                ("H", [128, KF * FG], BF16),
                ("C1", [128, KD * FG], BF16),
                ("XT2", [128, 4 * D], BF16), ("OST", [128, 512], F32),
                ("B2BC", [128, D], BF16),
                ("B1R", [128, KF], F32),
                ("QB2", [128, FG], BF16),
                ("MXC", [128, 4], F32),
                ("IV2", [128, 4], F32),
                ("TQ1", [128, FG], BF16),
                ("FTMP", [128, 512], BF16),
                ("ROWS", [1, 5 * G], F32),

            ]:
                s[nmm] = SBf(nmm, shape, dt)
            s["MAXA"] = s["OST"][:, :]
            s["QLD2"] = s["W2S0"][:, :].bitcast(F32)[:, 0:2 * QP]
            s["QSTG2"] = s["W2S1"][:, 0:2 * QP]
            s["QB1"] = s["QB2"][:, :]
            s["B2R"] = s["ROWS"][0:1, 0:D]
            pf = self.pf = {}
            for k in range(8):
                pf[k] = pbanks[k].ap()

            @block.tensor
            def _(eng):
                self.walk("tensor", eng)

            @block.vector
            def _(eng):
                self.walk("vector", eng)

            @block.scalar
            def _(eng):
                self.walk("scalar", eng)

            @block.sync
            def _(eng):
                self.walk("sync", eng)

            @block.gpsimd
            def _(eng):
                self.walk("gpsimd", eng)

        return nc

    def walk(self, ename, eng):
        self.ename = ename
        self.eng = eng
        self._nstage = 0
        self.c = {k: 0 for k in self.sems}
        self.qi = 0
        self.q_pending = []
        self.q_lpar = 0
        self.q_spar = 0
        self.q_stmark = [0, 0]
        self.q_ldmark = [(None, 0), (None, 0)]
        self.q_wsdone = [0, 0]
        self.q_ammark = 0
        t_, v_, s_, y_, g_ = (self.on("tensor"), self.on("vector"), self.on("scalar"),
                              self.on("sync"), self.on("gpsimd"))
        s, d, pa, pf = self.s, self.d, self.pa, self.pf
        T, NG, NCH = self.T, self.NG, self.NCH

        def ld(dst_ap, src_ap, q="sync"):
            ins = None
            if (q == "sync" and y_) or (q == "gpsimd" and g_):
                ins = eng.dma_start(out=dst_ap, in_=src_ap)
            return self.i(ins, "LD" if q == "sync" else "LG", 16)

        ld(s["XF"][:, :].rearrange("p (k t) -> p k t", k=KD),
           d["xT"].rearrange("(k p) t -> p k t", p=128))
        for nmm, srcn in [("IDENTB", "c_identb"), ("IDENTF", "c_identf"),
                          ("TRI1", "c_tri1"), ("TRI2", "c_tri2"), ("MASK", "c_mask"),
                          ("PARITYF", "c_parityf"), ("PARITYB", "c_parityb"),
                          ("PAIRSEL", "c_pairsel"), ("PJ8", "c_pj8"),
                          ("ONES1", "c_ones1"),
                          ("ONESC", "c_onesc"), ("ONESCB", "c_onescb"),
                          ("BGRP", "bgp")]:
            ap = s[nmm]
            ld(ap[:, :] if not isinstance(ap, bass.AP) else ap, d[srcn])
        ld(s["IND17"][0:NH, :], d["c_ind16"])
        ld(s["QCOL"][:, :], d["c_qcol"])
        ld_const = self.c["LD"]

        for nmm, src, wide in [("WKT", "wkT", D), ("WVT", "wvT", D), ("WGT", "wgT", NH),
                               ("WQT", "wqT", D), ("WOT", "woT", D)]:
            ld(s[nmm][:, :].rearrange("p (k f) -> p k f", k=KD),
               d[src].rearrange("(k p) f -> p k f", p=128), q="gpsimd")
        lg_kvg = self.c["LG"]
        lg_allw = self.c["LG"]

        ins = None
        if v_:
            eng.memset(s["SM"][:, :], 0.0)
            eng.memset(s["SHIZ"][:, :], 0.0)
            ins = eng.memset(s["STAGE17"][:, :], 1.0)
        self.i(ins, "VT")

        def ln_group(xsrc, Tx, g0, Gw, dst, rows, scr1, scr2):
            scr = [scr1, scr2]
            for k in range(KD):
                xc = xsrc[:, k * Tx + g0: k * Tx + g0 + Gw]
                tile = scr[k % 2][:, 0:Gw]
                if k >= 2:
                    self.w("scalar", "TV", tv_sq[k % 2])
                ins = None
                if s_:
                    ins = eng.activation(tile, xc, AF.Square)
                self.i(ins, "SC")
                sq_done = self.c["SC"]
                self.w("tensor", "SC", sq_done)
                if k == 0:
                    self.w("tensor", "VT")
                ins = None
                if t_:
                    onesq = s["ONESC"] if tile.dtype == F32 else s["ONESCB"]
                    eng.matmul(self.P(6)[0:1, 0:Gw], s["ONESC"][:, :], xc,
                               start=(k == 0), stop=(k == KD - 1))
                    ins = eng.matmul(self.P(7)[0:1, 0:Gw], onesq[:, :], tile,
                                     start=(k == 0), stop=(k == KD - 1))
                mark = self.i(ins, "TV")
                if k == 0:
                    tv_sq = {0: mark, 1: mark}
                tv_sq[k % 2] = mark
            tv_stats = self.c["TV"]
            self.w("vector", "TV", tv_stats)
            self.w("vector", "LG")
            L = rows
            ins = None
            if v_:
                eng.tensor_scalar(L[0:1, 0:Gw], self.P(6)[0:1, 0:Gw], 1.0 / D, None, OP.mult)
                eng.tensor_scalar(L[0:1, G:G + Gw], self.P(7)[0:1, 0:Gw], 1.0 / D, None, OP.mult)
                eng.drain()
                eng.tensor_tensor(scr1[0:1, 0:Gw], L[0:1, 0:Gw], L[0:1, 0:Gw], OP.mult)
                eng.drain()
                eng.tensor_scalar(scr1[0:1, 0:Gw], scr1[0:1, 0:Gw],
                                  -1.0, EPS, OP.mult, OP.add)
                eng.drain()
                ins = eng.tensor_tensor(L[0:1, G:G + Gw], L[0:1, G:G + Gw],
                                        scr1[0:1, 0:Gw], OP.add)
            self.i(ins, "VT")
            self.w("scalar", "VT")
            ins = None
            if s_:
                eng.activation(L[0:1, G:G + Gw], L[0:1, G:G + Gw], AF.Ln)
                eng.drain()
                ins = eng.activation(L[0:1, G:G + Gw], L[0:1, G:G + Gw],
                                     AF.Exp, scale=-0.5)
            self.i(ins, "SC")
            self.w("vector", "SC")
            ins = None
            if v_:
                eng.tensor_tensor(L[0:1, 0:Gw], L[0:1, 0:Gw],
                                  L[0:1, G:G + Gw], OP.mult)
                eng.drain()
                ins = eng.tensor_scalar(L[0:1, 0:Gw], L[0:1, 0:Gw],
                                        -1.0, None, OP.mult)
            self.i(ins, "VT")
            self.w("tensor", "VT")
            ins = None
            if t_:
                eng.matmul(self.P(0)[:, 0:Gw], s["ONES1"][0:1, :], L[0:1, G:G + Gw],
                           start=True, stop=True)
                ins = eng.matmul(self.P(1)[:, 0:Gw], s["ONES1"][0:1, :], L[0:1, 0:Gw],
                                 start=True, stop=True)
            self.i(ins, "TV")
            tv_ab = self.c["TV"]
            self.w("vector", "TV", tv_ab)
            ins = None
            for k0 in range(0, KD, 2):
                xc0 = xsrc[:, k0 * Tx + g0: k0 * Tx + g0 + Gw]
                xc1 = xsrc[:, (k0 + 1) * Tx + g0: (k0 + 1) * Tx + g0 + Gw]
                if v_:
                    eng.tensor_tensor(scr1[:, 0:Gw], xc0, self.P(0)[:, 0:Gw], OP.mult)
                    eng.tensor_tensor(scr2[:, 0:Gw], xc1, self.P(0)[:, 0:Gw], OP.mult)
                    eng.drain()
                    eng.tensor_tensor(dst[:, k0 * Gw:(k0 + 1) * Gw], scr1[:, 0:Gw],
                                      self.P(1)[:, 0:Gw], OP.add)
                    ins = eng.tensor_tensor(dst[:, (k0 + 1) * Gw:(k0 + 2) * Gw],
                                            scr2[:, 0:Gw], self.P(1)[:, 0:Gw], OP.add)
                    if k0 + 2 < KD:
                        eng.drain()
            return self.i(ins, "VT")

        self.Pmap = pa
        self.walk_attention(t_, v_, s_, y_, g_, ld_const, lg_kvg, lg_allw, ln_group)
        if not (self.debug_xmid or (self.stop_stage is not None and self._nstage >= self.stop_stage)):
            self.Pmap = pf
            self.walk_ffn(t_, v_, s_, y_, g_, ln_group)
        if self.stop_stage is not None and not self.debug_xmid:
            self.w("sync", "VT")
            self.w("sync", "TV")
            self.w("sync", "SC")
            ins = None
            if y_:
                ins = eng.dma_start(out=d["out"], in_=s["XF"][:, :])
            self.i(ins, "ST", 16)

        self.w("sync", "ST", self.c["ST"])

    def P(self, k):
        return self.Pmap[k][:, :]

    def q_src(self, w, p):
        if w == 0:
            r, c0 = divmod(p, FF // QP)
            return self.d["w1T"][r * 128:(r + 1) * 128, c0 * QP:(c0 + 1) * QP]
        r, c0 = divmod(p, D // QP)
        return self.d["w2T"][r * 128:(r + 1) * 128, c0 * QP:(c0 + 1) * QP]

    def q_dst(self, w, p):
        assert w == 1
        r, c0 = divmod(p, D // QP)
        return self.d["w2q_dram"][r * 128:(r + 1) * 128, c0 * QP:(c0 + 1) * QP]

    def svc(self, n=1):
        eng, s = self.eng, self.s
        g_ = self.on("gpsimd")
        s_ = self.on("scalar")
        v_ = self.on("vector")
        for _ in range(n):
            if self.qi >= len(QSTEPS):
                return
            st = QSTEPS[self.qi]
            self.qi += 1
            kind = st[0]
            if kind == "ld":
                _, w, p = st
                pb = self.q_lpar % 2
                self.q_lpar += 1
                gsem, gval = self.q_ldmark[pb]
                if gsem is not None:
                    self.w("scalar", gsem, gval)
                self.w("scalar", "QF", max(self.q_wsdone))
                ins = None
                if s_:
                    ins = eng.dma_start(out=s["QLD"][:, pb * QP:(pb + 1) * QP],
                                        in_=self.q_src(w, p))
                self.i(ins, "QL%d" % pb, 16)
                self.q_pending.append((pb, self.c["QL%d" % pb]))
            elif kind == "am":
                _, w, p = st
                pb, qlc = self.q_pending.pop(0)
                self.w("scalar", "QL%d" % pb, qlc)
                ql = s["QLD"][:, pb * QP:(pb + 1) * QP]
                col = w * QNP + p
                ins = None
                if s_:
                    ins = eng.activation(ql, ql, AF.Abs,
                                         accum_out=s["QACC"][:, col:col + 1])
                self.q_ldmark[pb] = (None, 0)
                self.i(ins, "QV")
                self.q_ammark = self.c["QV"]
            elif kind == "ws":
                _, w = st
                numel = float(D * FF)
                QL_ = s["QLD"]
                base = w * QNP
                self.w("vector", "QV", self.q_ammark)
                ins = None
                if v_:
                    ins = eng.tensor_reduce(s["QACC"][:, base:base + 1],
                                            s["QACC"][:, base:base + QNP],
                                            AX, OP.add)
                mw1 = self.i(ins, "QW")
                self.w("gpsimd", "QW", mw1)
                ins = None
                if g_:
                    ins = eng.dma_start(out=QL_[0:1, 0:128],
                                        in_=s["QACC"][:, base:base + 1])
                self.i(ins, "QF", 16)
                qf_row = self.c["QF"]
                self.w("vector", "QF", qf_row)
                ins = None
                if v_:
                    eng.tensor_reduce(QL_[0:1, 0:1], QL_[0:1, 0:128], AX, OP.add)
                    eng.tensor_scalar(QL_[0:1, 0:1], QL_[0:1, 0:1],
                                      1.0 / numel, 1e-5, OP.mult, OP.max)
                    eng.reciprocal(QL_[0:1, 1:2], QL_[0:1, 0:1])
                    eng.tensor_scalar(QL_[0:1, 128:256], s["ONES1"][0:1, :],
                                      QL_[0:1, 1:2], None, OP.mult)
                    ins = eng.tensor_scalar(QL_[0:1, 256:384], s["ONES1"][0:1, :],
                                            QL_[0:1, 0:1], None, OP.mult)
                m2 = self.i(ins, "QW")
                self.w("gpsimd", "QW", m2)
                ins = None
                if g_:
                    eng.dma_start(out=s["WSC"][:, w:w + 1],
                                  in_=QL_[0:1, 128:256]).then_inc(
                        self.sems["QF"], 16)
                    ins = eng.dma_start(out=s["MABP"][:, w:w + 1],
                                        in_=QL_[0:1, 256:384])
                self.c["QF"] += 16
                self.i(ins, "QF", 16)
                self.q_wsdone[w] = self.c["QF"]


    def walk_attention(self, t_, v_, s_, y_, g_, ld_const, lg_kvg, lg_allw, ln_group):
        eng, s, d = self.eng, self.s, self.d
        T, NG, NCH = self.T, self.NG, self.NCH

        def projections(with_q):
            plist = [("k", "WKT", "KA")] + ([("q", "WQT", "QA")] if with_q else [])
            for nm, W, dstA in plist:
                sc = SCALE if nm == "q" else 1.0
                for j in range(KD):
                    bank = self.P(0) if (j % 2 == 0) else self.P(1)
                    self.w("tensor", "VT")
                    ins = None
                    if t_:
                        for k in range(KD):
                            ins = eng.matmul(bank[:, 0:G],
                                             s[W][:, k * D + j * 128: k * D + j * 128 + 128],
                                             s["U1"][:, k * G:(k + 1) * G],
                                             start=(k == 0), stop=(k == KD - 1))
                    self.i(ins, "TV")
                    tvp = self.c["TV"]
                    self.w("vector", "TV", tvp)
                    ins = None
                    if v_:
                        ins = eng.tensor_scalar(s["TMPE"][:, :], bank[:, 0:G], sc, 0.0,
                                                OP.mult, OP.min)
                    self.i(ins, "VT")
                    self.w("scalar", "VT")
                    ebf = s["TMPE2"][:, :].bitcast(BF16)
                    ins = None
                    if s_:
                        eng.activation(ebf[:, 0:G], s["TMPE"][:, :], AF.Exp)
                        ins = eng.activation(ebf[:, G:2 * G], bank[:, 0:G], AF.Relu, scale=sc)
                    self.i(ins, "SC")
                    self.w("vector", "SC")
                    ins = None
                    if v_:
                        ins = eng.tensor_tensor(s[dstA][:, j * G:(j + 1) * G],
                                                ebf[:, 0:G], ebf[:, G:2 * G], OP.add)
                    self.i(ins, "VT")
            for tt in range(NCH):
                for half in range(2):
                    bank = self.P(0) if (half == 0) else self.P(1)
                    self.w("tensor", "VT")
                    ins = None
                    if t_:
                        for k in range(KD):
                            ins = eng.matmul(
                                bank[:, 0:512],
                                s["U1"][:, k * G + tt * C: k * G + tt * C + C],
                                s["WVT"][:, k * D + half * 512: k * D + half * 512 + 512],
                                start=(k == 0), stop=(k == KD - 1))
                    self.i(ins, "TV")
                    tvv = self.c["TV"]
                    self.w("vector", "TV", tvv)
                    ins = None
                    if v_:
                        ins = eng.tensor_copy(
                            out=s["VA"][:, tt * D + half * 512: tt * D + half * 512 + 512],
                            in_=bank[:, 0:512])
                    self.i(ins, "VT")
            self.w("tensor", "VT")
            ins = None
            if t_:
                for tt in range(NCH):
                    for k in range(KD):
                        eng.matmul(self.P(4)[:, tt * NH:(tt + 1) * NH],
                                   s["U1"][:, k * G + tt * C: k * G + tt * C + C],
                                   s["WGT"][:, k * NH:(k + 1) * NH],
                                   start=(k == 0), stop=False)
                    ins = eng.matmul(self.P(4)[:, tt * NH:(tt + 1) * NH],
                                     s["TRI1"][:, :], s["BGRP"][:, :],
                                     start=False, stop=True)
            self.i(ins, "TV")
            tv_g = self.c["TV"]
            self.w("vector", "TV", tv_g)
            ins = None
            if v_:
                ins = eng.tensor_copy(out=s["ZALL"][:, :], in_=self.P(4)[:, 0:4 * NH])
            self.i(ins, "VT")
            return self.c["VT"]

        def gate_pipeline(tt):
            self.w("scalar", "VT")
            ins = None
            if s_:
                eng.activation(s["SPt"][:, :], s["ZALL"][:, tt * NH:(tt + 1) * NH],
                               AF.Exp, scale=-1.0)
                eng.drain()
                ins = eng.activation(s["SPt"][:, :], s["SPt"][:, :], AF.Ln, bias=1.0)
            self.i(ins, "SC")
            sc_sp = self.c["SC"]
            if self.cut("g_sp"):
                return None
            self.w("tensor", "SC", sc_sp)
            self.w("tensor", "VT")
            ins = None
            if t_:
                eng.matmul(self.P(4)[:, 64:64 + NH], s["TRI1"][:, :], s["SPt"][:, :],
                           start=True, stop=True)
                ins = eng.matmul(self.P(4)[:, 64 + NH:64 + 2 * NH], s["TRI2"][:, :],
                                 s["SPt"][:, :], start=True, stop=True)
            self.i(ins, "TV")
            tv_cum = self.c["TV"]
            self.w("vector", "TV", tv_cum)
            ins = None
            if v_:
                ins = eng.tensor_copy(out=s["CP"][:, :], in_=self.P(4)[:, 64:64 + NH])
            self.i(ins, "VT")
            vt_cp = self.c["VT"]
            self.w("scalar", "TV", tv_cum)
            ins = None
            if s_:
                ins = eng.activation(s["EK"][:, :], self.P(4)[:, 64 + NH:64 + 2 * NH],
                                     AF.Exp, scale=-1.0)
            self.i(ins, "SC")
            sc_ek = self.c["SC"]
            if self.cut("g_ek"):
                return None
            self.w("tensor", "VT", vt_cp)
            self.w("tensor", "SC", sc_ek)
            ins = None
            if t_:
                ins = eng.transpose(self.P(4)[0:NH, 128:256], s["CP"][:, :], s["IDENTF"][:, :])
            self.i(ins, "TV")
            tv_cpt = self.c["TV"]
            self.w("vector", "TV", tv_cpt)
            self.w("vector", "LG")
            ins = None
            if v_:
                ins = eng.tensor_copy(out=s["STAGE17"][0:NH, :], in_=self.P(4)[0:NH, 128:256])
            self.i(ins, "VT")
            vt_cpt = self.c["VT"]
            self.w("scalar", "VT", vt_cpt)
            ins = None
            if s_:
                ins = eng.activation(s["ETT"][:, :], s["STAGE17"][0:NH, :], AF.Exp, scale=-1.0)
            self.i(ins, "SC")
            sc_ett = self.c["SC"]
            self.w("vector", "VT", vt_cpt)
            ins = None
            if v_:
                ins = eng.tensor_scalar(s["AR16"][:, :], s["PARITYF"][:, :],
                                        s["STAGE17"][0:NH, 127:128], None, OP.mult)
            self.i(ins, "VT")
            vt_ar = self.c["VT"]
            if self.cut("g_ett"):
                return None
            self.w("gpsimd", "VT", vt_cpt)
            self.w("gpsimd", "TV")
            self.w("gpsimd", "LG")
            ins = None
            if g_:
                ins = eng.dma_start(
                    out=s["IND17"][NH:NH + 1, :].rearrange("o (h t) -> o h t", h=NH),
                    in_=s["STAGE17"][0:NH, :])
            self.i(ins, "LG", 16)
            lg_flat = self.c["LG"]
            if self.cut("g_flat"):
                return None
            self.w("tensor", "VT", vt_ar)
            ins = None
            if t_:
                ins = eng.matmul(self.P(4)[:, 256:264], s["AR16"][:, :],
                                 s["PAIRSEL"][:, :], start=True, stop=True)
            self.i(ins, "TV")
            tv_drp = self.c["TV"]
            self.w("scalar", "TV", tv_drp)
            ins = None
            if s_:
                ins = eng.activation(s["DRP"][:, :], self.P(4)[:, 256:264], AF.Exp, scale=-1.0)
            self.i(ins, "SC")
            sc_drp = self.c["SC"]
            if self.cut("g_drp"):
                return None
            self.w("vector", "SC", sc_ett)
            ins = None
            if v_:
                ev = s["ETT"][:, :].unsqueeze(1).broadcast_to([NH, 4, 128])
                eng.tensor_tensor(s["EB2"][:, 0:512].rearrange("p (j t) -> p j t", j=4),
                                  s["PJ8"][:, 0:512].rearrange("p (j t) -> p j t", j=4),
                                  ev, OP.mult)
                ins = eng.tensor_tensor(s["EB2"][:, 512:1024].rearrange("p (j t) -> p j t", j=4),
                                        s["PJ8"][:, 512:1024].rearrange("p (j t) -> p j t", j=4),
                                        ev, OP.mult)
            self.i(ins, "VT")
            vt_eb2 = self.c["VT"]
            self.w("tensor", "VT", vt_eb2)
            self.w("tensor", "SC")
            ins = None
            if t_:
                eng.matmul(self.P(5)[:, 0:512], s["PARITYB"][:, :], s["EB2"][:, 0:512],
                           start=True, stop=True)
                ins = eng.matmul(self.P(6)[:, 0:512], s["PARITYB"][:, :], s["EB2"][:, 512:1024],
                                 start=True, stop=True)
            self.i(ins, "TV")
            tv_ebc = self.c["TV"]
            if self.cut("g_ebcmm"):
                return None
            self.w("vector", "TV", tv_ebc)
            ins = None
            if v_:
                eng.tensor_copy(out=s["EBC"][:, 0:512], in_=self.P(5)[:, 0:512])
                ins = eng.tensor_copy(out=s["EBC"][:, 512:1024], in_=self.P(6)[:, 0:512])
            self.i(ins, "VT")
            return {"sc_ek": sc_ek, "sc_drp": sc_drp, "vt_rest": self.c["VT"],
                    "lg_flat": lg_flat}

        def chunk_heads(gi, tt, gm, with_qy):
            self.w("tensor", "VT")
            ins = None
            if t_:
                for j in range(KD):
                    ins = eng.transpose(self.pa[2][:, j * 128:(j + 1) * 128],
                                        s["KA"][:, j * G + tt * C: j * G + tt * C + C],
                                        s["IDENTB"][:, :])
            self.i(ins, "TV")
            tv_ktr = self.c["TV"]
            self.w("vector", "TV", tv_ktr)
            self.w("vector", "SC", gm["sc_ek"])
            ins = None
            if v_:
                ekb = s["EK"][:, :].unsqueeze(2).broadcast_to([128, NH, DH])
                ins = eng.tensor_tensor(
                    s["KH"][:, :].rearrange("p (h e) -> p h e", h=NH),
                    self.pa[2][:, 0:1024].rearrange("p (h e) -> p h e", h=NH),
                    ekb, OP.mult)
            self.i(ins, "VT")
            vt_kh = self.c["VT"]
            if self.cut("c_kh"):
                return
            if with_qy:
                ins = None
                if v_:
                    qsl = s["QA"].rearrange("p (j t) -> p j t", j=KD)[:, :, tt * C:(tt + 1) * C]
                    esl = s["EBC"][:, :].rearrange("p (j t) -> p j t", j=KD)
                    ins = eng.tensor_tensor(
                        s["QT"][:, :].rearrange("p (j t) -> p j t", j=KD), qsl, esl, OP.mult)
                self.i(ins, "VT")
            vt_qt = self.c["VT"]

            for pj in range(8):
                h0, h1 = 2 * pj, 2 * pj + 1
                j = pj
                if with_qy:
                    self.w("tensor", "VT")
                    ins = None
                    if t_:
                        eng.matmul(self.P(0)[:, 0:128],
                                   s["KA"][0:64, j * G + tt * C: j * G + tt * C + C],
                                   s["QA"][0:64, j * G + tt * C: j * G + tt * C + C],
                                   start=True, stop=True)
                        ins = eng.matmul(self.P(1)[:, 0:128],
                                         s["KA"][64:128, j * G + tt * C: j * G + tt * C + C],
                                         s["QA"][64:128, j * G + tt * C: j * G + tt * C + C],
                                         start=True, stop=True)
                    self.i(ins, "TV")
                    tv_sc = self.c["TV"]
                    if self.cut("b_sc"):
                        return
                    self.w("tensor", "LG", gm["lg_flat"])
                    ins = None
                    if t_:
                        bank = self.P(3) if (pj % 2 == 0) else self.P(4)
                        ins = eng.matmul(bank[:, 0:256], s["STAGE17"][:, :],
                                         s["IND17"][:, pj * 256:(pj + 1) * 256],
                                         start=True, stop=True)
                    self.i(ins, "TV")
                    tv_rt = self.c["TV"]
                    self.w("vector", "TV", tv_rt)
                    ins = None
                    if v_:
                        eng.drain()
                        bank = self.P(3) if (pj % 2 == 0) else self.P(4)
                        mb = s["MASK"][:, :].unsqueeze(1).broadcast_to([128, 2, 128])
                        ins = eng.tensor_tensor(
                            s["DL"][:, 0:256].rearrange("p (a t) -> p a t", a=2),
                            bank[:, 0:256].rearrange("p (a t) -> p a t", a=2),
                            mb, OP.add)
                    self.i(ins, "VT")
                    vt_dl = self.c["VT"]
                    self.w("scalar", "VT", vt_dl)
                    ins = None
                    if s_:
                        ins = eng.activation(s["DL"][:, 0:256], s["DL"][:, 0:256],
                                             AF.Exp, scale=-1.0)
                    self.i(ins, "SC")
                    sc_de = self.c["SC"]
                    self.w("vector", "SC", sc_de)
                    self.w("vector", "TV", tv_sc)
                    ins = None
                    if v_:
                        eng.tensor_tensor(s["AP4"][:, 0:128], s["DL"][:, 0:128],
                                          self.P(0)[:, 0:128], OP.mult)
                        ins = eng.tensor_tensor(s["AP4"][:, 128:256], s["DL"][:, 128:256],
                                                self.P(1)[:, 0:128], OP.mult)
                    self.i(ins, "VT")
                    vt_a = self.c["VT"]
                    if self.cut("b_a"):
                        return
                    ybank = self.P(5) if (pj % 2 == 0) else self.P(6)
                    self.w("tensor", "VT", vt_a)
                    self.w("tensor", "SC")
                    ins = None
                    if t_:
                        for z in range(2):
                            h = 2 * pj + z
                            base = z * 64
                            eng.matmul(ybank[base:base + 64, 0:128],
                                       s["VA"][:, tt * D + h * DH: tt * D + (h + 1) * DH],
                                       s["AP4"][:, z * 128:(z + 1) * 128],
                                       start=True, stop=False,
                                       tile_position=(0, base))
                            ins = eng.matmul(ybank[base:base + 64, 0:128],
                                             s["SHIZ"][:, h * DH:(h + 1) * DH],
                                             s["QT"][:, pj * 128:(pj + 1) * 128],
                                             start=False, stop=True,
                                             tile_position=(0, base))
                    self.i(ins, "TV")
                    tv_y = self.c["TV"]
                    if self.cut("b_y"):
                        return
                    self.w("scalar", "TV", tv_y)
                    ins = None
                    if s_:
                        ins = eng.activation(
                            s["YA"][:, pj * G + tt * C: pj * G + tt * C + C],
                            ybank[:, 0:128], AF.Identity)
                    self.i(ins, "SC")
                self.w("tensor", "VT", vt_kh)
                ins = None
                if t_:
                    for z in range(2):
                        h = 2 * pj + z
                        base = z * 64
                        ins = eng.matmul(self.P(7)[base:base + 64, pj * DH:(pj + 1) * DH],
                                         s["KH"][:, h * DH:(h + 1) * DH],
                                         s["VA"][:, tt * D + h * DH: tt * D + (h + 1) * DH],
                                         start=True, stop=True,
                                         tile_position=(0, base))
                self.i(ins, "TV")
                if self.cut("b_ds"):
                    return
            tv_ds_all = self.c["TV"]
            self.w("vector", "TV", tv_ds_all)
            self.w("vector", "SC", gm["sc_drp"])
            ins = None
            if v_:
                for pj in range(8):
                    eng.scalar_tensor_tensor(
                        s["SM"][:, pj * DH:(pj + 1) * DH], s["SM"][:, pj * DH:(pj + 1) * DH],
                        s["DRP"][:, pj:pj + 1], self.P(7)[:, pj * DH:(pj + 1) * DH],
                        OP.mult, OP.add)
                eng.drain()
                smv = s["SM"][:, :].rearrange("p (j c) -> p j c", j=8)
                shz = s["SHIZ"][:, :].rearrange("p (h2 two c) -> p h2 two c", two=2, c=DH)
                eng.tensor_copy(out=shz[0:64, :, 0, :], in_=smv[0:64, :, :])
                ins = eng.tensor_copy(out=shz[64:128, :, 1, :], in_=smv[64:128, :, :])
            self.i(ins, "VT")

        phases = ([("prefix", gi) for gi in range(NG)] if self.do_prefix else []) \
            + [("main", gi) for gi in range(NG)]
        for phase, gi in phases:
            if self.cut("group"):
                return
            is_main = (phase == "main")
            if not is_main:
                self.w("sync", "VT")
                self.w("sync", "TV")
                self.w("sync", "SC")
                self.w("sync", "LX")
                ins = None
                if y_:
                    ins = eng.dma_start(
                        out=s["XP"].rearrange("p (k t) -> p k t", k=KD),
                        in_=d["xpT"].rearrange("(k p) t -> p k t", p=128)[:, :, gi * G:(gi + 1) * G])
                self.i(ins, "LX", 16)
                ld_x = self.c["LX"]
                for e in ("tensor", "vector", "scalar"):
                    self.w(e, "LX", ld_x)
                    self.w(e, "LD", ld_const)
                self.w("tensor", "LG", lg_kvg)
                xsrc, g0, Tx = s["XP"], 0, G
            else:
                for e in ("tensor", "vector", "scalar"):
                    self.w(e, "LD", ld_const)
                self.w("tensor", "LG", lg_allw)
                xsrc, g0, Tx = s["XF"], gi * G, T
            ln_group(xsrc, Tx, g0, G, s["U1"], s["LNR"], s["TMPE"], s["TMPE2"])
            if self.cut("ln"):
                return
            projections(with_q=is_main)
            if self.cut("proj"):
                return
            self.svc(25)
            for tt in range(NCH):
                gm = gate_pipeline(tt)
                if gm is None:
                    return
                if self.cut("gate"):
                    return
                self.svc(8)
                chunk_heads(gi, tt, gm, with_qy=is_main)
                if self.cut("chunk"):
                    return
                self.svc(18)
            if is_main:
                for j in range(KD):
                    bank = self.P(0) if (j % 2 == 0) else self.P(1)
                    self.w("tensor", "VT")
                    self.w("tensor", "SC")
                    ins = None
                    if t_:
                        for k in range(KD):
                            ins = eng.matmul(
                                bank[:, 0:G],
                                s["WOT"][:, k * D + j * 128: k * D + j * 128 + 128],
                                s["YA"][:, k * G:(k + 1) * G],
                                start=(k == 0), stop=(k == KD - 1))
                    self.i(ins, "TV")
                    tvw = self.c["TV"]
                    self.w("vector", "TV", tvw)
                    ins = None
                    if v_:
                        xc = s["XF"][:, j * T + gi * G: j * T + gi * G + G]
                        ins = eng.tensor_tensor(xc, bank[:, 0:G], xc, OP.add)
                    self.i(ins, "VT")

        if self.debug_xmid:
            vt_all = self.w("sync", "VT")
            ins = None
            if y_:
                ins = eng.dma_start(out=d["out"], in_=s["XF"][:, :])
            self.i(ins, "ST", 16)

    def walk_ffn(self, t_, v_, s_, y_, g_, ln_group):
        eng, s, d = self.eng, self.s, self.d
        T = self.T
        if self.debug_xmid:
            return

        while self.qi < len(QSTEPS):
            self.svc(1)
        qf_final = self.c["QF"]

        for e in ENGINES:
            for sm in ("VT", "SC", "TV", "LD", "LG", "LX"):
                self.w(e, sm)
        self.w("vector", "QF", qf_final)
        self.w("scalar", "QF", qf_final)
        ins = None
        if y_:
            eng.dma_start(out=s["B1R"][:, :], in_=d["b1r"]).then_inc(self.sems["LW"], 16)
            ins = eng.dma_start(out=s["B2R"], in_=d["b2r"])
        self.c["LW"] += 16
        self.i(ins, "LW", 16)
        ld_fconst = self.c["LW"]
        for e in ("tensor", "vector", "scalar"):
            self.w(e, "LW", ld_fconst)

        self.q2_lpar = 0
        self.q2_spar = 0
        self.q2_pending = []
        self.q2_ldmark = [(None, 0), (None, 0)]
        self.q2_stmark = [0, 0]
        self.q2_w2marks = {}

        def q2_load(w, p):
            pb = self.q2_lpar % 2
            self.q2_lpar += 1
            gsem, gval = self.q2_ldmark[pb]
            if gsem is not None:
                self.w("scalar", gsem, gval)
            ins = None
            if s_:
                ins = eng.dma_start(out=s["QLD2"][:, pb * QP:(pb + 1) * QP],
                                    in_=self.q_src(w, p))
            self.i(ins, "QL%d" % pb, 16)
            self.q2_pending.append((pb, self.c["QL%d" % pb]))

        def q2_piece(w, p):
            pb, qlc = self.q2_pending.pop(0)
            self.w("scalar", "QL%d" % pb, qlc)
            ql = s["QLD2"][:, pb * QP:(pb + 1) * QP]
            ins = None
            if s_:
                ins = eng.activation(ql, ql, AF.Identity,
                                     scale=s["WSC"][:, w:w + 1])
            mva = self.i(ins, "QV")
            self.w("vector", "QV", mva)
            if w == 0:
                r, c0 = divmod(p, FF // QP)
                dst = s["W1Q"][:, r * FF + c0 * QP: r * FF + (c0 + 1) * QP]
            else:
                sb = self.q2_spar % 2
                self.q2_spar += 1
                self.w("vector", "QS%d" % sb, self.q2_stmark[sb])
                dst = s["QSTG2"][:, sb * QP:(sb + 1) * QP]
            ins = None
            if v_:
                eng.tensor_scalar(ql, ql, MAGIC, MAGIC, OP.add, OP.subtract)
                ins = eng.tensor_scalar(dst, ql, 1.0, -1.0, OP.min, OP.max)
            mvw = self.i(ins, "QW")
            self.q2_ldmark[pb] = ("QW", mvw)
            if w == 1:
                self.w("scalar", "QW", mvw)
                ins = None
                if s_:
                    ins = eng.dma_start(out=self.q_dst(1, p), in_=dst)
                self.i(ins, "QS%d" % sb, 16)
                self.q2_stmark[sb] = self.c["QS%d" % sb]
                self.q2_w2marks[p] = (self.c["QS0"], self.c["QS1"])
            return mvw

        w1pieces = [r * (FF // QP) + c0
                    for c0 in range(FF // QP) for r in range(KD)]
        q2_load(0, w1pieces[0])
        q2_load(0, w1pieces[1])
        w1q_mark = 0
        for i, p in enumerate(w1pieces):
            if i + 2 < len(w1pieces):
                q2_load(0, w1pieces[i + 2])
            w1q_mark = q2_piece(0, p)
        self.q2_w2queue = list(range(QNP))
        q2_load(1, 0)
        q2_load(1, 1)
        self.q2_w2emitted = 0

        def q2_w2_emit(nmax):
            for _ in range(nmax):
                if self.q2_w2emitted >= QNP:
                    return
                p = self.q2_w2emitted
                self.q2_w2emitted += 1
                if p + 2 < QNP:
                    q2_load(1, p + 2)
                q2_piece(1, p)
        self.q2_w2_emit = q2_w2_emit

        self.w("tensor", "VT")
        ins = None
        if t_:
            eng.matmul(self.P(2)[:, 0:512], s["ONES1"][0:1, :], s["B2R"][0:1, 0:512],
                       start=True, stop=True)
            ins = eng.matmul(self.P(3)[:, 0:512], s["ONES1"][0:1, :], s["B2R"][0:1, 512:1024],
                             start=True, stop=True)
        self.i(ins, "TV")
        tvb2 = self.c["TV"]
        self.w("vector", "TV", tvb2)
        ins = None
        if v_:
            eng.tensor_copy(out=s["B2BC"][:, 0:512], in_=self.P(2)[:, 0:512])
            ins = eng.tensor_copy(out=s["B2BC"][:, 512:1024], in_=self.P(3)[:, 0:512])
        self.i(ins, "VT")

        NFG = self.NFG
        prev_f2_marks = [None, None]
        RO = s["ROWS"]
        for fg in range(NFG):
            if self.cut("ffngroup"):
                return
            g0 = fg * FG
            ln_group(s["XF"], T, g0, FG, s["C1"], s["ROWS"][0:1, 0:2 * G],
                     s["FTMP"], s["TQ1"])
            ins = None
            if v_:
                eng.tensor_tensor(RO[0:1, 3 * G:3 * G + FG], RO[0:1, G:2 * G],
                                  RO[0:1, G:2 * G], OP.mult)
                eng.drain()
                ins = eng.tensor_scalar(RO[0:1, 3 * G:3 * G + FG], RO[0:1, 3 * G:3 * G + FG],
                                        -EPS, 1.0 + EPS, OP.mult, OP.add)
            self.i(ins, "VT")
            self.w("scalar", "VT")
            ins = None
            if s_:
                eng.activation(RO[0:1, 2 * G:2 * G + FG], RO[0:1, 3 * G:3 * G + FG], AF.Ln)
                eng.drain()
                ins = eng.activation(RO[0:1, 2 * G:2 * G + FG], RO[0:1, 2 * G:2 * G + FG],
                                     AF.Exp, scale=-0.5)
            self.i(ins, "SC")
            sc_rho = self.c["SC"]
            self.w("vector", "ST")
            ins = None
            if v_:
                eng.memset(s["MAXA"][:, 0:FG], 0.0)
                for k in range(KD):
                    c = s["C1"][:, k * FG:(k + 1) * FG]
                    eng.scalar_tensor_tensor(s["FTMP"][:, 0:FG], c, -1.0, c,
                                             OP.mult, OP.max)
                    eng.drain()
                    ins = eng.tensor_tensor(s["MAXA"][:, 0:FG], s["MAXA"][:, 0:FG],
                                            s["FTMP"][:, 0:FG], OP.max)
                    eng.drain()
            self.i(ins, "VT")
            vt_mx = self.c["VT"]
            self.cross_max(t_, v_, s_, y_, g_, s["MAXA"][:, 0:FG], FG, vt_mx, s["ROWS"], 3 * G)
            self.w("vector", "SC", sc_rho)
            ins = None
            if v_:
                eng.tensor_tensor(RO[0:1, 3 * G:3 * G + FG], RO[0:1, 2 * G:2 * G + FG],
                                  RO[0:1, 3 * G:3 * G + FG], OP.mult)
                eng.drain()
                eng.tensor_scalar(RO[0:1, 3 * G:3 * G + FG], RO[0:1, 3 * G:3 * G + FG],
                                  1e-5, None, OP.max)
                eng.drain()
                ins = eng.tensor_scalar(RO[0:1, 4 * G:4 * G + FG], RO[0:1, 3 * G:3 * G + FG],
                                        s["MABP"][0:1, 0:1], 1.0 / 127.0, OP.mult, OP.mult)
            self.i(ins, "VT")
            vt_inv = self.c["VT"]
            self.w("tensor", "VT", vt_inv)
            ins = None
            if t_:
                ins = eng.matmul(self.P(6)[:, 0:FG], s["ONES1"][0:1, :],
                                 RO[0:1, 4 * G:4 * G + FG], start=True, stop=True)
            self.i(ins, "TV")
            tv_inv = self.c["TV"]
            ins = None
            if v_:
                self.w("vector", "TV", tv_inv)
                eng.reciprocal(RO[0:1, 4 * G:4 * G + FG], RO[0:1, 3 * G:3 * G + FG])
                eng.drain()
                eng.tensor_scalar(RO[0:1, 4 * G:4 * G + FG], RO[0:1, 4 * G:4 * G + FG],
                                  127.0, None, OP.mult)
                eng.drain()
                eng.tensor_tensor(RO[0:1, 4 * G:4 * G + FG], RO[0:1, 4 * G:4 * G + FG],
                                  RO[0:1, 2 * G:2 * G + FG], OP.mult)
                eng.drain()
                eng.tensor_tensor(RO[0:1, 2 * G:2 * G + FG], RO[0:1, G:2 * G],
                                  RO[0:1, 4 * G:4 * G + FG], OP.mult)
                ins = eng.tensor_tensor(RO[0:1, 3 * G:3 * G + FG], RO[0:1, 0:G],
                                        RO[0:1, 4 * G:4 * G + FG], OP.mult)
            self.i(ins, "VT")
            vt_rows = self.c["VT"]
            self.w("tensor", "VT", vt_rows)
            ins = None
            if t_:
                eng.matmul(self.P(2)[:, 0:FG], s["ONES1"][0:1, :],
                           RO[0:1, 2 * G:2 * G + FG], start=True, stop=True)
                ins = eng.matmul(self.P(3)[:, 0:FG], s["ONES1"][0:1, :],
                                 RO[0:1, 3 * G:3 * G + FG], start=True, stop=True)
            self.i(ins, "TV")
            tv_qb = self.c["TV"]
            self.w("vector", "TV", tv_qb)
            ins = None
            if v_:
                ins = eng.tensor_copy(out=s["QB2"][:, 0:FG], in_=self.P(6)[:, 0:FG])
            self.i(ins, "VT")
            ins = None
            for k0 in range(0, KD, 2):
                c0 = s["C1"][:, k0 * FG:(k0 + 1) * FG]
                c1_ = s["C1"][:, (k0 + 1) * FG:(k0 + 2) * FG]
                x0 = s["XF"][:, k0 * T + g0: k0 * T + g0 + FG]
                x1 = s["XF"][:, (k0 + 1) * T + g0: (k0 + 1) * T + g0 + FG]
                if v_:
                    eng.tensor_tensor(c0, x0, self.P(2)[:, 0:FG], OP.mult)
                    eng.tensor_tensor(c1_, x1, self.P(2)[:, 0:FG], OP.mult)
                    eng.drain()
                    eng.tensor_tensor(c0, c0, self.P(3)[:, 0:FG], OP.add)
                    eng.tensor_tensor(c1_, c1_, self.P(3)[:, 0:FG], OP.add)
                    eng.drain()
                    eng.tensor_scalar(c0, c0, MAGIC, MAGIC, OP.add, OP.subtract)
                    eng.tensor_scalar(c1_, c1_, MAGIC, MAGIC, OP.add, OP.subtract)
                    eng.drain()
                    eng.tensor_scalar(c0, c0, 127.0, -128.0, OP.min, OP.max)
                    ins = eng.tensor_scalar(c1_, c1_, 127.0, -128.0, OP.min, OP.max)
            self.i(ins, "VT")
            vt_c1 = self.c["VT"]
            for tt in range(4):
                for k in range(KD):
                    self.w("tensor", "VT")
                    ins = None
                    if t_:
                        ins = eng.transpose(self.P(4)[:, 0:128] if (k % 2 == 0) else self.P(5)[:, 0:128],
                                            s["XF"][:, k * T + g0 + tt * 128: k * T + g0 + tt * 128 + 128],
                                            s["IDENTF"][:, :])
                    self.i(ins, "TV")
                    tvt = self.c["TV"]
                    self.w("vector", "TV", tvt)
                    ins = None
                    if v_:
                        bank = self.P(4) if (k % 2 == 0) else self.P(5)
                        ins = eng.tensor_tensor(s["XT2"][:, tt * D + k * 128:(k + 1) * 128 + tt * D],
                                                bank[:, 0:128], s["B2BC"][:, k * 128:(k + 1) * 128],
                                                OP.add)
                    self.i(ins, "VT")
            if fg == 0:
                self.w("tensor", "QW", w1q_mark)
            self.w("vector", "ST")
            ins = None
            if v_:
                ins = eng.memset(s["MAXA"][:, 0:FG], 0.0)
            self.i(ins, "VT")
            for Fj in range(KF):
                bank = self.P(0) if (Fj % 2 == 0) else self.P(1)
                hsl = s["H"][:, Fj * FG:(Fj + 1) * FG]
                self.w("tensor", "VT")
                ins = None
                if t_:
                    for k in range(KD):
                        ins = eng.matmul(bank[:, 0:FG],
                                         s["W1Q"][:, k * FF + Fj * 128: k * FF + Fj * 128 + 128],
                                         s["C1"][:, k * FG:(k + 1) * FG],
                                         start=(k == 0), stop=(k == KD - 1))
                self.i(ins, "TV")
                tvf = self.c["TV"]
                self.w("vector", "TV", tvf)
                ins = None
                if v_:
                    ins = eng.tensor_tensor(s["TQ1"][:, 0:FG], bank[:, 0:FG],
                                            s["QB2"][:, 0:FG], OP.mult)
                self.i(ins, "VT")
                vt_t1 = self.c["VT"]
                self.w("scalar", "VT", vt_t1)
                ins = None
                if s_:
                    ins = eng.activation(hsl, s["TQ1"][:, 0:FG],
                                         AF.Gelu, bias=s["B1R"][:, Fj:Fj + 1])
                self.i(ins, "SC")
                sch = self.c["SC"]
                self.w("vector", "SC", sch)
                if Fj > 0:
                    self.w("vector", "TV", ssq_mark)
                if Fj > 1:
                    pass
                ins = None
                if v_:
                    eng.tensor_tensor(s["TQ1"][:, 0:FG], hsl, hsl, OP.mult)
                    eng.scalar_tensor_tensor(s["FTMP"][:, 0:FG], hsl, -1.0, hsl,
                                             OP.mult, OP.max)
                    eng.drain()
                    ins = eng.tensor_tensor(s["MAXA"][:, 0:FG], s["MAXA"][:, 0:FG],
                                            s["FTMP"][:, 0:FG], OP.max)
                self.i(ins, "VT")
                vt_sq = self.c["VT"]
                self.w("tensor", "VT", vt_sq)
                ins = None
                if t_:
                    ins = eng.matmul(self.P(6)[0:1, 0:FG], s["ONESCB"][:, :], s["TQ1"][:, 0:FG],
                                     start=(Fj == 0), stop=(Fj == KF - 1))
                self.i(ins, "TV")
                ssq_mark = self.c["TV"]
                if fg == 0:
                    self.q2_w2_emit(4)
            tv_ssq = self.c["TV"]
            vt_mx2 = self.c["VT"]
            self.w("vector", "TV", tv_ssq)
            ins = None
            if v_:
                ins = eng.tensor_scalar(RO[0:1, 3 * G:3 * G + FG], self.P(6)[0:1, 0:FG],
                                        1.0 / FF, EPS, OP.mult, OP.add)
            self.i(ins, "VT")
            self.w("scalar", "VT")
            ins = None
            if s_:
                eng.activation(RO[0:1, 2 * G:2 * G + FG], RO[0:1, 3 * G:3 * G + FG], AF.Ln)
                eng.drain()
                ins = eng.activation(RO[0:1, 2 * G:2 * G + FG], RO[0:1, 2 * G:2 * G + FG],
                                     AF.Exp, scale=-0.5)
            self.i(ins, "SC")
            sc_rho2 = self.c["SC"]
            self.cross_max(t_, v_, s_, y_, g_, s["MAXA"][:, 0:FG], FG, vt_mx2, s["ROWS"], 3 * G)
            self.w("vector", "SC", sc_rho2)
            ins = None
            if v_:
                eng.tensor_tensor(RO[0:1, 3 * G:3 * G + FG], RO[0:1, 2 * G:2 * G + FG],
                                  RO[0:1, 3 * G:3 * G + FG], OP.mult)
                eng.drain()
                eng.tensor_scalar(RO[0:1, 3 * G:3 * G + FG], RO[0:1, 3 * G:3 * G + FG],
                                  1e-5, None, OP.max)
                eng.drain()
                eng.reciprocal(RO[0:1, 4 * G:4 * G + FG], RO[0:1, 3 * G:3 * G + FG])
                eng.drain()
                eng.tensor_scalar(RO[0:1, 4 * G:4 * G + FG], RO[0:1, 4 * G:4 * G + FG],
                                  127.0, None, OP.mult)
                eng.drain()
                ins = eng.tensor_tensor(RO[0:1, 4 * G:4 * G + FG], RO[0:1, 4 * G:4 * G + FG],
                                        RO[0:1, 2 * G:2 * G + FG], OP.mult)
            self.i(ins, "VT")
            vt_q2 = self.c["VT"]
            self.w("tensor", "VT", vt_q2)
            ins = None
            if t_:
                ins = eng.matmul(self.P(2)[:, 0:FG], s["ONES1"][0:1, :],
                                 RO[0:1, 4 * G:4 * G + FG], start=True, stop=True)
            self.i(ins, "TV")
            tv_q2b = self.c["TV"]
            self.w("vector", "TV", tv_q2b)
            ins = None
            if v_:
                ins = eng.tensor_copy(out=s["QB1"][:, 0:FG], in_=self.P(2)[:, 0:FG])
            self.i(ins, "VT")
            self.w("tensor", "VT")
            ins = None
            if t_:
                for tt in range(4):
                    ins = eng.matmul(self.P(3)[:, tt:tt + 1],
                                     RO[0:1, 2 * G + tt * 128: 2 * G + (tt + 1) * 128],
                                     s["ONES1"][0:1, 0:1], start=True, stop=True)
            self.i(ins, "TV")
            tvr = self.c["TV"]
            self.w("vector", "TV", tvr)
            ins = None
            if v_:
                eng.tensor_tensor(s["IV2"][:, 0:4], self.P(3)[:, 0:4],
                                  s["MXC"][:, 0:4], OP.mult)
                eng.drain()
                eng.tensor_scalar(s["IV2"][:, 0:4], s["IV2"][:, 0:4],
                                  1e-5, None, OP.max)
                eng.drain()
                ins = eng.tensor_scalar(s["IV2"][:, 0:4], s["IV2"][:, 0:4],
                                        s["MABP"][:, 1:2], 1.0 / 127.0, OP.mult, OP.mult)
            self.i(ins, "VT")
            HB = KF * FG // 2
            ins = None
            if v_:
                hv = s["H"][:, :]
                for h0 in (0, HB):
                    eng.tensor_tensor(
                        hv[:, h0:h0 + HB].rearrange("p (k t) -> p k t", t=FG),
                        hv[:, h0:h0 + HB].rearrange("p (k t) -> p k t", t=FG),
                        s["QB1"][:, 0:FG].unsqueeze(1).broadcast_to([128, KF // 2, FG]),
                        OP.mult)
                eng.drain()
                eng.tensor_scalar(hv[:, 0:HB], hv[:, 0:HB], MAGIC, MAGIC, OP.add, OP.subtract)
                eng.tensor_scalar(hv[:, HB:2 * HB], hv[:, HB:2 * HB], MAGIC, MAGIC,
                                  OP.add, OP.subtract)
                eng.drain()
                eng.tensor_scalar(hv[:, 0:HB], hv[:, 0:HB], 127.0, -128.0, OP.min, OP.max)
                ins = eng.tensor_scalar(hv[:, HB:2 * HB], hv[:, HB:2 * HB], 127.0, -128.0,
                                        OP.min, OP.max)
            self.i(ins, "VT")
            vt_c2 = self.c["VT"]
            NPC = 16
            def load_piece(pc, prev_mm_mark):
                buf = s["W2S0"] if (pc % 2 == 0) else s["W2S1"]
                sem = "LS" if (pc % 2 == 0) else "LS2"
                if prev_mm_mark is not None:
                    self.w("scalar", "TV", prev_mm_mark)
                self.w("scalar", sem)
                if fg == 0:
                    a0, a1 = self.q2_w2marks[QNP - 1]
                    self.w("scalar", "QS0", a0)
                    self.w("scalar", "QS1", a1)
                ins = None
                if s_:
                    ins = eng.dma_start(
                        out=buf[:, :].rearrange("p (a f) -> p a f", a=2),
                        in_=d["w2q_dram"].rearrange("(a p) f -> p a f", p=128)[:, 2 * pc:2 * pc + 2, :])
                return self.i(ins, sem, 16)

            mm_marks = [None] * NPC
            ld_marks = [None] * NPC
            ld_marks[0] = load_piece(0, prev_f2_marks[0])
            ld_marks[1] = load_piece(1, prev_f2_marks[1])
            self.w("tensor", "VT", vt_c2)
            for pc in range(NPC):
                buf = s["W2S0"] if (pc % 2 == 0) else s["W2S1"]
                self.w("tensor", "LS" if (pc % 2 == 0) else "LS2", ld_marks[pc])
                ins = None
                if t_:
                    for a in range(2):
                        Fk = 2 * pc + a
                        for tt in range(4):
                            for half in range(2):
                                ins = eng.matmul(
                                    self.pf[tt * 2 + half][:, 0:512],
                                    s["H"][:, Fk * FG + tt * 128: Fk * FG + tt * 128 + 128],
                                    buf[:, a * D + half * 512: a * D + half * 512 + 512],
                                    start=(Fk == 0), stop=(Fk == KF - 1))
                self.i(ins, "TV")
                mm_marks[pc] = self.c["TV"]
                if pc + 2 < NPC:
                    ld_marks[pc + 2] = load_piece(pc + 2, mm_marks[pc])
            prev_f2_marks = [mm_marks[NPC - 2], mm_marks[NPC - 1]]
            tv_f2 = self.c["TV"]
            for tt in range(4):
                for half in range(2):
                    self.w("vector", "TV", tv_f2)
                    self.w("vector", "ST")
                    ins = None
                    if v_:
                        ins = eng.scalar_tensor_tensor(
                            s["OST"][:, 0:512],
                            self.pf[tt * 2 + half][:, 0:512],
                            s["IV2"][:, tt:tt + 1],
                            s["XT2"][:, tt * D + half * 512: tt * D + half * 512 + 512],
                            OP.mult, OP.add)
                    self.i(ins, "VT")
                    vto = self.c["VT"]
                    self.w("sync", "VT", vto)
                    self.w("sync", "ST")
                    ins = None
                    if y_:
                        ins = eng.dma_start(
                            out=d["out"][g0 + tt * 128: g0 + tt * 128 + 128,
                                         half * 512:(half + 1) * 512],
                            in_=s["OST"][:, 0:512])
                    self.i(ins, "ST", 16)

    def cross_max(self, t_, v_, s_, y_, g_, src, Gw, vt_src, rowdst, rowoff, keep_cols=False):
        eng, s = self.eng, self.s
        nt = Gw // 128
        for tt in range(nt):
            self.w("tensor", "VT", vt_src)
            self.w("tensor", "VT")
            ins = None
            if t_:
                ins = eng.transpose(self.P(4)[:, 0:128] if tt % 2 == 0 else self.P(5)[:, 0:128],
                                    src[:, tt * 128:(tt + 1) * 128], s["IDENTF"][:, :])
            self.i(ins, "TV")
            tvt = self.c["TV"]
            self.w("vector", "TV", tvt)
            ins = None
            if v_:
                bank = self.P(4) if tt % 2 == 0 else self.P(5)
                ins = eng.tensor_reduce(s["MXC"][:, tt:tt + 1], bank[:, 0:128],
                                        AX, OP.max, apply_absolute_value=True)
            self.i(ins, "VT")
        vt_cols = self.c["VT"]
        self.w("tensor", "VT", vt_cols)
        ins = None
        if t_:
            ins = eng.transpose(self.P(4)[0:nt, 128:256], s["MXC"][:, 0:nt], s["IDENTF"][:, :])
        self.i(ins, "TV")
        tvt = self.c["TV"]
        self.w("vector", "TV", tvt)
        ins = None
        if v_:
            ins = eng.tensor_copy(out=s["TQ1"][0:nt, 0:128], in_=self.P(4)[0:nt, 128:256])
        self.i(ins, "VT")
        vtc = self.c["VT"]
        self.w("gpsimd", "VT", vtc)
        self.w("gpsimd", "SC")
        self.w("gpsimd", "LG")
        ins = None
        if g_:
            ins = eng.dma_start(
                out=rowdst[0:1, rowoff:rowoff + Gw].rearrange("o (a t) -> o a t", a=nt),
                in_=s["TQ1"][0:nt, 0:128])
        self.i(ins, "LG", 16)
        lgf = self.c["LG"]
        self.w("vector", "LG", lgf)
        return self.c["LG"]



def _consts():
    i = np.arange(128)
    k16 = np.arange(NH)
    tri1 = (i[:, None] <= i[None, :]).astype(ml_dtypes.bfloat16)
    tri2 = (i[:, None] > i[None, :]).astype(ml_dtypes.bfloat16)
    mask = np.where(i[:, None] <= i[None, :], 0.0, -MASKVAL).astype(ml_dtypes.bfloat16)
    parity = (i[None, :] // 64 == k16[:, None] % 2).astype(np.float32)
    pairsel = (k16[:, None] // 2 == np.arange(8)[None, :]).astype(np.float32)
    pj8 = np.repeat((k16[:, None] // 2 == np.arange(8)[None, :]).astype(ml_dtypes.bfloat16),
                    128, axis=1)
    ind16 = np.zeros((NH, NH * 128), np.float32)
    for k in range(NH):
        ind16[k, k * 128:(k + 1) * 128] = -1.0
    return dict(
        c_identb=np.eye(128, dtype=ml_dtypes.bfloat16),
        c_identf=np.eye(128, dtype=np.float32),
        c_tri1=tri1, c_tri2=tri2, c_mask=mask, c_ind16=ind16,
        c_parityf=parity, c_parityb=parity.astype(ml_dtypes.bfloat16),
        c_pairsel=pairsel, c_pj8=pj8,
        c_ones1=np.ones((1, 128), np.float32),
        c_qcol=np.concatenate([np.full((128, 1), MAGIC, np.float32),
                               np.full((128, 1), -1.0, np.float32),
                               np.zeros((128, 1), np.float32)],
                              axis=1),
        c_onesc=np.ones((128, 1), np.float32),
        c_onescb=np.ones((128, 1), ml_dtypes.bfloat16),
    )


def make_inputs_for_core(core, x, Wq, Wk, Wv, Wo, Wg, bgv, w1, b1, w2, b2, T=2048):
    b, sq, _ = x.shape
    xf = np.asarray(x, np.float32).reshape(b * sq, D)
    seq = core // 2
    half = core % 2
    own = xf[seq * sq + half * T: seq * sq + (half + 1) * T]
    if half == 1:
        pfx = xf[seq * sq: seq * sq + T]
    else:
        pfx = np.zeros((T, D), np.float32)
    ins = dict(
        xT=np.ascontiguousarray(own.T),
        xpT=np.ascontiguousarray(pfx.T),
        wqT=np.ascontiguousarray(np.asarray(Wq, np.float32).T),
        wkT=np.ascontiguousarray(np.asarray(Wk, np.float32).T),
        wvT=np.ascontiguousarray(np.asarray(Wv, np.float32).T),
        woT=np.ascontiguousarray(np.asarray(Wo, np.float32).T),
        wgT=np.ascontiguousarray(np.asarray(Wg, np.float32).T),
        bgp=np.concatenate([np.asarray(bgv, ml_dtypes.bfloat16).reshape(1, NH),
                            np.zeros((127, NH), ml_dtypes.bfloat16)], axis=0),
        w1T=np.ascontiguousarray(np.asarray(w1, np.float32).T),
        w2T=np.ascontiguousarray(np.asarray(w2, np.float32).T),
        b1r=np.ascontiguousarray(np.asarray(b1, np.float32).reshape(KF, 128).T),
        b2r=np.asarray(b2, np.float32).reshape(1, D),
    )
    ins.update(_consts())
    return ins


_NC_CACHE = {}


def kernel(L0_attn_x, L0_attn_Wq, L0_attn_Wk, L0_attn_Wv, L0_attn_Wo, L0_attn_Wg,
           L0_attn_bg, L0_ffn_w1, L0_ffn_b1, L0_ffn_w2, L0_ffn_b2):
    T = 2048
    key = ("full", T)
    if key not in _NC_CACHE:
        _NC_CACHE[key] = Builder(T=T, do_prefix=True).build()
    nc = _NC_CACHE[key]
    in_maps = [
        make_inputs_for_core(c, L0_attn_x, L0_attn_Wq, L0_attn_Wk, L0_attn_Wv,
                             L0_attn_Wo, L0_attn_Wg, L0_attn_bg,
                             L0_ffn_w1, L0_ffn_b1, L0_ffn_w2, L0_ffn_b2, T=T)
        for c in range(NCORES)
    ]
    res = run_bass_kernel_spmd(nc, in_maps, core_ids=list(range(NCORES)))
    outs = [np.asarray(r["out"], np.float32) for r in res.results]
    b, sq, _ = L0_attn_x.shape
    full = np.empty((b * sq, D), np.float32)
    for c in range(NCORES):
        seq, half = c // 2, c % 2
        full[seq * sq + half * T: seq * sq + (half + 1) * T] = outs[c]
    return full.reshape(b, sq, D)


# version 17
# speedup vs baseline: 1.1067x; 1.0303x over previous
import numpy as np
import ml_dtypes
import concourse.bass as bass
import concourse.mybir as mybir
from concourse.bass_utils import run_bass_kernel_spmd
from contextlib import ExitStack

F32 = mybir.dt.float32
BF16 = mybir.dt.bfloat16
AF = mybir.ActivationFunctionType
OP = mybir.AluOpType
AX = mybir.AxisListType.X

MAGIC = 12582912.0
MASKVAL = -30000.0
EPS = 1e-5
D = 1024
NH = 16
DH = 64
FF = 4096
C = 128
SCALE = 0.125
NCORES = 8
G = 512
FG = 512
KD = D // 128
KF = FF // 128

ENGINES = ("tensor", "vector", "scalar", "sync", "gpsimd")


class Pool:

    def __init__(self, base_ap_f32, nwords):
        self.base = base_ap_f32
        self.n = nwords
        self.off = 0

    def tensor(self, shape, dt):
        P, F = shape
        words = (F * (2 if dt == BF16 else 4) + 3) // 4
        words = (words + 7) & ~7
        assert self.off + words <= self.n, (self.off, words, self.n)
        view = self.base[:, self.off:self.off + words]
        self.off += words
        if dt == BF16:
            view = view.bitcast(BF16)
        return view[0:P, 0:F]


QP = 256
QNP = (D * FF) // (128 * QP)


def _mk_qphase(kind, w, np_):
    st = [("ld", w, 0), ("ld", w, 1)]
    for p in range(np_):
        if p + 2 < np_:
            st.append(("ld", w, p + 2))
        st.append((kind, w, p))
    return st


QSTEPS = (_mk_qphase("am", 0, QNP) + [("ws", 0)]
          + _mk_qphase("am", 1, QNP) + [("ws", 1)])


class Builder:
    def __init__(self, T=2048, do_prefix=True, debug_xmid=False, stop_stage=None):
        self.T = T
        self.do_prefix = do_prefix
        self.debug_xmid = debug_xmid
        self.stop_stage = stop_stage
        self.NG = T // G
        self.NFG = T // FG
        self.NCH = G // C

    def cut(self, label):
        if self.stop_stage is None:
            return False
        if self._nstage >= self.stop_stage:
            return True
        self._nstage += 1
        return False

    def on(self, e):
        return self.ename == e

    def w(self, engine, sem, val=None):
        if val is None:
            val = self.c[sem]
        if self.ename == engine:
            self.eng.wait_ge(self.sems[sem], val)
        return val

    def i(self, instr, sem, n=1):
        self.c[sem] += n
        if instr is not None:
            instr.then_inc(self.sems[sem], n)
        return self.c[sem]

    def build(self):
        nc = bass.Bass("TRN2", target_bir_lowering=False, debug=False)
        self.nc = nc
        T = self.T

        def din(name, shape, dt=F32):
            return nc.dram_tensor(name, shape, dt, kind="ExternalInput").ap()

        d = self.d = {}
        d["xT"] = din("xT", [D, T])
        d["xpT"] = din("xpT", [D, T])
        for nmm in ("wqT", "wkT", "wvT", "woT"):
            d[nmm] = din(nmm, [D, D])
        d["wgT"] = din("wgT", [D, NH])
        d["bgp"] = din("bgp", [128, NH], BF16)
        d["w1T"] = din("w1T", [D, FF])
        d["w2T"] = din("w2T", [FF, D])
        d["b1r"] = din("b1r", [128, KF])
        d["b2r"] = din("b2r", [1, D])
        d["c_identb"] = din("c_identb", [128, 128], BF16)
        d["c_identf"] = din("c_identf", [128, 128], F32)
        d["c_tri1"] = din("c_tri1", [128, 128], BF16)
        d["c_tri2"] = din("c_tri2", [128, 128], BF16)
        d["c_mask"] = din("c_mask", [128, 128], BF16)
        d["c_parityf"] = din("c_parityf", [NH, 128], F32)
        d["c_parityb"] = din("c_parityb", [NH, 128], BF16)
        d["c_pairsel"] = din("c_pairsel", [NH, 8], F32)
        d["c_pj8"] = din("c_pj8", [NH, 8 * 128], BF16)
        d["c_ind16"] = din("c_ind16", [NH, NH * 128], F32)
        d["c_ones1"] = din("c_ones1", [1, 128], F32)
        d["c_qcol"] = din("c_qcol", [128, 3], F32)
        d["c_onesc"] = din("c_onesc", [128, 1], F32)
        d["c_onescb"] = din("c_onescb", [128, 1], BF16)
        if self.debug_xmid or self.stop_stage is not None:
            d["out"] = nc.dram_tensor("out", [128, KD * T], F32, kind="ExternalOutput").ap()
        else:
            d["out"] = nc.dram_tensor("out", [T, D], F32, kind="ExternalOutput").ap()
        d["w2q_dram"] = nc.dram_tensor("w2q_scratch", [FF, D], BF16).ap()

        es = ExitStack()
        SB = lambda n, s, dt: es.enter_context(nc.sbuf_tensor(n, s, dt))
        with es:
            s = self.s = {}
            for nmm, shape, dt in [
                ("IDENTB", [128, 128], BF16), ("IDENTF", [128, 128], F32),
                ("ONES1", [1, 128], F32), ("ONESC", [128, 1], F32), ("ONESCB", [128, 1], BF16),
            ]:
                s[nmm] = SB(nmm, shape, dt)
            s["XF"] = SB("XF", [128, KD * T], F32)
            s["SM"] = SB("SM", [128, 8 * DH], F32)
            s["SHIZ"] = SB("SHIZ", [128, NH * DH], BF16)
            s["WSC"] = SB("WSC", [128, 2], F32)
            s["MAB"] = SB("MAB", [1, 2], F32)
            s["MABP"] = SB("MABP", [128, 2], F32)

            POOLW = (nc.sbuf_bytes_remaining // 4) & ~7
            s["POOL"] = SB("POOL", [128, POOLW], F32)
            att = Pool(s["POOL"][:, :], POOLW)
            SBa = lambda n, sh, dt: att.tensor(sh, dt)
            for nmm, shape, dt in [
                ("TRI1", [128, 128], BF16), ("TRI2", [128, 128], BF16),
                ("MASK", [128, 128], BF16),
                ("PARITYF", [NH, 128], F32), ("PARITYB", [NH, 128], BF16),
                ("PAIRSEL", [NH, 8], F32), ("PJ8", [NH, 8 * 128], BF16),
                ("IND17F", [128, NH * 128], F32), ("BGRP", [128, NH], BF16),
                ("WQT", [128, KD * D], BF16), ("WKT", [128, KD * D], BF16),
                ("WVT", [128, KD * D], BF16), ("WOT", [128, KD * D], BF16),
                ("WGT", [128, KD * NH], BF16),
                ("U1", [128, KD * G], BF16),
                ("KA", [128, KD * G], BF16), ("VA", [128, 4 * D], BF16),
                ("XPU", [128, KD * G], F32),
                ("QT", [128, 8 * C], BF16), ("KH", [128, NH * DH], BF16),
                ("EBC", [128, 8 * C], BF16),
                ("ZALL", [128, 4 * NH], F32),
                ("SPt", [128, NH], BF16), ("CP", [128, NH], F32),
                ("EK", [128, NH], F32), ("STAGE17", [NH + 1, 128], F32),
                ("ETT", [NH, 128], F32), ("AR16", [NH, 128], F32),
                ("EB2", [NH, 8 * 128], BF16), ("DRP", [128, 8], F32),
                ("TMPE", [128, 512], F32), ("TMPE2", [128, 512], F32),
                ("QLD", [128, 2 * QP], F32), ("QSTG", [128, 2 * QP], BF16),
                ("QACC", [128, QP], F32), ("QCOL", [128, 3], F32),
                ("LNRA", [1, 2 * G], F32),
            ]:
                s[nmm] = SBa(nmm, shape, dt)
            s["IND17"] = s["IND17F"][0:17, :]
            s["LNR"] = s["LNRA"][0:1, 0:2 * G]
            s["DL"] = s["TMPE2"][:, :]
            s["AP4"] = s["TMPE"][:, :].bitcast(BF16)[:, 0:512]
            s["XP"] = s["XPU"][:, :]
            _ubf = s["XPU"][:, :].bitcast(BF16)
            s["QA"] = _ubf[:, 0:KD * G]
            s["YA"] = _ubf[:, KD * G:2 * KD * G]
            pa = self.pa = {}
            pbanks = []
            for k in range(8):
                pbanks.append(es.enter_context(nc.psum_tensor("PBK%d" % k, [128, 512], F32)))
            for k in range(8):
                if k == 2:
                    pa[k] = pbanks[k].ap().bitcast(BF16)
                else:
                    pa[k] = pbanks[k].ap()

            semnames = ["LD", "LG", "LS", "LS2", "LX", "LW", "ST", "TV", "VT", "SC", "GC",
                        "QL0", "QL1", "QS0", "QS1", "QF", "QV", "QW"]
            self.sems = {k: es.enter_context(nc.semaphore(k)) for k in semnames}

            block = es.enter_context(nc.Block())

            ffn = Pool(s["POOL"][:, :], POOLW)
            SBf = lambda n, sh, dt: ffn.tensor(sh, dt)
            for nmm, shape, dt in [
                ("W1Q", [128, KD * FF], BF16),
                ("W2S0", [128, 2 * D], BF16), ("W2S1", [128, 2 * D], BF16),
                ("H", [128, KF * FG], BF16),
                ("C1", [128, KD * FG], BF16),
                ("XT2", [128, 4 * D], BF16), ("OST", [128, 512], F32),
                ("B2BC", [128, D], BF16),
                ("B1R", [128, KF], F32),
                ("QB2", [128, FG], BF16),
                ("MXC", [128, 4], F32),
                ("IV2", [128, 4], F32),
                ("TQ1", [128, FG], BF16),
                ("FTMP", [128, 512], BF16),
                ("ROWS", [1, 5 * G], F32),

            ]:
                s[nmm] = SBf(nmm, shape, dt)
            s["MAXA"] = s["OST"][:, :]
            s["QLD2"] = s["W2S0"][:, :].bitcast(F32)[:, 0:2 * QP]
            s["QSTG2"] = s["W2S1"][:, 0:2 * QP]
            s["QB1"] = s["QB2"][:, :]
            s["B2R"] = s["ROWS"][0:1, 0:D]
            pf = self.pf = {}
            for k in range(8):
                pf[k] = pbanks[k].ap()

            @block.tensor
            def _(eng):
                self.walk("tensor", eng)

            @block.vector
            def _(eng):
                self.walk("vector", eng)

            @block.scalar
            def _(eng):
                self.walk("scalar", eng)

            @block.sync
            def _(eng):
                self.walk("sync", eng)

            @block.gpsimd
            def _(eng):
                self.walk("gpsimd", eng)

        return nc

    def walk(self, ename, eng):
        self.ename = ename
        self.eng = eng
        self._nstage = 0
        self.c = {k: 0 for k in self.sems}
        self.qi = 0
        self.q_pending = []
        self.q_lpar = 0
        self.q_spar = 0
        self.q_stmark = [0, 0]
        self.q_ldmark = [(None, 0), (None, 0)]
        self.q_wsdone = [0, 0]
        self.q_ammark = 0
        t_, v_, s_, y_, g_ = (self.on("tensor"), self.on("vector"), self.on("scalar"),
                              self.on("sync"), self.on("gpsimd"))
        s, d, pa, pf = self.s, self.d, self.pa, self.pf
        T, NG, NCH = self.T, self.NG, self.NCH

        def ld(dst_ap, src_ap, q="sync"):
            ins = None
            if (q == "sync" and y_) or (q == "gpsimd" and g_):
                ins = eng.dma_start(out=dst_ap, in_=src_ap)
            return self.i(ins, "LD" if q == "sync" else "LG", 16)

        ld(s["XF"][:, :].rearrange("p (k t) -> p k t", k=KD),
           d["xT"].rearrange("(k p) t -> p k t", p=128))
        for nmm, srcn in [("IDENTB", "c_identb"), ("IDENTF", "c_identf"),
                          ("TRI1", "c_tri1"), ("TRI2", "c_tri2"), ("MASK", "c_mask"),
                          ("PARITYF", "c_parityf"), ("PARITYB", "c_parityb"),
                          ("PAIRSEL", "c_pairsel"), ("PJ8", "c_pj8"),
                          ("ONES1", "c_ones1"),
                          ("ONESC", "c_onesc"), ("ONESCB", "c_onescb"),
                          ("BGRP", "bgp")]:
            ap = s[nmm]
            ld(ap[:, :] if not isinstance(ap, bass.AP) else ap, d[srcn])
        ld(s["IND17"][0:NH, :], d["c_ind16"])
        ld(s["QCOL"][:, :], d["c_qcol"])
        ld_const = self.c["LD"]

        for nmm, src, wide in [("WKT", "wkT", D), ("WVT", "wvT", D), ("WGT", "wgT", NH),
                               ("WQT", "wqT", D), ("WOT", "woT", D)]:
            ld(s[nmm][:, :].rearrange("p (k f) -> p k f", k=KD),
               d[src].rearrange("(k p) f -> p k f", p=128), q="gpsimd")
        lg_kvg = self.c["LG"]
        lg_allw = self.c["LG"]

        ins = None
        if v_:
            eng.memset(s["SM"][:, :], 0.0)
            eng.memset(s["SHIZ"][:, :], 0.0)
            ins = eng.memset(s["STAGE17"][:, :], 1.0)
        self.i(ins, "VT")

        def ln_group(xsrc, Tx, g0, Gw, dst, rows, scr1, scr2):
            scr = [scr1, scr2]
            for k in range(KD):
                xc = xsrc[:, k * Tx + g0: k * Tx + g0 + Gw]
                tile = scr[k % 2][:, 0:Gw]
                if k >= 2:
                    self.w("scalar", "TV", tv_sq[k % 2])
                ins = None
                if s_:
                    ins = eng.activation(tile, xc, AF.Square)
                self.i(ins, "SC")
                sq_done = self.c["SC"]
                self.w("tensor", "SC", sq_done)
                if k == 0:
                    self.w("tensor", "VT")
                ins = None
                if t_:
                    onesq = s["ONESC"] if tile.dtype == F32 else s["ONESCB"]
                    eng.matmul(self.P(6)[0:1, 0:Gw], s["ONESC"][:, :], xc,
                               start=(k == 0), stop=(k == KD - 1))
                    ins = eng.matmul(self.P(7)[0:1, 0:Gw], onesq[:, :], tile,
                                     start=(k == 0), stop=(k == KD - 1))
                mark = self.i(ins, "TV")
                if k == 0:
                    tv_sq = {0: mark, 1: mark}
                tv_sq[k % 2] = mark
            tv_stats = self.c["TV"]
            self.w("vector", "TV", tv_stats)
            self.w("vector", "LG")
            L = rows
            ins = None
            if v_:
                eng.tensor_scalar(L[0:1, 0:Gw], self.P(6)[0:1, 0:Gw], 1.0 / D, None, OP.mult)
                eng.tensor_scalar(L[0:1, G:G + Gw], self.P(7)[0:1, 0:Gw], 1.0 / D, None, OP.mult)
                eng.tensor_tensor(scr1[0:1, 0:Gw], L[0:1, 0:Gw], L[0:1, 0:Gw], OP.mult)
                eng.tensor_scalar(scr1[0:1, 0:Gw], scr1[0:1, 0:Gw],
                                  -1.0, EPS, OP.mult, OP.add)
                ins = eng.tensor_tensor(L[0:1, G:G + Gw], L[0:1, G:G + Gw],
                                        scr1[0:1, 0:Gw], OP.add)
            self.i(ins, "VT")
            self.w("scalar", "VT")
            ins = None
            if s_:
                eng.activation(L[0:1, G:G + Gw], L[0:1, G:G + Gw], AF.Ln)
                ins = eng.activation(L[0:1, G:G + Gw], L[0:1, G:G + Gw],
                                     AF.Exp, scale=-0.5)
            self.i(ins, "SC")
            self.w("vector", "SC")
            ins = None
            if v_:
                eng.tensor_tensor(L[0:1, 0:Gw], L[0:1, 0:Gw],
                                  L[0:1, G:G + Gw], OP.mult)
                ins = eng.tensor_scalar(L[0:1, 0:Gw], L[0:1, 0:Gw],
                                        -1.0, None, OP.mult)
            self.i(ins, "VT")
            self.w("tensor", "VT")
            ins = None
            if t_:
                eng.matmul(self.P(0)[:, 0:Gw], s["ONES1"][0:1, :], L[0:1, G:G + Gw],
                           start=True, stop=True)
                ins = eng.matmul(self.P(1)[:, 0:Gw], s["ONES1"][0:1, :], L[0:1, 0:Gw],
                                 start=True, stop=True)
            self.i(ins, "TV")
            tv_ab = self.c["TV"]
            self.w("vector", "TV", tv_ab)
            ins = None
            for k0 in range(0, KD, 2):
                xc0 = xsrc[:, k0 * Tx + g0: k0 * Tx + g0 + Gw]
                xc1 = xsrc[:, (k0 + 1) * Tx + g0: (k0 + 1) * Tx + g0 + Gw]
                if v_:
                    eng.tensor_tensor(scr1[:, 0:Gw], xc0, self.P(0)[:, 0:Gw], OP.mult)
                    eng.tensor_tensor(scr2[:, 0:Gw], xc1, self.P(0)[:, 0:Gw], OP.mult)
                    eng.tensor_tensor(dst[:, k0 * Gw:(k0 + 1) * Gw], scr1[:, 0:Gw],
                                      self.P(1)[:, 0:Gw], OP.add)
                    ins = eng.tensor_tensor(dst[:, (k0 + 1) * Gw:(k0 + 2) * Gw],
                                            scr2[:, 0:Gw], self.P(1)[:, 0:Gw], OP.add)
            return self.i(ins, "VT")

        self.Pmap = pa
        self.walk_attention(t_, v_, s_, y_, g_, ld_const, lg_kvg, lg_allw, ln_group)
        if not (self.debug_xmid or (self.stop_stage is not None and self._nstage >= self.stop_stage)):
            self.Pmap = pf
            self.walk_ffn(t_, v_, s_, y_, g_, ln_group)
        if self.stop_stage is not None and not self.debug_xmid:
            self.w("sync", "VT")
            self.w("sync", "TV")
            self.w("sync", "SC")
            ins = None
            if y_:
                ins = eng.dma_start(out=d["out"], in_=s["XF"][:, :])
            self.i(ins, "ST", 16)

        self.w("sync", "ST", self.c["ST"])

    def P(self, k):
        return self.Pmap[k][:, :]

    def q_src(self, w, p):
        if w == 0:
            r, c0 = divmod(p, FF // QP)
            return self.d["w1T"][r * 128:(r + 1) * 128, c0 * QP:(c0 + 1) * QP]
        r, c0 = divmod(p, D // QP)
        return self.d["w2T"][r * 128:(r + 1) * 128, c0 * QP:(c0 + 1) * QP]

    def q_dst(self, w, p):
        assert w == 1
        r, c0 = divmod(p, D // QP)
        return self.d["w2q_dram"][r * 128:(r + 1) * 128, c0 * QP:(c0 + 1) * QP]

    def svc(self, n=1):
        eng, s = self.eng, self.s
        g_ = self.on("gpsimd")
        s_ = self.on("scalar")
        v_ = self.on("vector")
        for _ in range(n):
            if self.qi >= len(QSTEPS):
                return
            st = QSTEPS[self.qi]
            self.qi += 1
            kind = st[0]
            if kind == "ld":
                _, w, p = st
                pb = self.q_lpar % 2
                self.q_lpar += 1
                gsem, gval = self.q_ldmark[pb]
                if gsem is not None:
                    self.w("scalar", gsem, gval)
                self.w("scalar", "QF", max(self.q_wsdone))
                ins = None
                if s_:
                    ins = eng.dma_start(out=s["QLD"][:, pb * QP:(pb + 1) * QP],
                                        in_=self.q_src(w, p))
                self.i(ins, "QL%d" % pb, 16)
                self.q_pending.append((pb, self.c["QL%d" % pb]))
            elif kind == "am":
                _, w, p = st
                pb, qlc = self.q_pending.pop(0)
                self.w("scalar", "QL%d" % pb, qlc)
                ql = s["QLD"][:, pb * QP:(pb + 1) * QP]
                col = w * QNP + p
                ins = None
                if s_:
                    ins = eng.activation(ql, ql, AF.Abs,
                                         accum_out=s["QACC"][:, col:col + 1])
                self.q_ldmark[pb] = (None, 0)
                self.i(ins, "QV")
                self.q_ammark = self.c["QV"]
            elif kind == "ws":
                _, w = st
                numel = float(D * FF)
                QL_ = s["QLD"]
                base = w * QNP
                self.w("vector", "QV", self.q_ammark)
                ins = None
                if v_:
                    ins = eng.tensor_reduce(s["QACC"][:, base:base + 1],
                                            s["QACC"][:, base:base + QNP],
                                            AX, OP.add)
                mw1 = self.i(ins, "QW")
                self.w("gpsimd", "QW", mw1)
                ins = None
                if g_:
                    ins = eng.dma_start(out=QL_[0:1, 0:128],
                                        in_=s["QACC"][:, base:base + 1])
                self.i(ins, "QF", 16)
                qf_row = self.c["QF"]
                self.w("vector", "QF", qf_row)
                ins = None
                if v_:
                    eng.tensor_reduce(QL_[0:1, 0:1], QL_[0:1, 0:128], AX, OP.add)
                    eng.tensor_scalar(QL_[0:1, 0:1], QL_[0:1, 0:1],
                                      1.0 / numel, 1e-5, OP.mult, OP.max)
                    eng.reciprocal(QL_[0:1, 1:2], QL_[0:1, 0:1])
                    eng.tensor_scalar(QL_[0:1, 128:256], s["ONES1"][0:1, :],
                                      QL_[0:1, 1:2], None, OP.mult)
                    ins = eng.tensor_scalar(QL_[0:1, 256:384], s["ONES1"][0:1, :],
                                            QL_[0:1, 0:1], None, OP.mult)
                m2 = self.i(ins, "QW")
                self.w("gpsimd", "QW", m2)
                ins = None
                if g_:
                    eng.dma_start(out=s["WSC"][:, w:w + 1],
                                  in_=QL_[0:1, 128:256]).then_inc(
                        self.sems["QF"], 16)
                    ins = eng.dma_start(out=s["MABP"][:, w:w + 1],
                                        in_=QL_[0:1, 256:384])
                self.c["QF"] += 16
                self.i(ins, "QF", 16)
                self.q_wsdone[w] = self.c["QF"]


    def walk_attention(self, t_, v_, s_, y_, g_, ld_const, lg_kvg, lg_allw, ln_group):
        eng, s, d = self.eng, self.s, self.d
        T, NG, NCH = self.T, self.NG, self.NCH

        def projections(with_q):
            plist = [("k", "WKT", "KA")] + ([("q", "WQT", "QA")] if with_q else [])
            for nm, W, dstA in plist:
                sc = SCALE if nm == "q" else 1.0
                for j in range(KD):
                    bank = self.P(0) if (j % 2 == 0) else self.P(1)
                    self.w("tensor", "VT")
                    ins = None
                    if t_:
                        for k in range(KD):
                            ins = eng.matmul(bank[:, 0:G],
                                             s[W][:, k * D + j * 128: k * D + j * 128 + 128],
                                             s["U1"][:, k * G:(k + 1) * G],
                                             start=(k == 0), stop=(k == KD - 1))
                    self.i(ins, "TV")
                    tvp = self.c["TV"]
                    self.w("vector", "TV", tvp)
                    ins = None
                    if v_:
                        ins = eng.tensor_scalar(s["TMPE"][:, :], bank[:, 0:G], sc, 0.0,
                                                OP.mult, OP.min)
                    self.i(ins, "VT")
                    self.w("scalar", "VT")
                    ebf = s["TMPE2"][:, :].bitcast(BF16)
                    ins = None
                    if s_:
                        eng.activation(ebf[:, 0:G], s["TMPE"][:, :], AF.Exp)
                        ins = eng.activation(ebf[:, G:2 * G], bank[:, 0:G], AF.Relu, scale=sc)
                    self.i(ins, "SC")
                    self.w("vector", "SC")
                    ins = None
                    if v_:
                        ins = eng.tensor_tensor(s[dstA][:, j * G:(j + 1) * G],
                                                ebf[:, 0:G], ebf[:, G:2 * G], OP.add)
                    self.i(ins, "VT")
            for tt in range(NCH):
                for half in range(2):
                    bank = self.P(0) if (half == 0) else self.P(1)
                    self.w("tensor", "VT")
                    ins = None
                    if t_:
                        for k in range(KD):
                            ins = eng.matmul(
                                bank[:, 0:512],
                                s["U1"][:, k * G + tt * C: k * G + tt * C + C],
                                s["WVT"][:, k * D + half * 512: k * D + half * 512 + 512],
                                start=(k == 0), stop=(k == KD - 1))
                    self.i(ins, "TV")
                    tvv = self.c["TV"]
                    self.w("vector", "TV", tvv)
                    ins = None
                    if v_:
                        ins = eng.tensor_copy(
                            out=s["VA"][:, tt * D + half * 512: tt * D + half * 512 + 512],
                            in_=bank[:, 0:512])
                    self.i(ins, "VT")
            self.w("tensor", "VT")
            ins = None
            if t_:
                for tt in range(NCH):
                    for k in range(KD):
                        eng.matmul(self.P(4)[:, tt * NH:(tt + 1) * NH],
                                   s["U1"][:, k * G + tt * C: k * G + tt * C + C],
                                   s["WGT"][:, k * NH:(k + 1) * NH],
                                   start=(k == 0), stop=False)
                    ins = eng.matmul(self.P(4)[:, tt * NH:(tt + 1) * NH],
                                     s["TRI1"][:, :], s["BGRP"][:, :],
                                     start=False, stop=True)
            self.i(ins, "TV")
            tv_g = self.c["TV"]
            self.w("vector", "TV", tv_g)
            ins = None
            if v_:
                ins = eng.tensor_copy(out=s["ZALL"][:, :], in_=self.P(4)[:, 0:4 * NH])
            self.i(ins, "VT")
            return self.c["VT"]

        def gate_pipeline(tt):
            self.w("scalar", "VT")
            ins = None
            if s_:
                eng.activation(s["SPt"][:, :], s["ZALL"][:, tt * NH:(tt + 1) * NH],
                               AF.Exp, scale=-1.0)
                ins = eng.activation(s["SPt"][:, :], s["SPt"][:, :], AF.Ln, bias=1.0)
            self.i(ins, "SC")
            sc_sp = self.c["SC"]
            if self.cut("g_sp"):
                return None
            self.w("tensor", "SC", sc_sp)
            self.w("tensor", "VT")
            ins = None
            if t_:
                eng.matmul(self.P(4)[:, 64:64 + NH], s["TRI1"][:, :], s["SPt"][:, :],
                           start=True, stop=True)
                ins = eng.matmul(self.P(4)[:, 64 + NH:64 + 2 * NH], s["TRI2"][:, :],
                                 s["SPt"][:, :], start=True, stop=True)
            self.i(ins, "TV")
            tv_cum = self.c["TV"]
            self.w("vector", "TV", tv_cum)
            ins = None
            if v_:
                ins = eng.tensor_copy(out=s["CP"][:, :], in_=self.P(4)[:, 64:64 + NH])
            self.i(ins, "VT")
            vt_cp = self.c["VT"]
            self.w("scalar", "TV", tv_cum)
            ins = None
            if s_:
                ins = eng.activation(s["EK"][:, :], self.P(4)[:, 64 + NH:64 + 2 * NH],
                                     AF.Exp, scale=-1.0)
            self.i(ins, "SC")
            sc_ek = self.c["SC"]
            if self.cut("g_ek"):
                return None
            self.w("tensor", "VT", vt_cp)
            self.w("tensor", "SC", sc_ek)
            ins = None
            if t_:
                ins = eng.transpose(self.P(4)[0:NH, 128:256], s["CP"][:, :], s["IDENTF"][:, :])
            self.i(ins, "TV")
            tv_cpt = self.c["TV"]
            self.w("vector", "TV", tv_cpt)
            self.w("vector", "LG")
            ins = None
            if v_:
                ins = eng.tensor_copy(out=s["STAGE17"][0:NH, :], in_=self.P(4)[0:NH, 128:256])
            self.i(ins, "VT")
            vt_cpt = self.c["VT"]
            self.w("scalar", "VT", vt_cpt)
            ins = None
            if s_:
                ins = eng.activation(s["ETT"][:, :], s["STAGE17"][0:NH, :], AF.Exp, scale=-1.0)
            self.i(ins, "SC")
            sc_ett = self.c["SC"]
            self.w("vector", "VT", vt_cpt)
            ins = None
            if v_:
                ins = eng.tensor_scalar(s["AR16"][:, :], s["PARITYF"][:, :],
                                        s["STAGE17"][0:NH, 127:128], None, OP.mult)
            self.i(ins, "VT")
            vt_ar = self.c["VT"]
            if self.cut("g_ett"):
                return None
            self.w("gpsimd", "VT", vt_cpt)
            self.w("gpsimd", "TV")
            self.w("gpsimd", "LG")
            ins = None
            if g_:
                ins = eng.dma_start(
                    out=s["IND17"][NH:NH + 1, :].rearrange("o (h t) -> o h t", h=NH),
                    in_=s["STAGE17"][0:NH, :])
            self.i(ins, "LG", 16)
            lg_flat = self.c["LG"]
            if self.cut("g_flat"):
                return None
            self.w("tensor", "VT", vt_ar)
            ins = None
            if t_:
                ins = eng.matmul(self.P(4)[:, 256:264], s["AR16"][:, :],
                                 s["PAIRSEL"][:, :], start=True, stop=True)
            self.i(ins, "TV")
            tv_drp = self.c["TV"]
            self.w("scalar", "TV", tv_drp)
            ins = None
            if s_:
                ins = eng.activation(s["DRP"][:, :], self.P(4)[:, 256:264], AF.Exp, scale=-1.0)
            self.i(ins, "SC")
            sc_drp = self.c["SC"]
            if self.cut("g_drp"):
                return None
            self.w("vector", "SC", sc_ett)
            ins = None
            if v_:
                ev = s["ETT"][:, :].unsqueeze(1).broadcast_to([NH, 4, 128])
                eng.tensor_tensor(s["EB2"][:, 0:512].rearrange("p (j t) -> p j t", j=4),
                                  s["PJ8"][:, 0:512].rearrange("p (j t) -> p j t", j=4),
                                  ev, OP.mult)
                ins = eng.tensor_tensor(s["EB2"][:, 512:1024].rearrange("p (j t) -> p j t", j=4),
                                        s["PJ8"][:, 512:1024].rearrange("p (j t) -> p j t", j=4),
                                        ev, OP.mult)
            self.i(ins, "VT")
            vt_eb2 = self.c["VT"]
            self.w("tensor", "VT", vt_eb2)
            self.w("tensor", "SC")
            ins = None
            if t_:
                eng.matmul(self.P(5)[:, 0:512], s["PARITYB"][:, :], s["EB2"][:, 0:512],
                           start=True, stop=True)
                ins = eng.matmul(self.P(6)[:, 0:512], s["PARITYB"][:, :], s["EB2"][:, 512:1024],
                                 start=True, stop=True)
            self.i(ins, "TV")
            tv_ebc = self.c["TV"]
            if self.cut("g_ebcmm"):
                return None
            self.w("vector", "TV", tv_ebc)
            ins = None
            if v_:
                eng.tensor_copy(out=s["EBC"][:, 0:512], in_=self.P(5)[:, 0:512])
                ins = eng.tensor_copy(out=s["EBC"][:, 512:1024], in_=self.P(6)[:, 0:512])
            self.i(ins, "VT")
            return {"sc_ek": sc_ek, "sc_drp": sc_drp, "vt_rest": self.c["VT"],
                    "lg_flat": lg_flat}

        def chunk_heads(gi, tt, gm, with_qy):
            self.w("tensor", "VT")
            ins = None
            if t_:
                for j in range(KD):
                    ins = eng.transpose(self.pa[2][:, j * 128:(j + 1) * 128],
                                        s["KA"][:, j * G + tt * C: j * G + tt * C + C],
                                        s["IDENTB"][:, :])
            self.i(ins, "TV")
            tv_ktr = self.c["TV"]
            self.w("vector", "TV", tv_ktr)
            self.w("vector", "SC", gm["sc_ek"])
            ins = None
            if v_:
                ekb = s["EK"][:, :].unsqueeze(2).broadcast_to([128, NH, DH])
                ins = eng.tensor_tensor(
                    s["KH"][:, :].rearrange("p (h e) -> p h e", h=NH),
                    self.pa[2][:, 0:1024].rearrange("p (h e) -> p h e", h=NH),
                    ekb, OP.mult)
            self.i(ins, "VT")
            vt_kh = self.c["VT"]
            if self.cut("c_kh"):
                return
            if with_qy:
                ins = None
                if v_:
                    qsl = s["QA"].rearrange("p (j t) -> p j t", j=KD)[:, :, tt * C:(tt + 1) * C]
                    esl = s["EBC"][:, :].rearrange("p (j t) -> p j t", j=KD)
                    ins = eng.tensor_tensor(
                        s["QT"][:, :].rearrange("p (j t) -> p j t", j=KD), qsl, esl, OP.mult)
                self.i(ins, "VT")
            vt_qt = self.c["VT"]

            for pj in range(8):
                h0, h1 = 2 * pj, 2 * pj + 1
                j = pj
                if with_qy:
                    self.w("tensor", "VT")
                    ins = None
                    if t_:
                        eng.matmul(self.P(0)[:, 0:128],
                                   s["KA"][0:64, j * G + tt * C: j * G + tt * C + C],
                                   s["QA"][0:64, j * G + tt * C: j * G + tt * C + C],
                                   start=True, stop=True)
                        ins = eng.matmul(self.P(1)[:, 0:128],
                                         s["KA"][64:128, j * G + tt * C: j * G + tt * C + C],
                                         s["QA"][64:128, j * G + tt * C: j * G + tt * C + C],
                                         start=True, stop=True)
                    self.i(ins, "TV")
                    tv_sc = self.c["TV"]
                    if self.cut("b_sc"):
                        return
                    self.w("tensor", "LG", gm["lg_flat"])
                    ins = None
                    if t_:
                        bank = self.P(3) if (pj % 2 == 0) else self.P(4)
                        ins = eng.matmul(bank[:, 0:256], s["STAGE17"][:, :],
                                         s["IND17"][:, pj * 256:(pj + 1) * 256],
                                         start=True, stop=True)
                    self.i(ins, "TV")
                    tv_rt = self.c["TV"]
                    self.w("vector", "TV", tv_rt)
                    ins = None
                    if v_:
                        bank = self.P(3) if (pj % 2 == 0) else self.P(4)
                        mb = s["MASK"][:, :].unsqueeze(1).broadcast_to([128, 2, 128])
                        ins = eng.tensor_tensor(
                            s["DL"][:, 0:256].rearrange("p (a t) -> p a t", a=2),
                            bank[:, 0:256].rearrange("p (a t) -> p a t", a=2),
                            mb, OP.add)
                    self.i(ins, "VT")
                    vt_dl = self.c["VT"]
                    self.w("scalar", "VT", vt_dl)
                    ins = None
                    if s_:
                        ins = eng.activation(s["DL"][:, 0:256], s["DL"][:, 0:256],
                                             AF.Exp, scale=-1.0)
                    self.i(ins, "SC")
                    sc_de = self.c["SC"]
                    self.w("vector", "SC", sc_de)
                    self.w("vector", "TV", tv_sc)
                    ins = None
                    if v_:
                        eng.tensor_tensor(s["AP4"][:, 0:128], s["DL"][:, 0:128],
                                          self.P(0)[:, 0:128], OP.mult)
                        ins = eng.tensor_tensor(s["AP4"][:, 128:256], s["DL"][:, 128:256],
                                                self.P(1)[:, 0:128], OP.mult)
                    self.i(ins, "VT")
                    vt_a = self.c["VT"]
                    if self.cut("b_a"):
                        return
                    ybank = self.P(5) if (pj % 2 == 0) else self.P(6)
                    self.w("tensor", "VT", vt_a)
                    self.w("tensor", "SC")
                    ins = None
                    if t_:
                        for z in range(2):
                            h = 2 * pj + z
                            base = z * 64
                            eng.matmul(ybank[base:base + 64, 0:128],
                                       s["VA"][:, tt * D + h * DH: tt * D + (h + 1) * DH],
                                       s["AP4"][:, z * 128:(z + 1) * 128],
                                       start=True, stop=False,
                                       tile_position=(0, base))
                            ins = eng.matmul(ybank[base:base + 64, 0:128],
                                             s["SHIZ"][:, h * DH:(h + 1) * DH],
                                             s["QT"][:, pj * 128:(pj + 1) * 128],
                                             start=False, stop=True,
                                             tile_position=(0, base))
                    self.i(ins, "TV")
                    tv_y = self.c["TV"]
                    if self.cut("b_y"):
                        return
                    self.w("scalar", "TV", tv_y)
                    ins = None
                    if s_:
                        ins = eng.activation(
                            s["YA"][:, pj * G + tt * C: pj * G + tt * C + C],
                            ybank[:, 0:128], AF.Identity)
                    self.i(ins, "SC")
                self.w("tensor", "VT", vt_kh)
                ins = None
                if t_:
                    for z in range(2):
                        h = 2 * pj + z
                        base = z * 64
                        ins = eng.matmul(self.P(7)[base:base + 64, pj * DH:(pj + 1) * DH],
                                         s["KH"][:, h * DH:(h + 1) * DH],
                                         s["VA"][:, tt * D + h * DH: tt * D + (h + 1) * DH],
                                         start=True, stop=True,
                                         tile_position=(0, base))
                self.i(ins, "TV")
                if self.cut("b_ds"):
                    return
            tv_ds_all = self.c["TV"]
            self.w("vector", "TV", tv_ds_all)
            self.w("vector", "SC", gm["sc_drp"])
            ins = None
            if v_:
                for pj in range(8):
                    eng.scalar_tensor_tensor(
                        s["SM"][:, pj * DH:(pj + 1) * DH], s["SM"][:, pj * DH:(pj + 1) * DH],
                        s["DRP"][:, pj:pj + 1], self.P(7)[:, pj * DH:(pj + 1) * DH],
                        OP.mult, OP.add)
                smv = s["SM"][:, :].rearrange("p (j c) -> p j c", j=8)
                shz = s["SHIZ"][:, :].rearrange("p (h2 two c) -> p h2 two c", two=2, c=DH)
                eng.tensor_copy(out=shz[0:64, :, 0, :], in_=smv[0:64, :, :])
                ins = eng.tensor_copy(out=shz[64:128, :, 1, :], in_=smv[64:128, :, :])
            self.i(ins, "VT")

        phases = ([("prefix", gi) for gi in range(NG)] if self.do_prefix else []) \
            + [("main", gi) for gi in range(NG)]
        for phase, gi in phases:
            if self.cut("group"):
                return
            is_main = (phase == "main")
            if not is_main:
                self.w("sync", "VT")
                self.w("sync", "TV")
                self.w("sync", "SC")
                self.w("sync", "LX")
                ins = None
                if y_:
                    ins = eng.dma_start(
                        out=s["XP"].rearrange("p (k t) -> p k t", k=KD),
                        in_=d["xpT"].rearrange("(k p) t -> p k t", p=128)[:, :, gi * G:(gi + 1) * G])
                self.i(ins, "LX", 16)
                ld_x = self.c["LX"]
                for e in ("tensor", "vector", "scalar"):
                    self.w(e, "LX", ld_x)
                    self.w(e, "LD", ld_const)
                self.w("tensor", "LG", lg_kvg)
                xsrc, g0, Tx = s["XP"], 0, G
            else:
                for e in ("tensor", "vector", "scalar"):
                    self.w(e, "LD", ld_const)
                self.w("tensor", "LG", lg_allw)
                xsrc, g0, Tx = s["XF"], gi * G, T
            ln_group(xsrc, Tx, g0, G, s["U1"], s["LNR"], s["TMPE"], s["TMPE2"])
            if self.cut("ln"):
                return
            projections(with_q=is_main)
            if self.cut("proj"):
                return
            self.svc(25)
            for tt in range(NCH):
                gm = gate_pipeline(tt)
                if gm is None:
                    return
                if self.cut("gate"):
                    return
                self.svc(8)
                chunk_heads(gi, tt, gm, with_qy=is_main)
                if self.cut("chunk"):
                    return
                self.svc(18)
            if is_main:
                for j in range(KD):
                    bank = self.P(0) if (j % 2 == 0) else self.P(1)
                    self.w("tensor", "VT")
                    self.w("tensor", "SC")
                    ins = None
                    if t_:
                        for k in range(KD):
                            ins = eng.matmul(
                                bank[:, 0:G],
                                s["WOT"][:, k * D + j * 128: k * D + j * 128 + 128],
                                s["YA"][:, k * G:(k + 1) * G],
                                start=(k == 0), stop=(k == KD - 1))
                    self.i(ins, "TV")
                    tvw = self.c["TV"]
                    self.w("vector", "TV", tvw)
                    ins = None
                    if v_:
                        xc = s["XF"][:, j * T + gi * G: j * T + gi * G + G]
                        ins = eng.tensor_tensor(xc, bank[:, 0:G], xc, OP.add)
                    self.i(ins, "VT")

        if self.debug_xmid:
            vt_all = self.w("sync", "VT")
            ins = None
            if y_:
                ins = eng.dma_start(out=d["out"], in_=s["XF"][:, :])
            self.i(ins, "ST", 16)

    def walk_ffn(self, t_, v_, s_, y_, g_, ln_group):
        eng, s, d = self.eng, self.s, self.d
        T = self.T
        if self.debug_xmid:
            return

        while self.qi < len(QSTEPS):
            self.svc(1)
        qf_final = self.c["QF"]

        for e in ENGINES:
            for sm in ("VT", "SC", "TV", "LD", "LG", "LX"):
                self.w(e, sm)
        self.w("vector", "QF", qf_final)
        self.w("scalar", "QF", qf_final)
        ins = None
        if y_:
            eng.dma_start(out=s["B1R"][:, :], in_=d["b1r"]).then_inc(self.sems["LW"], 16)
            ins = eng.dma_start(out=s["B2R"], in_=d["b2r"])
        self.c["LW"] += 16
        self.i(ins, "LW", 16)
        ld_fconst = self.c["LW"]
        for e in ("tensor", "vector", "scalar"):
            self.w(e, "LW", ld_fconst)

        self.q2_lpar = 0
        self.q2_spar = 0
        self.q2_pending = []
        self.q2_ldmark = [(None, 0), (None, 0)]
        self.q2_stmark = [0, 0]
        self.q2_w2marks = {}

        def q2_load(w, p):
            pb = self.q2_lpar % 2
            self.q2_lpar += 1
            gsem, gval = self.q2_ldmark[pb]
            if gsem is not None:
                self.w("scalar", gsem, gval)
            ins = None
            if s_:
                ins = eng.dma_start(out=s["QLD2"][:, pb * QP:(pb + 1) * QP],
                                    in_=self.q_src(w, p))
            self.i(ins, "QL%d" % pb, 16)
            self.q2_pending.append((pb, self.c["QL%d" % pb]))

        def q2_piece(w, p):
            pb, qlc = self.q2_pending.pop(0)
            self.w("scalar", "QL%d" % pb, qlc)
            ql = s["QLD2"][:, pb * QP:(pb + 1) * QP]
            ins = None
            if s_:
                ins = eng.activation(ql, ql, AF.Identity,
                                     scale=s["WSC"][:, w:w + 1])
            mva = self.i(ins, "QV")
            self.w("vector", "QV", mva)
            if w == 0:
                r, c0 = divmod(p, FF // QP)
                dst = s["W1Q"][:, r * FF + c0 * QP: r * FF + (c0 + 1) * QP]
            else:
                sb = self.q2_spar % 2
                self.q2_spar += 1
                self.w("vector", "QS%d" % sb, self.q2_stmark[sb])
                dst = s["QSTG2"][:, sb * QP:(sb + 1) * QP]
            ins = None
            if v_:
                eng.tensor_scalar(ql, ql, MAGIC, MAGIC, OP.add, OP.subtract)
                ins = eng.tensor_scalar(dst, ql, 1.0, -1.0, OP.min, OP.max)
            mvw = self.i(ins, "QW")
            self.q2_ldmark[pb] = ("QW", mvw)
            if w == 1:
                self.w("scalar", "QW", mvw)
                ins = None
                if s_:
                    ins = eng.dma_start(out=self.q_dst(1, p), in_=dst)
                self.i(ins, "QS%d" % sb, 16)
                self.q2_stmark[sb] = self.c["QS%d" % sb]
                self.q2_w2marks[p] = (self.c["QS0"], self.c["QS1"])
            return mvw

        w1pieces = [r * (FF // QP) + c0
                    for c0 in range(FF // QP) for r in range(KD)]
        q2_load(0, w1pieces[0])
        q2_load(0, w1pieces[1])
        w1q_mark = 0
        for i, p in enumerate(w1pieces):
            if i + 2 < len(w1pieces):
                q2_load(0, w1pieces[i + 2])
            w1q_mark = q2_piece(0, p)
        self.q2_w2queue = list(range(QNP))
        q2_load(1, 0)
        q2_load(1, 1)
        self.q2_w2emitted = 0

        def q2_w2_emit(nmax):
            for _ in range(nmax):
                if self.q2_w2emitted >= QNP:
                    return
                p = self.q2_w2emitted
                self.q2_w2emitted += 1
                if p + 2 < QNP:
                    q2_load(1, p + 2)
                q2_piece(1, p)
        self.q2_w2_emit = q2_w2_emit

        self.w("tensor", "VT")
        ins = None
        if t_:
            eng.matmul(self.P(2)[:, 0:512], s["ONES1"][0:1, :], s["B2R"][0:1, 0:512],
                       start=True, stop=True)
            ins = eng.matmul(self.P(3)[:, 0:512], s["ONES1"][0:1, :], s["B2R"][0:1, 512:1024],
                             start=True, stop=True)
        self.i(ins, "TV")
        tvb2 = self.c["TV"]
        self.w("vector", "TV", tvb2)
        ins = None
        if v_:
            eng.tensor_copy(out=s["B2BC"][:, 0:512], in_=self.P(2)[:, 0:512])
            ins = eng.tensor_copy(out=s["B2BC"][:, 512:1024], in_=self.P(3)[:, 0:512])
        self.i(ins, "VT")

        NFG = self.NFG
        prev_f2_marks = [None, None]
        RO = s["ROWS"]
        for fg in range(NFG):
            if self.cut("ffngroup"):
                return
            g0 = fg * FG
            ln_group(s["XF"], T, g0, FG, s["C1"], s["ROWS"][0:1, 0:2 * G],
                     s["FTMP"], s["TQ1"])
            ins = None
            if v_:
                eng.tensor_tensor(RO[0:1, 3 * G:3 * G + FG], RO[0:1, G:2 * G],
                                  RO[0:1, G:2 * G], OP.mult)
                ins = eng.tensor_scalar(RO[0:1, 3 * G:3 * G + FG], RO[0:1, 3 * G:3 * G + FG],
                                        -EPS, 1.0 + EPS, OP.mult, OP.add)
            self.i(ins, "VT")
            self.w("scalar", "VT")
            ins = None
            if s_:
                eng.activation(RO[0:1, 2 * G:2 * G + FG], RO[0:1, 3 * G:3 * G + FG], AF.Ln)
                ins = eng.activation(RO[0:1, 2 * G:2 * G + FG], RO[0:1, 2 * G:2 * G + FG],
                                     AF.Exp, scale=-0.5)
            self.i(ins, "SC")
            sc_rho = self.c["SC"]
            self.w("vector", "ST")
            ins = None
            if v_:
                eng.memset(s["MAXA"][:, 0:FG], 0.0)
                for k in range(KD):
                    c = s["C1"][:, k * FG:(k + 1) * FG]
                    eng.scalar_tensor_tensor(s["FTMP"][:, 0:FG], c, -1.0, c,
                                             OP.mult, OP.max)
                    ins = eng.tensor_tensor(s["MAXA"][:, 0:FG], s["MAXA"][:, 0:FG],
                                            s["FTMP"][:, 0:FG], OP.max)
            self.i(ins, "VT")
            vt_mx = self.c["VT"]
            self.cross_max(t_, v_, s_, y_, g_, s["MAXA"][:, 0:FG], FG, vt_mx, s["ROWS"], 3 * G)
            self.w("vector", "SC", sc_rho)
            ins = None
            if v_:
                eng.tensor_tensor(RO[0:1, 3 * G:3 * G + FG], RO[0:1, 2 * G:2 * G + FG],
                                  RO[0:1, 3 * G:3 * G + FG], OP.mult)
                eng.tensor_scalar(RO[0:1, 3 * G:3 * G + FG], RO[0:1, 3 * G:3 * G + FG],
                                  1e-5, None, OP.max)
                ins = eng.tensor_scalar(RO[0:1, 4 * G:4 * G + FG], RO[0:1, 3 * G:3 * G + FG],
                                        s["MABP"][0:1, 0:1], 1.0 / 127.0, OP.mult, OP.mult)
            self.i(ins, "VT")
            vt_inv = self.c["VT"]
            self.w("tensor", "VT", vt_inv)
            ins = None
            if t_:
                ins = eng.matmul(self.P(6)[:, 0:FG], s["ONES1"][0:1, :],
                                 RO[0:1, 4 * G:4 * G + FG], start=True, stop=True)
            self.i(ins, "TV")
            tv_inv = self.c["TV"]
            ins = None
            if v_:
                self.w("vector", "TV", tv_inv)
                eng.reciprocal(RO[0:1, 4 * G:4 * G + FG], RO[0:1, 3 * G:3 * G + FG])
                eng.tensor_scalar(RO[0:1, 4 * G:4 * G + FG], RO[0:1, 4 * G:4 * G + FG],
                                  127.0, None, OP.mult)
                eng.tensor_tensor(RO[0:1, 4 * G:4 * G + FG], RO[0:1, 4 * G:4 * G + FG],
                                  RO[0:1, 2 * G:2 * G + FG], OP.mult)
                eng.tensor_tensor(RO[0:1, 2 * G:2 * G + FG], RO[0:1, G:2 * G],
                                  RO[0:1, 4 * G:4 * G + FG], OP.mult)
                ins = eng.tensor_tensor(RO[0:1, 3 * G:3 * G + FG], RO[0:1, 0:G],
                                        RO[0:1, 4 * G:4 * G + FG], OP.mult)
            self.i(ins, "VT")
            vt_rows = self.c["VT"]
            self.w("tensor", "VT", vt_rows)
            ins = None
            if t_:
                eng.matmul(self.P(2)[:, 0:FG], s["ONES1"][0:1, :],
                           RO[0:1, 2 * G:2 * G + FG], start=True, stop=True)
                ins = eng.matmul(self.P(3)[:, 0:FG], s["ONES1"][0:1, :],
                                 RO[0:1, 3 * G:3 * G + FG], start=True, stop=True)
            self.i(ins, "TV")
            tv_qb = self.c["TV"]
            self.w("vector", "TV", tv_qb)
            ins = None
            if v_:
                ins = eng.tensor_copy(out=s["QB2"][:, 0:FG], in_=self.P(6)[:, 0:FG])
            self.i(ins, "VT")
            ins = None
            for k0 in range(0, KD, 2):
                c0 = s["C1"][:, k0 * FG:(k0 + 1) * FG]
                c1_ = s["C1"][:, (k0 + 1) * FG:(k0 + 2) * FG]
                x0 = s["XF"][:, k0 * T + g0: k0 * T + g0 + FG]
                x1 = s["XF"][:, (k0 + 1) * T + g0: (k0 + 1) * T + g0 + FG]
                if v_:
                    eng.tensor_tensor(c0, x0, self.P(2)[:, 0:FG], OP.mult)
                    eng.tensor_tensor(c1_, x1, self.P(2)[:, 0:FG], OP.mult)
                    eng.tensor_tensor(c0, c0, self.P(3)[:, 0:FG], OP.add)
                    eng.tensor_tensor(c1_, c1_, self.P(3)[:, 0:FG], OP.add)
                    eng.tensor_scalar(c0, c0, MAGIC, MAGIC, OP.add, OP.subtract)
                    eng.tensor_scalar(c1_, c1_, MAGIC, MAGIC, OP.add, OP.subtract)
                    eng.tensor_scalar(c0, c0, 127.0, -128.0, OP.min, OP.max)
                    ins = eng.tensor_scalar(c1_, c1_, 127.0, -128.0, OP.min, OP.max)
            self.i(ins, "VT")
            vt_c1 = self.c["VT"]
            for tt in range(4):
                for k in range(KD):
                    self.w("tensor", "VT")
                    ins = None
                    if t_:
                        ins = eng.transpose(self.P(4)[:, 0:128] if (k % 2 == 0) else self.P(5)[:, 0:128],
                                            s["XF"][:, k * T + g0 + tt * 128: k * T + g0 + tt * 128 + 128],
                                            s["IDENTF"][:, :])
                    self.i(ins, "TV")
                    tvt = self.c["TV"]
                    self.w("vector", "TV", tvt)
                    ins = None
                    if v_:
                        bank = self.P(4) if (k % 2 == 0) else self.P(5)
                        ins = eng.tensor_tensor(s["XT2"][:, tt * D + k * 128:(k + 1) * 128 + tt * D],
                                                bank[:, 0:128], s["B2BC"][:, k * 128:(k + 1) * 128],
                                                OP.add)
                    self.i(ins, "VT")
            if fg == 0:
                self.w("tensor", "QW", w1q_mark)
            self.w("vector", "ST")
            ins = None
            if v_:
                ins = eng.memset(s["MAXA"][:, 0:FG], 0.0)
            self.i(ins, "VT")
            for Fj in range(KF):
                bank = self.P(0) if (Fj % 2 == 0) else self.P(1)
                hsl = s["H"][:, Fj * FG:(Fj + 1) * FG]
                self.w("tensor", "VT")
                ins = None
                if t_:
                    for k in range(KD):
                        ins = eng.matmul(bank[:, 0:FG],
                                         s["W1Q"][:, k * FF + Fj * 128: k * FF + Fj * 128 + 128],
                                         s["C1"][:, k * FG:(k + 1) * FG],
                                         start=(k == 0), stop=(k == KD - 1))
                self.i(ins, "TV")
                tvf = self.c["TV"]
                self.w("vector", "TV", tvf)
                ins = None
                if v_:
                    ins = eng.tensor_tensor(s["TQ1"][:, 0:FG], bank[:, 0:FG],
                                            s["QB2"][:, 0:FG], OP.mult)
                self.i(ins, "VT")
                vt_t1 = self.c["VT"]
                self.w("scalar", "VT", vt_t1)
                ins = None
                if s_:
                    ins = eng.activation(hsl, s["TQ1"][:, 0:FG],
                                         AF.Gelu, bias=s["B1R"][:, Fj:Fj + 1])
                self.i(ins, "SC")
                sch = self.c["SC"]
                self.w("vector", "SC", sch)
                if Fj > 0:
                    self.w("vector", "TV", ssq_mark)
                if Fj > 1:
                    pass
                ins = None
                if v_:
                    eng.tensor_tensor(s["TQ1"][:, 0:FG], hsl, hsl, OP.mult)
                    eng.scalar_tensor_tensor(s["FTMP"][:, 0:FG], hsl, -1.0, hsl,
                                             OP.mult, OP.max)
                    ins = eng.tensor_tensor(s["MAXA"][:, 0:FG], s["MAXA"][:, 0:FG],
                                            s["FTMP"][:, 0:FG], OP.max)
                self.i(ins, "VT")
                vt_sq = self.c["VT"]
                self.w("tensor", "VT", vt_sq)
                ins = None
                if t_:
                    ins = eng.matmul(self.P(6)[0:1, 0:FG], s["ONESCB"][:, :], s["TQ1"][:, 0:FG],
                                     start=(Fj == 0), stop=(Fj == KF - 1))
                self.i(ins, "TV")
                ssq_mark = self.c["TV"]
                if fg == 0:
                    self.q2_w2_emit(4)
            tv_ssq = self.c["TV"]
            vt_mx2 = self.c["VT"]
            self.w("vector", "TV", tv_ssq)
            ins = None
            if v_:
                ins = eng.tensor_scalar(RO[0:1, 3 * G:3 * G + FG], self.P(6)[0:1, 0:FG],
                                        1.0 / FF, EPS, OP.mult, OP.add)
            self.i(ins, "VT")
            self.w("scalar", "VT")
            ins = None
            if s_:
                eng.activation(RO[0:1, 2 * G:2 * G + FG], RO[0:1, 3 * G:3 * G + FG], AF.Ln)
                ins = eng.activation(RO[0:1, 2 * G:2 * G + FG], RO[0:1, 2 * G:2 * G + FG],
                                     AF.Exp, scale=-0.5)
            self.i(ins, "SC")
            sc_rho2 = self.c["SC"]
            self.cross_max(t_, v_, s_, y_, g_, s["MAXA"][:, 0:FG], FG, vt_mx2, s["ROWS"], 3 * G)
            self.w("vector", "SC", sc_rho2)
            ins = None
            if v_:
                eng.tensor_tensor(RO[0:1, 3 * G:3 * G + FG], RO[0:1, 2 * G:2 * G + FG],
                                  RO[0:1, 3 * G:3 * G + FG], OP.mult)
                eng.tensor_scalar(RO[0:1, 3 * G:3 * G + FG], RO[0:1, 3 * G:3 * G + FG],
                                  1e-5, None, OP.max)
                eng.reciprocal(RO[0:1, 4 * G:4 * G + FG], RO[0:1, 3 * G:3 * G + FG])
                eng.tensor_scalar(RO[0:1, 4 * G:4 * G + FG], RO[0:1, 4 * G:4 * G + FG],
                                  127.0, None, OP.mult)
                ins = eng.tensor_tensor(RO[0:1, 4 * G:4 * G + FG], RO[0:1, 4 * G:4 * G + FG],
                                        RO[0:1, 2 * G:2 * G + FG], OP.mult)
            self.i(ins, "VT")
            vt_q2 = self.c["VT"]
            self.w("tensor", "VT", vt_q2)
            ins = None
            if t_:
                ins = eng.matmul(self.P(2)[:, 0:FG], s["ONES1"][0:1, :],
                                 RO[0:1, 4 * G:4 * G + FG], start=True, stop=True)
            self.i(ins, "TV")
            tv_q2b = self.c["TV"]
            self.w("vector", "TV", tv_q2b)
            ins = None
            if v_:
                ins = eng.tensor_copy(out=s["QB1"][:, 0:FG], in_=self.P(2)[:, 0:FG])
            self.i(ins, "VT")
            self.w("tensor", "VT")
            ins = None
            if t_:
                for tt in range(4):
                    ins = eng.matmul(self.P(3)[:, tt:tt + 1],
                                     RO[0:1, 2 * G + tt * 128: 2 * G + (tt + 1) * 128],
                                     s["ONES1"][0:1, 0:1], start=True, stop=True)
            self.i(ins, "TV")
            tvr = self.c["TV"]
            self.w("vector", "TV", tvr)
            ins = None
            if v_:
                eng.tensor_tensor(s["IV2"][:, 0:4], self.P(3)[:, 0:4],
                                  s["MXC"][:, 0:4], OP.mult)
                eng.tensor_scalar(s["IV2"][:, 0:4], s["IV2"][:, 0:4],
                                  1e-5, None, OP.max)
                ins = eng.tensor_scalar(s["IV2"][:, 0:4], s["IV2"][:, 0:4],
                                        s["MABP"][:, 1:2], 1.0 / 127.0, OP.mult, OP.mult)
            self.i(ins, "VT")
            HB = KF * FG // 2
            ins = None
            if v_:
                hv = s["H"][:, :]
                for h0 in (0, HB):
                    eng.tensor_tensor(
                        hv[:, h0:h0 + HB].rearrange("p (k t) -> p k t", t=FG),
                        hv[:, h0:h0 + HB].rearrange("p (k t) -> p k t", t=FG),
                        s["QB1"][:, 0:FG].unsqueeze(1).broadcast_to([128, KF // 2, FG]),
                        OP.mult)
                eng.tensor_scalar(hv[:, 0:HB], hv[:, 0:HB], MAGIC, MAGIC, OP.add, OP.subtract)
                eng.tensor_scalar(hv[:, HB:2 * HB], hv[:, HB:2 * HB], MAGIC, MAGIC,
                                  OP.add, OP.subtract)
                eng.tensor_scalar(hv[:, 0:HB], hv[:, 0:HB], 127.0, -128.0, OP.min, OP.max)
                ins = eng.tensor_scalar(hv[:, HB:2 * HB], hv[:, HB:2 * HB], 127.0, -128.0,
                                        OP.min, OP.max)
            self.i(ins, "VT")
            vt_c2 = self.c["VT"]
            NPC = 16
            def load_piece(pc, prev_mm_mark):
                buf = s["W2S0"] if (pc % 2 == 0) else s["W2S1"]
                sem = "LS" if (pc % 2 == 0) else "LS2"
                if prev_mm_mark is not None:
                    self.w("scalar", "TV", prev_mm_mark)
                self.w("scalar", sem)
                if fg == 0:
                    a0, a1 = self.q2_w2marks[QNP - 1]
                    self.w("scalar", "QS0", a0)
                    self.w("scalar", "QS1", a1)
                ins = None
                if s_:
                    ins = eng.dma_start(
                        out=buf[:, :].rearrange("p (a f) -> p a f", a=2),
                        in_=d["w2q_dram"].rearrange("(a p) f -> p a f", p=128)[:, 2 * pc:2 * pc + 2, :])
                return self.i(ins, sem, 16)

            mm_marks = [None] * NPC
            ld_marks = [None] * NPC
            ld_marks[0] = load_piece(0, prev_f2_marks[0])
            ld_marks[1] = load_piece(1, prev_f2_marks[1])
            self.w("tensor", "VT", vt_c2)
            for pc in range(NPC):
                buf = s["W2S0"] if (pc % 2 == 0) else s["W2S1"]
                self.w("tensor", "LS" if (pc % 2 == 0) else "LS2", ld_marks[pc])
                ins = None
                if t_:
                    for a in range(2):
                        Fk = 2 * pc + a
                        for tt in range(4):
                            for half in range(2):
                                ins = eng.matmul(
                                    self.pf[tt * 2 + half][:, 0:512],
                                    s["H"][:, Fk * FG + tt * 128: Fk * FG + tt * 128 + 128],
                                    buf[:, a * D + half * 512: a * D + half * 512 + 512],
                                    start=(Fk == 0), stop=(Fk == KF - 1))
                self.i(ins, "TV")
                mm_marks[pc] = self.c["TV"]
                if pc + 2 < NPC:
                    ld_marks[pc + 2] = load_piece(pc + 2, mm_marks[pc])
            prev_f2_marks = [mm_marks[NPC - 2], mm_marks[NPC - 1]]
            tv_f2 = self.c["TV"]
            for tt in range(4):
                for half in range(2):
                    self.w("vector", "TV", tv_f2)
                    self.w("vector", "ST")
                    ins = None
                    if v_:
                        ins = eng.scalar_tensor_tensor(
                            s["OST"][:, 0:512],
                            self.pf[tt * 2 + half][:, 0:512],
                            s["IV2"][:, tt:tt + 1],
                            s["XT2"][:, tt * D + half * 512: tt * D + half * 512 + 512],
                            OP.mult, OP.add)
                    self.i(ins, "VT")
                    vto = self.c["VT"]
                    self.w("sync", "VT", vto)
                    self.w("sync", "ST")
                    ins = None
                    if y_:
                        ins = eng.dma_start(
                            out=d["out"][g0 + tt * 128: g0 + tt * 128 + 128,
                                         half * 512:(half + 1) * 512],
                            in_=s["OST"][:, 0:512])
                    self.i(ins, "ST", 16)

    def cross_max(self, t_, v_, s_, y_, g_, src, Gw, vt_src, rowdst, rowoff, keep_cols=False):
        eng, s = self.eng, self.s
        nt = Gw // 128
        for tt in range(nt):
            self.w("tensor", "VT", vt_src)
            self.w("tensor", "VT")
            ins = None
            if t_:
                ins = eng.transpose(self.P(4)[:, 0:128] if tt % 2 == 0 else self.P(5)[:, 0:128],
                                    src[:, tt * 128:(tt + 1) * 128], s["IDENTF"][:, :])
            self.i(ins, "TV")
            tvt = self.c["TV"]
            self.w("vector", "TV", tvt)
            ins = None
            if v_:
                bank = self.P(4) if tt % 2 == 0 else self.P(5)
                ins = eng.tensor_reduce(s["MXC"][:, tt:tt + 1], bank[:, 0:128],
                                        AX, OP.max, apply_absolute_value=True)
            self.i(ins, "VT")
        vt_cols = self.c["VT"]
        self.w("tensor", "VT", vt_cols)
        ins = None
        if t_:
            ins = eng.transpose(self.P(4)[0:nt, 128:256], s["MXC"][:, 0:nt], s["IDENTF"][:, :])
        self.i(ins, "TV")
        tvt = self.c["TV"]
        self.w("vector", "TV", tvt)
        ins = None
        if v_:
            ins = eng.tensor_copy(out=s["TQ1"][0:nt, 0:128], in_=self.P(4)[0:nt, 128:256])
        self.i(ins, "VT")
        vtc = self.c["VT"]
        self.w("gpsimd", "VT", vtc)
        self.w("gpsimd", "SC")
        self.w("gpsimd", "LG")
        ins = None
        if g_:
            ins = eng.dma_start(
                out=rowdst[0:1, rowoff:rowoff + Gw].rearrange("o (a t) -> o a t", a=nt),
                in_=s["TQ1"][0:nt, 0:128])
        self.i(ins, "LG", 16)
        lgf = self.c["LG"]
        self.w("vector", "LG", lgf)
        return self.c["LG"]



def _consts():
    i = np.arange(128)
    k16 = np.arange(NH)
    tri1 = (i[:, None] <= i[None, :]).astype(ml_dtypes.bfloat16)
    tri2 = (i[:, None] > i[None, :]).astype(ml_dtypes.bfloat16)
    mask = np.where(i[:, None] <= i[None, :], 0.0, -MASKVAL).astype(ml_dtypes.bfloat16)
    parity = (i[None, :] // 64 == k16[:, None] % 2).astype(np.float32)
    pairsel = (k16[:, None] // 2 == np.arange(8)[None, :]).astype(np.float32)
    pj8 = np.repeat((k16[:, None] // 2 == np.arange(8)[None, :]).astype(ml_dtypes.bfloat16),
                    128, axis=1)
    ind16 = np.zeros((NH, NH * 128), np.float32)
    for k in range(NH):
        ind16[k, k * 128:(k + 1) * 128] = -1.0
    return dict(
        c_identb=np.eye(128, dtype=ml_dtypes.bfloat16),
        c_identf=np.eye(128, dtype=np.float32),
        c_tri1=tri1, c_tri2=tri2, c_mask=mask, c_ind16=ind16,
        c_parityf=parity, c_parityb=parity.astype(ml_dtypes.bfloat16),
        c_pairsel=pairsel, c_pj8=pj8,
        c_ones1=np.ones((1, 128), np.float32),
        c_qcol=np.concatenate([np.full((128, 1), MAGIC, np.float32),
                               np.full((128, 1), -1.0, np.float32),
                               np.zeros((128, 1), np.float32)],
                              axis=1),
        c_onesc=np.ones((128, 1), np.float32),
        c_onescb=np.ones((128, 1), ml_dtypes.bfloat16),
    )


def make_inputs_for_core(core, x, Wq, Wk, Wv, Wo, Wg, bgv, w1, b1, w2, b2, T=2048):
    b, sq, _ = x.shape
    xf = np.asarray(x, np.float32).reshape(b * sq, D)
    seq = core // 2
    half = core % 2
    own = xf[seq * sq + half * T: seq * sq + (half + 1) * T]
    if half == 1:
        pfx = xf[seq * sq: seq * sq + T]
    else:
        pfx = np.zeros((T, D), np.float32)
    ins = dict(
        xT=np.ascontiguousarray(own.T),
        xpT=np.ascontiguousarray(pfx.T),
        wqT=np.ascontiguousarray(np.asarray(Wq, np.float32).T),
        wkT=np.ascontiguousarray(np.asarray(Wk, np.float32).T),
        wvT=np.ascontiguousarray(np.asarray(Wv, np.float32).T),
        woT=np.ascontiguousarray(np.asarray(Wo, np.float32).T),
        wgT=np.ascontiguousarray(np.asarray(Wg, np.float32).T),
        bgp=np.concatenate([np.asarray(bgv, ml_dtypes.bfloat16).reshape(1, NH),
                            np.zeros((127, NH), ml_dtypes.bfloat16)], axis=0),
        w1T=np.ascontiguousarray(np.asarray(w1, np.float32).T),
        w2T=np.ascontiguousarray(np.asarray(w2, np.float32).T),
        b1r=np.ascontiguousarray(np.asarray(b1, np.float32).reshape(KF, 128).T),
        b2r=np.asarray(b2, np.float32).reshape(1, D),
    )
    ins.update(_consts())
    return ins


_NC_CACHE = {}


def kernel(L0_attn_x, L0_attn_Wq, L0_attn_Wk, L0_attn_Wv, L0_attn_Wo, L0_attn_Wg,
           L0_attn_bg, L0_ffn_w1, L0_ffn_b1, L0_ffn_w2, L0_ffn_b2):
    T = 2048
    key = ("full", T)
    if key not in _NC_CACHE:
        _NC_CACHE[key] = Builder(T=T, do_prefix=True).build()
    nc = _NC_CACHE[key]
    in_maps = [
        make_inputs_for_core(c, L0_attn_x, L0_attn_Wq, L0_attn_Wk, L0_attn_Wv,
                             L0_attn_Wo, L0_attn_Wg, L0_attn_bg,
                             L0_ffn_w1, L0_ffn_b1, L0_ffn_w2, L0_ffn_b2, T=T)
        for c in range(NCORES)
    ]
    res = run_bass_kernel_spmd(nc, in_maps, core_ids=list(range(NCORES)))
    outs = [np.asarray(r["out"], np.float32) for r in res.results]
    b, sq, _ = L0_attn_x.shape
    full = np.empty((b * sq, D), np.float32)
    for c in range(NCORES):
        seq, half = c // 2, c % 2
        full[seq * sq + half * T: seq * sq + (half + 1) * T] = outs[c]
    return full.reshape(b, sq, D)
